# version 01
# baseline (speedup 1.0000x reference)
import numpy as np
import ml_dtypes
import concourse.bass as bass
import concourse.mybir as mybir
from concourse.bass_utils import run_bass_kernel_spmd

F32 = mybir.dt.float32
BF16 = mybir.dt.bfloat16
AF = mybir.ActivationFunctionType
ALU = mybir.AluOpType

T = 4096
D = 1024
NT = 32
L = 128
BF = ml_dtypes.bfloat16

_DEV_STOP = None
_DEV_DBG = None


class Ch:
    def __init__(self, ctx, nc, name):
        self.s = ctx.enter_context(nc.semaphore(name))
        self.n = 0

    def inc(self, ins, k=1):
        ins.then_inc(self.s, k)
        self.n += k
        return ins

    def incd(self, ins):
        return self.inc(ins, 16)

    def wait(self, eng, at=None):
        eng.wait_ge(self.s, self.n if at is None else at)


def _consts():
    i = np.arange(L)
    c = {}
    c["c_ident"] = np.eye(L).astype(BF)
    c["c_identf"] = np.eye(L).astype(np.float32)
    c["c_maskb"] = np.where(i[:, None] <= i[None, :], 0.0, -30000.0).astype(np.float32)
    c["c_trilS"] = (i[None, :] < i[:, None]).astype(BF)
    c["c_triuI"] = (i[:, None] <= i[None, :]).astype(BF)
    c["c_ones_row"] = np.ones((1, L), dtype=BF)
    c["c_sel8"] = (np.arange(512)[None, :] // 64 == np.arange(8)[:, None]).astype(BF)
    c["c_oneh"] = (np.arange(1024)[None, :] // 128 == np.arange(8)[:, None]).astype(BF)
    return c


class P:
    pass


def build():
    nc = bass.Bass("TRN2", target_bir_lowering=False, detect_race_conditions=False)
    p = P()
    p.nc = nc
    ap = lambda name, shape, dt, kind: nc.dram_tensor(name, shape, dt, kind=kind).ap()
    p.x0 = ap("x0", [T, D], F32, "ExternalInput")
    p.W = {}
    for name, shape in [
        ("L0_attn_Wz", [D, 2048]), ("L0_attn_Win", [D, 648]), ("L0_attn_Wzg", [2048, 512]),
        ("L0_attn_Wout", [512, D]), ("L0_attn_dt_bias", [8]), ("L0_attn_A_log", [8]),
        ("L0_attn_D_param", [8]),
        ("L0_ffn_w_gate", [2816, D]), ("L0_ffn_w_up", [2816, D]), ("L0_ffn_w_down", [D, 2816]),
        ("L1_attn_Wq", [D, D]), ("L1_attn_Wk", [D, D]), ("L1_attn_Wv", [D, D]),
        ("L1_attn_Wb", [D, 4]), ("L1_attn_conv_q", [4, D]), ("L1_attn_conv_k", [4, D]),
        ("L1_attn_conv_v", [4, D]), ("L1_attn_g_rms", [256]), ("L1_attn_Wo", [D, D]),
        ("L1_attn_ln_g", [D]), ("L1_attn_ln_b", [D]),
        ("L1_ffn_W_fc", [4096, D]), ("L1_ffn_ln_w", [4096]), ("L1_ffn_ln_b", [4096]),
        ("L1_ffn_W_proj", [D, 4096]),
    ]:
        p.W[name] = ap(name, shape, F32, "ExternalInput")
    p.C = {}
    for name, arr in _consts().items():
        p.C[name] = ap(name, list(arr.shape), BF16 if arr.dtype == BF else F32, "ExternalInput")
    p.out = ap("out", [T, D], F32, "ExternalOutput")
    p.dbg = ap("dbg", [4096, 4096], F32, "ExternalOutput") if _DEV_DBG else None
    p.scr = {}
    for name, shape, dt in [
        ("xres", [T, D], F32), ("xln", [T, D], BF16), ("oc", [D, T], BF16),
        ("s_xt", [512, T], BF16), ("s_xx", [512, T], BF16), ("s_y0", [512, T], BF16),
        ("s_xq", [T, D], BF16), ("s_wqg", [2816, D], BF16), ("s_wqu", [2816, D], BF16),
        ("s_wqd", [D, 2816], BF16), ("s_hq", [T, 2816], BF16),
        ("s_kk", [D, T], BF16), ("s_vv", [D, T], BF16), ("s_ot", [T, D], BF16),
        ("s_wf", [4096, D], BF16), ("s_wp", [D, 4096], BF16), ("s_h1", [T, 4096], BF16),
    ]:
        p.scr[name] = ap(name, shape, dt, "Internal")

    sb = lambda name, shape, dt: nc.alloc_sbuf_tensor(name, shape, dt)
    p.AA = sb("AA", [128, 16384], BF16)
    p.AB = sb("AB", [128, 32768], BF16)
    p.AC = sb("AC", [128, 13312], BF16)
    p.AF = sb("AFX", [128, 4096], F32)
    p.SM = sb("SMB", [128, 5120], BF16)
    p.xtm = [sb(f"xtm{i}", [128, D], F32) for i in range(2)]
    p.otm = [sb(f"otm{i}", [128, D], BF16) for i in range(2)]
    p.st48 = sb("st48", [128, 48], F32)
    p.agg2 = sb("agg2", [128, 2], F32)
    p.sc1 = sb("sc1", [128, 8], F32)
    p.lng = sb("lng", [128, D], BF16)
    p.lnb = sb("lnb", [128, D], BF16)
    p.grms = sb("grms", [128, D], BF16)
    p.sm32 = sb("sm32", [128, 1024], F32)
    p.bfm = sb("bfm", [4, 4096], BF16)
    p.cst = {}
    for n, sh, dt in [("c_ident", [L, L], BF16), ("c_identf", [L, L], F32),
                     ("c_maskb", [L, L], F32), ("c_trilS", [L, L], BF16),
                     ("c_triuI", [L, L], BF16), ("c_ones_row", [1, L], BF16),
                     ("c_sel8", [8, 512], BF16), ("c_oneh", [8, 1024], BF16)]:
        p.cst[n] = sb("s" + n, sh, dt)
    p.pb = [nc.alloc_psum_tensor(f"pb{i}", [128, 512], F32) for i in range(7)]
    p.pbt = nc.alloc_psum_tensor("pbt", [128, 1024], BF16)

    from contextlib import ExitStack
    ctx = ExitStack()
    p.dW = Ch(ctx, nc, "dW")
    p.dX = Ch(ctx, nc, "dX")
    p.dS = Ch(ctx, nc, "dS")
    p.cT = Ch(ctx, nc, "cT")
    p.cV = Ch(ctx, nc, "cV")
    p.cS = Ch(ctx, nc, "cS")
    blk = ctx.enter_context(nc.Block())

    @blk.sync
    def _(sync):
        _emit(p)

    ctx.close()
    return nc


def _emit(p):
    _load_consts(p)
    _glue(p, first=True)
    if _DEV_STOP == "ln0":
        _final(p); return
    _l0_attn(p)
    _glue(p)
    if _DEV_STOP == "l0a":
        _final(p); return
    _l0_ffn(p)
    _glue(p)
    if _DEV_STOP == "l0f":
        _final(p); return
    _l1_attn(p)
    _glue(p, inner_ln=True)
    if _DEV_STOP == "l1a":
        _final(p); return
    _l1_ffn(p)
    _glue(p, last=True)
    _final(p)


def _final(p):
    nc = p.nc
    if _DEV_DBG is not None and _DEV_DBG in p.scr:
        s = p.scr[_DEV_DBG]
        rows, cols = s.tensor.shape if hasattr(s, 'tensor') else (None, None)
        sh = list(p.scr[_DEV_DBG].shape)
        stage = p.SM[0:128, 0:4096]
        for r0 in range(0, sh[0], 128):
            p.cV.wait(nc.sync)
            p.dX.incd(nc.sync.dma_start(stage[:, 0:sh[1]], p.scr[_DEV_DBG][r0:r0 + 128, :]))
            p.dX.wait(nc.vector)
            p.cV.inc(nc.vector.tensor_copy(p.AF[:, 0:sh[1]], stage[:, 0:sh[1]]))
            p.cV.wait(nc.gpsimd)
            p.dW.incd(nc.gpsimd.dma_start(p.dbg[r0:r0 + 128, 0:sh[1]], p.AF[:, 0:sh[1]]))
            p.dW.wait(nc.vector)
    if _DEV_STOP is not None:
        src = p.scr["xres"]
        for tt in range(NT):
            sl = slice(tt * 128, (tt + 1) * 128)
            b = p.xtm[tt % 2]
            p.dW.wait(nc.sync)
            p.dX.incd(nc.sync.dma_start(b[:], src[sl, :]))
            p.dX.wait(nc.gpsimd)
            p.dW.incd(nc.gpsimd.dma_start(p.out[sl, :], b[:]))
    p.dW.wait(nc.gpsimd)
    p.dX.wait(nc.gpsimd)
    p.dS.wait(nc.gpsimd)
    p.cV.wait(nc.gpsimd)
    p.cS.wait(nc.gpsimd)
    p.cT.wait(nc.gpsimd)


def _load_consts(p):
    nc = p.nc
    for n in p.cst:
        p.dX.incd(nc.sync.dma_start(p.cst[n][:], p.C[n]))
    s_lg = p.SM[0:1, 0:D]
    s_lb = p.SM[0:1, D:2 * D]
    s_gr = p.SM[0:1, 2 * D:2 * D + 256]
    p.dW.incd(nc.gpsimd.dma_start(s_lg, p.W["L1_attn_ln_g"][None, :]))
    p.dW.incd(nc.gpsimd.dma_start(s_lb, p.W["L1_attn_ln_b"][None, :]))
    p.dW.incd(nc.gpsimd.dma_start(s_gr, p.W["L1_attn_g_rms"][None, :]))
    p.dX.wait(nc.tensor)
    p.dW.wait(nc.tensor)
    orow = p.cst["c_ones_row"]
    for j, src in enumerate([s_lg, s_lb]):
        for c in range(2):
            p.cT.inc(nc.tensor.matmul(p.pb[2 * j + c][:, :], orow[0:1, :],
                                      src[0:1, c * 512:(c + 1) * 512], start=True, stop=True))
    p.cT.inc(nc.tensor.matmul(p.pb[4][:, 0:256], orow[0:1, :], s_gr[0:1, :], start=True, stop=True))
    p.cT.wait(nc.vector)
    for j, dst in enumerate([p.lng, p.lnb]):
        for c in range(2):
            p.cV.inc(nc.vector.tensor_copy(dst[:, c * 512:(c + 1) * 512], p.pb[2 * j + c][:, :]))
    for rep in range(4):
        p.cV.inc(nc.vector.tensor_copy(p.grms[:, rep * 256:(rep + 1) * 256], p.pb[4][:, 0:256]))
    p.cV.wait(nc.tensor)


def _ln_tile(p, src, xl_out, eps=1e-5, sub_mu=True):
    nc = p.nc
    vec, sc = nc.vector, nc.scalar
    N = src.shape[-1]
    scrap = p.AF[0:128, 2 * D:3 * D]
    c = p.sc1
    p.cV.inc(vec.reduce_sum(c[:, 0:1], src, mybir.AxisListType.X))
    p.cV.wait(sc)
    p.cS.inc(sc.activation(scrap[:, 0:N], src, AF.Square, accum_out=c[:, 1:2]))
    p.cS.wait(vec)
    p.cV.inc(vec.tensor_scalar(c[:, 2:3], c[:, 0:1], 1.0 / N, None, ALU.mult))
    p.cV.wait(sc)
    p.cS.inc(sc.activation(c[:, 3:4], c[:, 2:3], AF.Square))
    p.cS.wait(vec)
    vec.scalar_tensor_tensor(c[:, 4:5], c[:, 1:2], 1.0 / N, c[:, 3:4],
                             ALU.mult, ALU.subtract)
    vec.drain()
    vec.tensor_scalar(c[:, 5:6], c[:, 4:5], eps, None, ALU.add)
    vec.drain()
    p.cV.inc(vec.reciprocal(c[:, 6:7], c[:, 5:6]))
    p.cV.wait(sc)
    p.cS.inc(sc.activation(c[:, 7:8], c[:, 6:7], AF.Sqrt))
    p.cS.wait(vec)
    vec.tensor_scalar(xl_out, src, c[:, 2:3], c[:, 7:8], ALU.subtract, ALU.mult)


def _glue(p, first=False, last=False, inner_ln=False):
    nc = p.nc
    vec, sc, gp = nc.vector, nc.scalar, nc.gpsimd
    xin = p.x0 if first else p.scr["xres"]
    oc = p.scr["oc"]
    xl2 = [p.SM[0:128, 3 * D:4 * D], p.SM[0:128, 4 * D:5 * D]]
    yb = p.AF[0:128, 0:D]
    zt = p.AF[0:128, D:2 * D]
    p.dW.wait(nc.scalar)
    p.dW.wait(nc.sync)
    vneed = [p.cV.n, p.cV.n]
    wneed = [p.dW.n, p.dW.n]
    for tt in range(NT):
        sl = slice(tt * 128, (tt + 1) * 128)
        b = tt % 2
        xb, ob, xl = p.xtm[b], p.otm[b], xl2[b]
        nc.sync.wait_ge(p.cV.s, vneed[b])
        nc.sync.wait_ge(p.dW.s, wneed[b])
        p.dX.incd(nc.sync.dma_start(xb[:], xin[sl, :]))
        if not first:
            nc.scalar.wait_ge(p.cV.s, vneed[b])
            nc.scalar.wait_ge(p.dW.s, wneed[b])
            p.dS.incd(nc.scalar.dma_start(ob[:], oc[:, sl], transpose=True))
        if inner_ln:
            nc.scalar.wait_ge(p.cV.s, vneed[b])
            nc.scalar.wait_ge(p.dW.s, wneed[b])
            p.dS.incd(nc.scalar.dma_start(xl, p.scr["xln"][sl, :]))
        p.dX.wait(vec)
        if not first:
            p.dS.wait(vec)
            if inner_ln:
                vec.tensor_tensor(yb, ob[:], xl, ALU.add)
                _ln_tile(p, yb, zt)
                vec.tensor_tensor(zt, zt, p.lng[:], ALU.mult)
                vec.tensor_tensor(zt, zt, p.lnb[:], ALU.add)
                vec.tensor_tensor(xb[:], xb[:], zt, ALU.add)
            else:
                vec.tensor_tensor(xb[:], xb[:], ob[:], ALU.add)
        if not last:
            _ln_tile(p, xb[:], xl)
        p.cV.inc(vec.engine_nop())
        vneed[b] = p.cV.n
        p.cV.wait(gp)
        if not last:
            p.dW.incd(gp.dma_start(p.scr["xln"][sl, :], xl))
        dst = p.out if last else p.scr["xres"]
        p.dW.incd(gp.dma_start(dst[sl, :], xb[:]))
        wneed[b] = p.dW.n
    p.dW.wait(nc.sync)
    p.dW.wait(nc.scalar)
    p.dW.wait(nc.vector)
    p.dW.wait(nc.tensor)


def _load_xlf(p, half, src=None, nrow=D):
    nc = p.nc
    src = src if src is not None else p.scr["xln"]
    t0 = half * 2048
    nc.sync.wait_ge(p.cT.s, p.cT.n)
    for i in range(nrow // 128):
        p.dX.incd(nc.sync.dma_start(p.AA[:, i * 2048:(i + 1) * 2048],
                                    src[t0:t0 + 2048, i * 128:(i + 1) * 128], transpose=True))


def _l0_attn(p):
    nc = p.nc
    pb, pbt = p.pb, p.pbt
    vec, sc, te, gp, sy = nc.vector, nc.scalar, nc.tensor, nc.gpsimd, nc.sync
    dW, dX, cT, cV, cS = p.dW, p.dX, p.cT, p.cV, p.cS
    SM, sm32, AFX = p.SM, p.sm32, p.AF

    db = sm32[0:8, 0:1]
    al = sm32[0:8, 1:2]
    dpc = sm32[0:8, 2:3]
    expa = sm32[0:8, 3:4]
    negA = sm32[0:8, 4:5]
    ones8 = sm32[0:8, 8:136]
    dpcols = sm32[0:128, 136:140]
    Wc = sm32[0:8, 144:272]
    WcT = sm32[0:128, 272:280]
    decT = sm32[0:128, 280:288]
    Dmf = sm32[0:128, 288:416]
    dAL = sm32[0:64, 416:424]
    expw_f = sm32[0:8, 424:552]
    delta = AFX[0:8, 0:2048]
    stF = AFX[0:64, 2048:2560]
    wlo_f = sm32[0:8, 680:808]
    dec_f = sm32[0:8, 552:680]
    w_chunk = sm32[0:8, 808:936]
    xdst = [SM[0:128, j * 512:(j + 1) * 512] for j in range(4)]
    deltab_c = SM[0:8, 2048:2560]
    xdtm = SM[0:128, 0:512]
    CBT = SM[0:128, 512:640]
    Btm = SM[0:128, 640:704]
    GT = SM[0:128, 704:832]
    UT = SM[0:128, 832:960]
    Xdec = SM[0:128, 960:1024]
    ytile = [SM[0:128, 1024 + 128 * j:1152 + 128 * j] for j in range(2)]
    Csc = SM[0:64, 1280:1408]
    whi = SM[0:8, 1408:1536]
    wlo = SM[0:8, 1536:1664]
    expw_b = SM[0:8, 1664:1792]
    stB = p.AC[0:64, 5184:5696]
    dpb = p.AC[0:8, 5696:5697]
    oneh = p.cst["c_oneh"]

    dW.incd(gp.dma_start(db, p.W["L0_attn_dt_bias"][:, None]))
    dW.incd(gp.dma_start(al, p.W["L0_attn_A_log"][:, None]))
    dW.incd(gp.dma_start(dpc, p.W["L0_attn_D_param"][:, None]))
    WinT = [p.AC[0:128, i * 648:(i + 1) * 648] for i in range(8)]
    for i in range(8):
        dW.incd(gp.dma_start(WinT[i], p.W["L0_attn_Win"][i * 128:(i + 1) * 128, :]))

    cV.inc(vec.memset(stF, 0.0))
    cV.inc(vec.memset(stB, 0.0))
    cV.inc(vec.memset(ones8, 1.0))
    dW.wait(sc)
    cS.inc(sc.activation(expa, al, AF.Exp))
    cS.wait(vec)
    cV.inc(vec.tensor_scalar(negA, expa, -1.0, None, ALU.mult))
    cV.inc(vec.tensor_copy(dpb, dpc))
    cV.wait(te)
    dX.wait(te)
    for j in range(4):
        cT.inc(te.matmul(pb[6][:, j:j + 1], p.cst["c_sel8"][:, j * 128:(j + 1) * 128],
                         dpb, start=True, stop=True))
    cT.wait(vec)
    cV.inc(vec.tensor_copy(dpcols, pb[6][:, 0:4]))
    cV.wait(te)

    xd_store = [dW.n] * 4
    yst = [dW.n, dW.n]
    for half in range(2):
        _load_xlf(p, half)
        XLF = [p.AA[0:128, i * 2048:(i + 1) * 2048] for i in range(8)]
        Bt = p.AB[0:64, 0:2048]
        Ct = p.AB[0:64, 2048:4096]
        Xa = [p.AB[0:128, (2 + j) * 2048:(3 + j) * 2048] for j in range(4)]

        for cc in range(4):
            csl = slice(cc * 512, (cc + 1) * 512)
            g0 = half * 2048 + cc * 512
            tsl = slice(g0, g0 + 512)
            dX.wait(te)
            dW.wait(te)
            cV.wait(te)
            cS.wait(te)
            for kt in range(8):
                cT.inc(te.matmul(pb[0][0:64, :], WinT[kt][:, 0:64], XLF[kt][:, csl],
                                 start=(kt == 0), stop=(kt == 7)))
            for kt in range(8):
                cT.inc(te.matmul(pb[6][0:64, :], WinT[kt][:, 64:128], XLF[kt][:, csl],
                                 start=(kt == 0), stop=(kt == 7)))
            for kt in range(8):
                cT.inc(te.matmul(pb[1][0:8, :], WinT[kt][:, 128:136], XLF[kt][:, csl],
                                 start=(kt == 0), stop=(kt == 7)))
            for j in range(4):
                for kt in range(8):
                    cT.inc(te.matmul(pb[2 + j][:, :], WinT[kt][:, 136 + 128 * j:264 + 128 * j],
                                     XLF[kt][:, csl], start=(kt == 0), stop=(kt == 7)))
            cT.wait(vec)
            cT.wait(sc)
            cV.inc(vec.tensor_copy(Bt[:, csl], pb[0][0:64, :]))
            cV.inc(vec.tensor_copy(Ct[:, csl], pb[6][0:64, :]))
            cS.inc(sc.activation(delta[:, csl], pb[1][0:8, :], AF.Exp, bias=db))
            cS.inc(sc.activation(delta[:, csl], delta[:, csl], AF.Ln, bias=1.0))
            cS.wait(vec)
            cV.inc(vec.tensor_copy(deltab_c, delta[:, csl]))
            for j in range(4):
                cV.inc(vec.tensor_copy(Xa[j][:, csl], pb[2 + j][:, :]))
            cV.wait(gp)
            for j in range(4):
                dW.incd(gp.dma_start(p.scr["s_xx"][j * 128:(j + 1) * 128, tsl], Xa[j][:, csl]))
            cV.wait(te)
            for j in range(4):
                cT.inc(te.matmul(pb[6][:, :], p.cst["c_sel8"][:, j * 128:(j + 1) * 128],
                                 deltab_c, start=True, stop=True))
                cT.wait(vec)
                vec.wait_ge(dW.s, xd_store[j])
                cV.inc(vec.tensor_tensor(xdst[j], Xa[j][:, csl], pb[6][:, :], ALU.mult))
                cV.wait(te)
                cV.wait(gp)
                dW.incd(gp.dma_start(p.scr["s_xt"][j * 128:(j + 1) * 128, tsl], xdst[j]))
                xd_store[j] = dW.n
        sy.wait_ge(dW.s, dW.n)
        vec.wait_ge(dW.s, dW.n)
        sc.wait_ge(dW.s, dW.n)
        for scl in range(16):
            t0 = (half * 16 + scl) * 128
            ch_sl = slice(scl * 128, (scl + 1) * 128)
            cS.wait(vec)
            cV.inc(vec.tensor_scalar(w_chunk, delta[:, scl * 128:(scl + 1) * 128],
                                     negA, None, ALU.mult))
            cV.inc(vec.tensor_tensor_scan(Wc, ones8, w_chunk, 0.0, ALU.mult, ALU.add))
            cV.inc(vec.tensor_copy(whi, Wc))
            cV.inc(vec.tensor_tensor(wlo_f, Wc, whi, ALU.subtract))
            cV.inc(vec.tensor_copy(wlo, wlo_f))
            cV.inc(vec.tensor_scalar(dec_f, Wc, Wc[:, 127:128], None, ALU.subtract))
            cV.wait(sc)
            cS.inc(sc.activation(expw_f, Wc, AF.Exp))
            cS.inc(sc.activation(dec_f, dec_f, AF.Exp, scale=-1.0))
            cS.wait(vec)
            cV.inc(vec.tensor_copy(expw_b, expw_f))
            cV.wait(te)
            cS.wait(te)
            cT.inc(te.transpose(pb[5][:, 0:8], Wc, p.cst["c_identf"][0:8, 0:8]))
            cT.inc(te.transpose(pb[5][:, 8:16], dec_f, p.cst["c_identf"][0:8, 0:8]))
            cT.inc(te.matmul(pb[0][:, 0:128], Bt[:, ch_sl], Ct[:, ch_sl],
                             start=True, stop=True))
            cT.inc(te.transpose(pbt[:, 0:64], Bt[:, ch_sl], p.cst["c_ident"][0:64, 0:64]))
            cT.wait(vec)
            cV.inc(vec.tensor_copy(WcT, pb[5][:, 0:8]))
            cV.inc(vec.tensor_copy(decT, pb[5][:, 8:16]))
            cV.inc(vec.tensor_copy(CBT, pb[0][:, 0:128]))
            cV.inc(vec.tensor_copy(Btm, pbt[:, 0:64]))
            sy.wait_ge(cV.s, cV.n)
            sy.wait_ge(cT.s, cT.n)
            dX.incd(sy.dma_start(xdtm, p.scr["s_xt"][0:512, t0:t0 + 128], transpose=True))
            cV.wait(te)
            dX.wait(te)
            dX.wait(vec)
            tprev = cT.n
            for h in range(8):
                pw = pb[1 + (h % 2)]
                pe = pb[3 + (h % 2)]
                py = pb[5 + ((h // 2) % 2)]
                cV.wait(te)
                cS.wait(te)
                lh = oneh[:, h * 128:(h + 1) * 128]
                cT.inc(te.matmul(pw[:, 0:128], lh, whi, start=True, stop=False))
                cT.inc(te.matmul(pw[:, 0:128], lh, wlo, start=False, stop=True))
                cT.inc(te.matmul(pe[0:64, 0:128], lh[:, 0:64], expw_b, start=True, stop=True))
                cT.wait(vec)
                cV.inc(vec.tensor_scalar(Dmf, pw[:, 0:128], WcT[:, h:h + 1],
                                         None, ALU.subtract))
                cV.inc(vec.tensor_tensor(Dmf, Dmf, p.cst["c_maskb"][:, :], ALU.add))
                cV.wait(sc)
                cS.inc(sc.activation(GT, Dmf, AF.Exp))
                vec.wait_ge(cT.s, tprev)
                cV.inc(vec.tensor_tensor(Csc, Ct[:, ch_sl], pe[0:64, 0:128], ALU.mult))
                cV.inc(vec.tensor_copy(dAL[:, h:h + 1], pe[0:64, 127:128]))
                cS.wait(vec)
                cV.inc(vec.tensor_tensor(UT, GT, CBT, ALU.mult))
                if _DEV_DBG == "probe6" and half == 0 and scl == 12 and h == 6:
                    cV.wait(gp)
                    dW.incd(gp.dma_start(p.dbg[2712:2776, 0:512], stB))
                    dW.incd(gp.dma_start(p.dbg[2776:2840, 0:128], Csc))
                    dW.wait(vec)
                if False and _DEV_DBG == "probe6" and half == 0 and scl == 12 and h == 6:
                    cV.wait(gp)
                    cS.wait(gp)
                    dW.incd(gp.dma_start(p.dbg[2048:2176, 0:128], Dmf))
                    dW.incd(gp.dma_start(p.dbg[2176:2304, 0:128], GT))
                    dW.incd(gp.dma_start(p.dbg[2304:2432, 0:128], UT))
                    dW.incd(gp.dma_start(p.dbg[2432:2560, 0:128], CBT))
                    dW.incd(gp.dma_start(p.dbg[2560:2568, 0:128], Wc))
                    dW.incd(gp.dma_start(p.dbg[2568:2696, 0:512], xdtm))
                    dW.incd(gp.dma_start(p.dbg[2696:2704, 0:128], whi))
                    dW.incd(gp.dma_start(p.dbg[2704:2712, 0:128], wlo))
                    dW.wait(vec)
                    dW.wait(sc)
                cV.inc(vec.tensor_scalar(Xdec, xdtm[:, 64 * h:64 * (h + 1)],
                                         decT[:, h:h + 1], None, ALU.mult))
                cV.wait(te)
                ro = 64 * (h % 2)
                cT.inc(te.matmul(py[ro:ro + 64, 0:128], xdtm[:, 64 * h:64 * (h + 1)], UT,
                                 start=True, stop=False))
                cT.inc(te.matmul(py[ro:ro + 64, 0:128], stB[:, 64 * h:64 * (h + 1)], Csc,
                                 start=False, stop=True))
                cT.inc(te.matmul(pb[0][0:64, 128 + 64 * (h % 4):192 + 64 * (h % 4)],
                                 Btm, Xdec, start=True, stop=True))
                tprev = cT.n
                if h % 2 == 1:
                    cT.wait(sc)
                    yt = ytile[(h // 2) % 2]
                    sc.wait_ge(dW.s, yst[(h // 2) % 2])
                    cS.inc(sc.copy(yt, py[:, 0:128]))
                    cS.wait(gp)
                    dW.incd(gp.dma_start(
                        p.scr["s_y0"][128 * (h // 2):128 * (h // 2 + 1), t0:t0 + 128], yt))
                    yst[(h // 2) % 2] = dW.n
                    if _DEV_DBG == "probe6" and half == 0 and scl == 12 and h == 7:
                        dW.incd(gp.dma_start(p.dbg[2840:2968, 0:128], yt))
                        dW.incd(gp.dma_start(p.dbg[2968:3096, 0:64], Xdec))
                        dW.incd(gp.dma_start(p.dbg[3096:3224, 0:8], decT))
                        dW.wait(sc)
                if h % 4 == 3:
                    cT.wait(vec)
                    for hh in range(h - 3, h + 1):
                        cV.inc(vec.scalar_tensor_tensor(
                            stF[:, 64 * hh:64 * (hh + 1)], stF[:, 64 * hh:64 * (hh + 1)],
                            dAL[:, hh:hh + 1],
                            pb[0][0:64, 128 + 64 * (hh % 4):192 + 64 * (hh % 4)],
                            ALU.mult, ALU.add))
            cV.inc(vec.tensor_copy(stB, stF))
            cV.wait(te)

    WzSB = [p.AB[0:128, i * 2048:(i + 1) * 2048] for i in range(8)]
    WzgSB = [p.AB[0:128, 16384 + i * 512:16384 + (i + 1) * 512] for i in range(16)]
    WoutSB = [p.AB[0:128, 24576 + i * 1024:24576 + (i + 1) * 1024] for i in range(4)]
    cT.wait(gp)
    cV.wait(gp)
    for i in range(8):
        dW.incd(gp.dma_start(WzSB[i], p.W["L0_attn_Wz"][i * 128:(i + 1) * 128, :]))
    for i in range(16):
        dW.incd(gp.dma_start(WzgSB[i], p.W["L0_attn_Wzg"][i * 128:(i + 1) * 128, :]))
    for i in range(4):
        dW.incd(gp.dma_start(WoutSB[i], p.W["L0_attn_Wout"][i * 128:(i + 1) * 128, :]))
    zdw = dW.n
    zsil = [p.AC[0:128, i * 512:(i + 1) * 512] for i in range(16)]
    sg2 = [SM[0:128, 0:512], SM[0:128, 4096:4608]]
    ych = [SM[0:128, 512 + j * 512:1024 + j * 512] for j in range(4)]
    t1b = SM[0:128, 2560:3072]
    otile = [SM[0:128, 3072 + j * 512:3584 + j * 512] for j in range(2)]
    xch = [p.AC[0:128, 10240 + j * 512:10752 + j * 512] for j in range(4)]
    prodt = [p.AC[0:128, 8192 + j * 512:8704 + j * 512] for j in range(4)]
    for half in range(2):
        _load_xlf(p, half)
        XLF = [p.AA[0:128, i * 2048:(i + 1) * 2048] for i in range(8)]
        for cc in range(4):
            csl = slice(cc * 512, (cc + 1) * 512)
            g0 = half * 2048 + cc * 512
            tsl = slice(g0, g0 + 512)
            dX.wait(te)
            dW.wait(te)
            zv = [cV.n, cV.n]
            sgv = [cV.n, cV.n]
            for mz in range(16):
                pz = pb[mz % 2]
                sg = sg2[mz % 2]
                te.wait_ge(cV.s, zv[mz % 2])
                for kt in range(8):
                    cT.inc(te.matmul(pz[:, :], WzSB[kt][:, mz * 128:(mz + 1) * 128],
                                     XLF[kt][:, csl], start=(kt == 0), stop=(kt == 7)))
                cT.wait(sc)
                sc.wait_ge(cV.s, sgv[mz % 2])
                cS.inc(sc.activation(sg, pz[:, :], AF.Sigmoid))
                cS.wait(vec)
                cV.inc(vec.tensor_tensor(zsil[mz], sg, pz[:, :], ALU.mult))
                zv[mz % 2] = cV.n
                sgv[mz % 2] = cV.n
            sy.wait_ge(cV.s, cV.n)
            sy.wait_ge(cT.s, cT.n)
            sy.wait_ge(dW.s, zdw)
            for j in range(4):
                dX.incd(sy.dma_start(ych[j], p.scr["s_y0"][j * 128:(j + 1) * 128, tsl]))
                dX.incd(sy.dma_start(xch[j], p.scr["s_xx"][j * 128:(j + 1) * 128, tsl]))
            if _DEV_DBG == "probez" and half == 0 and cc == 0:
                cV.wait(gp)
                cS.wait(gp)
                for j in range(4):
                    dW.incd(gp.dma_start(p.dbg[2048 + 128 * j:2176 + 128 * j, 0:512], ych[j]))
                    dW.incd(gp.dma_start(p.dbg[2560 + 128 * j:2688 + 128 * j, 0:512], xch[j]))
                for mzd in range(4):
                    dW.incd(gp.dma_start(p.dbg[3072 + 128 * mzd:3200 + 128 * mzd, 0:512], zsil[mzd]))
                dW.wait(vec)
                dW.wait(sy)
            pv = [cV.n, cV.n]
            for mg in range(4):
                pzg = pb[2 + mg % 2]
                te.wait_ge(cV.s, pv[mg % 2])
                for kt2 in range(16):
                    cT.inc(te.matmul(pzg[:, :], WzgSB[kt2][:, mg * 128:(mg + 1) * 128],
                                     zsil[kt2], start=(kt2 == 0), stop=(kt2 == 15)))
                dX.wait(vec)
                cT.wait(vec)
                cV.inc(vec.tensor_scalar(t1b, xch[mg], dpcols[:, mg:mg + 1], None, ALU.mult))
                cV.inc(vec.tensor_tensor(t1b, t1b, ych[mg], ALU.add))
                cV.inc(vec.tensor_tensor(prodt[mg], t1b, pzg[:, :], ALU.mult))
                pv[mg % 2] = cV.n
            ov = [dW.n, dW.n]
            for mo in range(8):
                po = pb[4 + mo % 2]
                cS.wait(te)
                cV.wait(te)
                for kt3 in range(4):
                    cT.inc(te.matmul(po[:, :], WoutSB[kt3][:, mo * 128:(mo + 1) * 128],
                                     prodt[kt3], start=(kt3 == 0), stop=(kt3 == 3)))
                cT.wait(sc)
                sc.wait_ge(dW.s, ov[mo % 2])
                cS.inc(sc.copy(otile[mo % 2], po[:, :]))
                cS.wait(gp)
                dW.incd(gp.dma_start(p.scr["oc"][mo * 128:(mo + 1) * 128, tsl], otile[mo % 2]))
                ov[mo % 2] = dW.n
    dW.wait(sy)


def _bc16(handle, row_pitch, col0, nblk):
    return bass.AP(handle, col0, [[row_pitch, 128], [1, nblk], [0, 16]])

def _blk16(handle, row_pitch, col0, nblk):
    return bass.AP(handle, col0, [[row_pitch, 128], [16, nblk], [1, 16]])


def _quant_fp4(p, W, f32_in=False):
    nc = p.nc
    vec, sc = nc.vector, nc.scalar
    nb = W // 16
    if f32_in:
        xt = p.AF[0:128, 0:W]
        xv = _blk16(p.AF, 4096, 0, nb)
        ax = p.AF[0:128, 2048:2048 + W]
    else:
        xt = p.AC[0:128, 0:W]
        xv = _blk16(p.AC, 13312, 0, nb)
        ax = p.AC[0:128, 5632:5632 + W]
    an0 = 1024 if f32_in else 0
    an = p.AF[0:128, an0:an0 + W]
    axq = p.AC[0:128, 2816:2816 + W]
    tb = p.AC[0:128, 8448:8448 + W]
    q2 = p.SM[0:128, 0:W]
    m = p.sm32[0:128, 0:nb]
    rcp6 = p.sm32[0:128, 256:256 + nb]
    shb = p.SM[0:128, 2816:2816 + nb]
    vec.tensor_reduce(m, xv, mybir.AxisListType.X, ALU.max, apply_absolute_value=True)
    vec.drain()
    vec.tensor_scalar(m, m, 1e-12, None, ALU.max)
    vec.drain()
    vec.reciprocal(rcp6, m)
    vec.drain()
    vec.tensor_scalar(rcp6, rcp6, 6.0, None, ALU.mult)
    p.cV.inc(vec.tensor_scalar(shb, m, 1.0 / 12.0, None, ALU.mult))
    p.cV.wait(sc)
    p.cS.inc(sc.activation(ax, xt, AF.Abs))
    p.cS.wait(vec)
    vec.drain()
    vec.tensor_tensor(an, ax, _bc16(p.sm32, 1024, 256, nb), ALU.mult)
    vec.tensor_scalar(q2, an, 0.25, 1.0, ALU.is_gt, ALU.mult)
    for th, wgt in [(0.75, 1.0), (1.25, 1.0), (1.75, 1.0), (2.5, 2.0), (3.5, 2.0), (5.0, 4.0)]:
        vec.tensor_scalar(tb, an, th, wgt, ALU.is_gt, ALU.mult)
        vec.tensor_tensor(q2, q2, tb, ALU.add)
    p.cV.inc(vec.engine_nop())
    p.cV.wait(sc)
    p.cS.inc(sc.sign(tb, xt))
    p.cS.wait(vec)
    vec.tensor_tensor(q2, q2, tb, ALU.mult)
    p.cV.inc(vec.tensor_tensor(axq, q2, _bc16(p.SM, 5120, 2816, nb), ALU.mult))


def _l0_ffn(p):
    nc = p.nc
    pb = p.pb
    vec, sc, te, gp, sy = nc.vector, nc.scalar, nc.tensor, nc.gpsimd, nc.sync
    dW, dX, dS, cT, cV, cS = p.dW, p.dX, p.dS, p.cT, p.cV, p.cS
    SM = p.SM

    jobs = []
    for tt in range(NT):
        jobs.append((p.scr["xln"][tt * 128:(tt + 1) * 128, :], 1024, "s_xq",
                     (tt * 128, 0), False))
    for wn, sn in [("L0_ffn_w_gate", "s_wqg"), ("L0_ffn_w_up", "s_wqu")]:
        for r in range(22):
            jobs.append((p.W[wn][r * 128:(r + 1) * 128, :], 1024, sn, (r * 128, 0), True))
    for r in range(8):
        for c0, w in [(0, 1024), (1024, 1024), (2048, 768)]:
            jobs.append((p.W["L0_ffn_w_down"][r * 128:(r + 1) * 128, c0:c0 + w], w,
                         "s_wqd", (r * 128, c0), True))
    stq = dW.n
    vq = cV.n
    sq = cS.n
    for (srcap, W, dst, (r0, c0), isw) in jobs:
        q = sy
        ch = dX
        q.wait_ge(cV.s, vq)
        q.wait_ge(cS.s, sq)
        q.wait_ge(dW.s, stq)
        xt = p.AF[0:128, 0:W] if isw else p.AC[0:128, 0:W]
        ch.incd(q.dma_start(xt, srcap))
        ch.wait(vec)
        ch.wait(sc)
        _quant_fp4(p, W, f32_in=isw)
        vq = cV.n
        sq = cS.n
        cV.wait(gp)
        dW.incd(gp.dma_start(p.scr[dst][r0:r0 + 128, c0:c0 + W],
                             p.AC[0:128, 2816:2816 + W]))
        stq = dW.n

    sgv = [cV.n, cV.n]
    hst = dW.n
    for nh in range(2):
        WG = [p.AB[0:128, kt * 1408:(kt + 1) * 1408] for kt in range(8)]
        WU = [p.AB[0:128, 11264 + kt * 1408:11264 + (kt + 1) * 1408] for kt in range(8)]
        sy.wait_ge(cT.s, cT.n)
        sy.wait_ge(dW.s, dW.n)
        for kt in range(8):
            dX.incd(sy.dma_start(WG[kt], p.scr["s_wqg"][nh * 1408:(nh + 1) * 1408,
                                                        kt * 128:(kt + 1) * 128], transpose=True))
            dX.incd(sy.dma_start(WU[kt], p.scr["s_wqu"][nh * 1408:(nh + 1) * 1408,
                                                        kt * 128:(kt + 1) * 128], transpose=True))
        for half in range(2):
            sy.wait_ge(cT.s, cT.n)
            XQ = [p.AA[0:128, kt * 2048:(kt + 1) * 2048] for kt in range(8)]
            for kt in range(8):
                dX.incd(sy.dma_start(XQ[kt], p.scr["s_xq"][half * 2048:(half + 1) * 2048,
                                                           kt * 128:(kt + 1) * 128], transpose=True))
            dX.wait(te)
            for tl in range(16):
                tt = half * 16 + tl
                h_row = p.AC[0:128, 0:1408]
                for nci, (nc0, nw) in enumerate([(0, 512), (512, 512), (1024, 384)]):
                    pg = pb[4 + nci % 2]
                    pu = pb[2 + nci % 2]
                    cV.wait(te)
                    cS.wait(te)
                    for kt in range(8):
                        cT.inc(te.matmul(pg[:, 0:nw], XQ[kt][:, tl * 128:(tl + 1) * 128],
                                         WG[kt][:, nc0:nc0 + nw], start=(kt == 0), stop=(kt == 7)))
                    for kt in range(8):
                        cT.inc(te.matmul(pu[:, 0:nw], XQ[kt][:, tl * 128:(tl + 1) * 128],
                                         WU[kt][:, nc0:nc0 + nw], start=(kt == 0), stop=(kt == 7)))
                    cT.wait(sc)
                    sg = p.AC[0:128, 11264 + (nci % 2) * 512:11264 + (nci % 2) * 512 + nw]

                    sc.wait_ge(cV.s, sgv[nci % 2])
                    cS.inc(sc.activation(sg, pg[:, 0:nw], AF.Sigmoid))
                    cS.wait(vec)
                    vec.tensor_tensor(sg, sg, pg[:, 0:nw], ALU.mult)
                    p.cV.inc(vec.tensor_tensor(h_row[:, nc0:nc0 + nw], sg, pu[:, 0:nw], ALU.mult))
                    sgv[nci % 2] = cV.n
                vec.wait_ge(dW.s, hst)
                _quant_fp4(p, 1408, f32_in=False)
                cV.wait(gp)
                dW.incd(gp.dma_start(p.scr["s_hq"][tt * 128:(tt + 1) * 128,
                                                   nh * 1408:(nh + 1) * 1408],
                                     p.AC[0:128, 2816:2816 + 1408]))
                hst = dW.n
    WD = [p.AB[0:128, kt * 1024:(kt + 1) * 1024] for kt in range(22)]
    sy.wait_ge(cT.s, cT.n)
    sy.wait_ge(dW.s, dW.n)
    for kt in range(22):
        dX.incd(sy.dma_start(WD[kt], p.scr["s_wqd"][:, kt * 128:(kt + 1) * 128], transpose=True))
    ov = [dW.n, dW.n]
    for qt in range(4):
        HQ = []
        sy.wait_ge(cT.s, cT.n)
        for kt in range(22):
            dst = (p.AA[0:128, kt * 1024:(kt + 1) * 1024] if kt < 16
                   else p.AC[0:128, (kt - 16) * 1024:(kt - 15) * 1024])
            HQ.append(dst)
            dX.incd(sy.dma_start(dst, p.scr["s_hq"][qt * 1024:(qt + 1) * 1024,
                                                    kt * 128:(kt + 1) * 128], transpose=True))
        dX.wait(te)
        for dt in range(8):
            for tch in range(2):
                po = pb[tch % 2]
                cS.wait(te)
                for kt in range(22):
                    cT.inc(te.matmul(po[:, :], WD[kt][:, dt * 128:(dt + 1) * 128],
                                     HQ[kt][:, tch * 512:(tch + 1) * 512],
                                     start=(kt == 0), stop=(kt == 21)))
                cT.wait(sc)
                otile = SM[0:128, 3072 + (tch % 2) * 512:3584 + (tch % 2) * 512]
                sc.wait_ge(dW.s, ov[tch % 2])
                cS.inc(sc.copy(otile, po[:, :]))
                cS.wait(gp)
                dW.incd(gp.dma_start(
                    p.scr["oc"][dt * 128:(dt + 1) * 128,
                                qt * 1024 + tch * 512:qt * 1024 + (tch + 1) * 512], otile))
                ov[tch % 2] = dW.n
    dW.wait(sy)


def _l1_attn(p):
    nc = p.nc
    pb, pbt = p.pb, p.pbt
    vec, sc, te, gp, sy = nc.vector, nc.scalar, nc.tensor, nc.gpsimd, nc.sync
    dW, dX, dS, cT, cV, cS = p.dW, p.dX, p.dS, p.cT, p.cV, p.cS
    SM, sm32, AC, AA, AB, AFX = p.SM, p.sm32, p.AC, p.AA, p.AB, p.AF

    WSB = [AC[0:128, kt * 1024:(kt + 1) * 1024] for kt in range(8)]
    csrc2 = [AC[0:128, 8192 + j * 515:8707 + j * 515] for j in range(2)]
    convT = lambda wi, mt: sm32[0:128, (wi * 8 + mt) * 4:(wi * 8 + mt) * 4 + 4]
    WbSB = [AC[0:128, 9318 + kt * 4:9322 + kt * 4] for kt in range(8)]
    halo = lambda wi, mt: AC[0:128, 9350 + (wi * 8 + mt) * 3:9353 + (wi * 8 + mt) * 3]
    onesc = AC[0:128, 9422:9423]
    acc = SM[0:128, 0:512]
    tmp = SM[0:128, 512:1024]
    sg2 = [SM[0:128, 1024 + j * 512:1536 + j * 512] for j in range(2)]
    stash2 = [SM[0:128, 2048 + j * 512:2560 + j * 512] for j in range(2)]
    q2b2 = [SM[0:128, 3072 + j * 512:3584 + j * 512] for j in range(2)]
    rsqb = SM[0:1, 4096:4608]
    nbetaT = sm32[0:128, 384:512]
    betaT = sm32[0:128, 256:384]
    cvst = SM[0:4, 0:3072]
    bfm = p.bfm
    wnames = ["L1_attn_Wq", "L1_attn_Wk", "L1_attn_Wv"]
    cnames = ["L1_attn_conv_q", "L1_attn_conv_k", "L1_attn_conv_v"]

    for wi in range(3):
        dW.incd(gp.dma_start(cvst[:, wi * 1024:(wi + 1) * 1024], p.W[cnames[wi]]))
    for kt in range(8):
        dW.incd(gp.dma_start(WbSB[kt], p.W["L1_attn_Wb"][kt * 128:(kt + 1) * 128, :]))
    cV.inc(vec.memset(AC[0:128, 9350:9422], 0.0))
    cV.inc(vec.memset(onesc, 1.0))
    dW.wait(te)
    cV.wait(te)
    cS.wait(te)
    for wi in range(3):
        for mt in range(8):
            j4 = (wi * 8 + mt) * 4
            cT.inc(te.transpose(pbt[:, j4:j4 + 4],
                                cvst[0:4, wi * 1024 + mt * 128:wi * 1024 + (mt + 1) * 128],
                                p.cst["c_ident"][0:4, 0:4]))
    cT.wait(vec)
    cV.inc(vec.tensor_copy(sm32[0:128, 0:96], pbt[:, 0:96]))
    cvev = cV.n

    qst = [dW.n] * 2
    sgv = [cV.n] * 2
    csv = [cV.n] * 2
    pzm = [cS.n] * 2
    q2m = [cT.n] * 2
    accm = cS.n
    p2m = cV.n
    p3m = cV.n
    bcm = cT.n
    p4m = cS.n
    for half in range(2):
        _load_xlf(p, half)
        XLF = [AA[0:128, i * 2048:(i + 1) * 2048] for i in range(8)]
        for wi in range(3):
            wmark = cT.n
            for kt in range(8):
                gp.wait_ge(cT.s, wmark)
                dW.incd(gp.dma_start(WSB[kt], p.W[wnames[wi]][kt * 128:(kt + 1) * 128, :]))
            for cc in range(4):
                csl = slice(cc * 512, (cc + 1) * 512)
                g0 = half * 2048 + cc * 512
                dX.wait(te)
                dW.wait(te)
                for h4 in range(4):
                    for mi in range(2):
                        mt = 2 * h4 + mi
                        cb = mt % 2
                        pz = pb[mt % 2]
                        cs = csrc2[cb]
                        te.wait_ge(cS.s, pzm[mt % 2])
                        for kt in range(8):
                            cT.inc(te.matmul(pz[:, :], WSB[kt][:, mt * 128:(mt + 1) * 128],
                                             XLF[kt][:, csl], start=(kt == 0), stop=(kt == 7)))
                        sc.wait_ge(cV.s, csv[cb])
                        cS.inc(sc.copy(cs[:, 0:3], halo(wi, mt)))
                        cT.wait(sc)
                        cS.inc(sc.copy(cs[:, 3:515], pz[:, :]))
                        pzm[mt % 2] = cS.n
                        cS.wait(vec)
                        wc = convT(wi, mt)
                        vec.tensor_scalar(acc, cs[:, 0:512], wc[:, 0:1], None, ALU.mult)
                        for j in (1, 2, 3):
                            vec.tensor_scalar(tmp, cs[:, j:j + 512], wc[:, j:j + 1],
                                              None, ALU.mult)
                            vec.tensor_tensor(acc, acc, tmp, ALU.add)
                        cV.inc(vec.tensor_copy(halo(wi, mt), cs[:, 512:515]))
                        csv[cb] = cV.n
                        sg = sg2[mi]
                        cV.wait(sc)
                        cS.inc(sc.activation(sg, acc, AF.Sigmoid))
                        accm = cS.n
                        cS.wait(vec)
                        st = stash2[mi]
                        vec.wait_ge(dW.s, qst[mi])
                        cV.inc(vec.tensor_tensor(st, sg, acc, ALU.mult))
                        sgv[mi] = cV.n
                        if wi < 2:
                            cV.wait(sc)
                            sc.wait_ge(cT.s, q2m[mi])
                            cS.inc(sc.activation(q2b2[mi], st, AF.Square))
                            cS.wait(te)
                            te.wait_ge(cV.s, p2m)
                            cT.inc(te.matmul(pb[2][0:1, 0:512], onesc, q2b2[mi],
                                             start=(mi == 0), stop=(mi == 1)))
                            q2m[mi] = cT.n
                        else:
                            cV.wait(gp)
                            dW.incd(gp.dma_start(
                                p.scr["s_vv"][mt * 128:(mt + 1) * 128, g0:g0 + 512], st))
                            qst[mi] = dW.n
                    if wi < 2:
                        cT.wait(vec)
                        cV.inc(vec.tensor_scalar(AFX[0:1, 0:512], pb[2][0:1, 0:512],
                                                 1e-6, None, ALU.add))
                        p2m = cV.n
                        vec.drain()
                        vec.reciprocal(AFX[0:1, 512:1024], AFX[0:1, 0:512])
                        cV.inc(vec.engine_nop())
                        cV.wait(sc)
                        sc.wait_ge(cT.s, bcm)
                        cS.inc(sc.activation(rsqb, AFX[0:1, 512:1024], AF.Sqrt))
                        cS.wait(te)
                        te.wait_ge(cV.s, p3m)
                        cT.inc(te.matmul(pb[3][:, :], p.cst["c_ones_row"][0:1, :], rsqb,
                                         start=True, stop=True))
                        bcm = cT.n
                        cT.wait(vec)
                        for mi in range(2):
                            mt = 2 * h4 + mi
                            st = stash2[mi]
                            if wi == 0:
                                cV.inc(vec.tensor_tensor(
                                    AB[0:128, mt * 4096 + g0:mt * 4096 + g0 + 512],
                                    st, pb[3][:, :], ALU.mult))
                            else:
                                cV.inc(vec.tensor_tensor(st, st, pb[3][:, :], ALU.mult))
                                cV.wait(gp)
                                dW.incd(gp.dma_start(
                                    p.scr["s_kk"][mt * 128:(mt + 1) * 128, g0:g0 + 512], st))
                                qst[mi] = dW.n
                        p3m = cV.n
        for cc in range(4):
            csl = slice(cc * 512, (cc + 1) * 512)
            g0 = half * 2048 + cc * 512
            te.wait_ge(cS.s, p4m)
            for kt in range(8):
                cT.inc(te.matmul(pb[4][0:4, :], WbSB[kt], XLF[kt][:, csl],
                                 start=(kt == 0), stop=(kt == 7)))
            cT.wait(sc)
            cS.inc(sc.activation(bfm[0:4, g0:g0 + 512], pb[4][0:4, :], AF.Sigmoid))
            p4m = cS.n
    cS.wait(te)
    cV.wait(te)
    for sco in range(32):
        cT.inc(te.transpose(pbt[:, 4 * sco:4 * sco + 4], bfm[0:4, sco * 128:(sco + 1) * 128],
                            p.cst["c_ident"][0:4, 0:4]))
    cT.wait(vec)
    cV.inc(vec.tensor_copy(betaT, pbt[:, 0:128]))
    vec.tensor_scalar(nbetaT, betaT, -1.0, None, ALU.mult)
    cV.inc(vec.engine_nop())

    S_f = lambda h, kt: AFX[0:128, (2 * h + kt) * 256:(2 * h + kt + 1) * 256]
    S_b = lambda h, kt: AC[0:128, (2 * h + kt) * 256:(2 * h + kt + 1) * 256]
    ktm2 = [AC[0:128, 2048 + j * 1024:3072 + j * 1024] for j in range(2)]
    vtm2 = [AC[0:128, 4096 + j * 1024:5120 + j * 1024] for j in range(2)]
    Dl = [SM[0:128, 0:256], SM[0:128, 256:512]]
    Cp = [(SM[0:128, 512:640], SM[0:128, 640:768]),
          (SM[0:128, 768:896], SM[0:128, 896:1024])]
    Cpair = [SM[0:128, 512:768], SM[0:128, 768:1024]]
    PTb = SM[0:128, 1024:1152]
    scrap = SM[0:128, 1152:1408]
    Rv = SM[0:128, 1408:1664]
    och = SM[0:128, 1664:2688]
    dW.wait(sy)
    dW.wait(te)
    cV.wait(te)
    cS.wait(te)
    cV.inc(vec.memset(AFX[0:128, 0:2048], 0.0))
    cV.inc(vec.memset(AC[0:128, 0:2048], 0.0))
    p0m = cV.n
    ptm = cV.n
    p4m2 = cV.n
    p5mS = cS.n
    p5mV = cV.n
    p6m = cV.n
    pbtm = cS.n
    p3evV = cV.n
    p1mV = cV.n
    p1mS = cS.n
    sbm = [cS.n] * 4
    ost = dW.n
    kvmark = [cT.n, cT.n]

    def twait(dme, dm):
        te.wait_ge(cV.s if dme == "V" else cS.s, dm)

    for sci in range(32):
        half = sci // 16
        if sci % 16 == 0:
            sy.wait_ge(cT.s, cT.n)
            for kt2 in range(8):
                dX.incd(sy.dma_start(AA[:, kt2 * 2048:(kt2 + 1) * 2048],
                                     p.scr["s_kk"][kt2 * 128:(kt2 + 1) * 128,
                                                   half * 2048:(half + 1) * 2048]))
        b = sci % 2
        sy.wait_ge(cT.s, kvmark[b])
        tsl = slice(sci * 128, (sci + 1) * 128)
        dX.incd(sy.dma_start(ktm2[b], p.scr["s_kk"][:, tsl], transpose=True))
        dX.incd(sy.dma_start(vtm2[b], p.scr["s_vv"][:, tsl], transpose=True))
        dX.wait(te)
        dX.wait(vec)
        lsl = (sci % 16) * 128
        for h in range(4):
            kf = [AA[0:128, (2 * h + kt) * 2048 + lsl:(2 * h + kt) * 2048 + lsl + 128]
                  for kt in range(2)]
            qf = [AB[0:128, (2 * h + kt) * 4096 + sci * 128:(2 * h + kt) * 4096 + sci * 128 + 128]
                  for kt in range(2)]
            bcol = betaT[:, 4 * sci + h:4 * sci + h + 1]
            nbcol = nbetaT[:, 4 * sci + h:4 * sci + h + 1]
            te.wait_ge(cV.s, max(p0m, ptm))
            for kt in range(2):
                cT.inc(te.matmul(pb[0][:, 0:128], kf[kt], kf[kt],
                                 start=(kt == 0), stop=(kt == 1)))
            for kt in range(2):
                cT.inc(te.matmul(pb[0][:, 128:256], kf[kt], qf[kt],
                                 start=(kt == 0), stop=(kt == 1)))
            te.wait_ge(cS.s, sbm[h])
            te.wait_ge(cV.s, p4m2)
            for kt in range(2):
                cT.inc(te.matmul(pb[4][:, 0:256], kf[kt], S_b(h, kt),
                                 start=(kt == 0), stop=(kt == 1)))
            cT.wait(vec)
            vec.wait_ge(cS.s, p1mS)
            C0, C0T = Cp[0]
            vec.tensor_scalar(C0, pb[0][:, 0:128], nbcol, None, ALU.mult)
            cV.inc(vec.tensor_tensor(C0, C0, p.cst["c_trilS"][:, :], ALU.mult))
            c1m = cV.n
            cV.inc(vec.tensor_tensor(PTb, pb[0][:, 128:256], p.cst["c_triuI"][:, :], ALU.mult))
            p0m = cV.n
            ptm = cV.n
            vec.tensor_scalar(Rv, vtm2[b][:, 256 * h:256 * h + 256], bcol, None, ALU.mult)
            cV.inc(vec.scalar_tensor_tensor(Dl[0], pb[4][:, 0:256], nbcol, Rv,
                                            ALU.mult, ALU.add))
            p4m2 = cV.n
            dme, dm = "V", cV.n
            te.wait_ge(cV.s, c1m)
            te.wait_ge(cS.s, pbtm)
            cT.inc(te.transpose(pbt[:, 0:128], C0, p.cst["c_ident"][:, :]))
            cT.wait(sc)
            cS.inc(sc.copy(C0T, pbt[:, 0:128]))
            pbtm = cS.n
            ctme, ctm = ["S", None], [cS.n, 0]
            for i in range(7):
                cur = i % 2
                nxt = 1 - cur
                Cc, CcT = Cp[cur]
                te.wait_ge(cV.s, p1mV)
                te.wait_ge(cS.s, p1mS)
                twait(ctme[cur], ctm[cur])
                twait(dme, dm)
                cT.inc(te.matmul(pb[1][:, 0:256], p.cst["c_ident"][:, :], Dl[cur],
                                 start=True, stop=False))
                cT.inc(te.matmul(pb[1][:, 0:256], CcT, Dl[cur], start=False, stop=True))
                am = cT.n
                if i < 6:
                    te.wait_ge(cV.s, p3evV)
                    cT.inc(te.matmul(pb[3][:, 0:128], CcT, Cc, start=True, stop=True))
                    cT.inc(te.matmul(pb[3][:, 128:256], Cc, CcT, start=True, stop=True))
                    sqm = cT.n
                if i % 2 == 0:
                    vec.wait_ge(cT.s, am)
                    cV.inc(vec.tensor_copy(Dl[nxt], pb[1][:, 0:256]))
                    p1mV = cV.n
                    dme, dm = "V", cV.n
                else:
                    sc.wait_ge(cT.s, am)
                    cS.inc(sc.copy(Dl[nxt], pb[1][:, 0:256]))
                    p1mS = cS.n
                    dme, dm = "S", cS.n
                if i < 6:
                    vec.wait_ge(cT.s, sqm)
                    cV.inc(vec.tensor_copy(Cpair[nxt], pb[3][:, 0:256]))
                    p3evV = cV.n
                    ctme[nxt], ctm[nxt] = "V", cV.n
            te.wait_ge(cS.s, p5mS)
            te.wait_ge(cV.s, p5mV)
            twait(dme, dm)
            cT.inc(te.matmul(pb[5][:, 0:256], PTb, Dl[1], start=True, stop=False))
            for kt in range(2):
                cT.inc(te.matmul(pb[5][:, 0:256], qf[kt], S_b(h, kt),
                                 start=False, stop=(kt == 1)))
            om = cT.n
            sc.wait_ge(cT.s, om)
            cS.inc(sc.activation(scrap, pb[5][:, 0:256], AF.Square,
                                 accum_out=sm32[0:128, 16 + h:17 + h]))
            p5mS = cS.n
            cS.wait(vec)
            vec.tensor_scalar(sm32[:, 20 + h:21 + h], sm32[:, 16 + h:17 + h],
                              1.0 / 256.0, 1e-5, ALU.mult, ALU.add)
            vec.drain()
            cV.inc(vec.reciprocal(sm32[:, 24 + h:25 + h], sm32[:, 20 + h:21 + h]))
            cV.wait(sc)
            cS.inc(sc.activation(sm32[:, 28 + h:29 + h], sm32[:, 24 + h:25 + h], AF.Sqrt))
            cS.wait(vec)
            if h == 0:
                vec.wait_ge(dW.s, ost)
            osl = och[:, 256 * h:256 * h + 256]
            vec.tensor_scalar(osl, pb[5][:, 0:256], sm32[:, 28 + h:29 + h], None, ALU.mult)
            cV.inc(vec.tensor_tensor(osl, osl, p.grms[:, 0:256], ALU.mult))
            p5mV = cV.n
            te.wait_ge(cV.s, p6m)
            twait(dme, dm)
            for kt in range(2):
                cT.inc(te.matmul(pb[6][:, kt * 256:(kt + 1) * 256],
                                 ktm2[b][:, 256 * h + kt * 128:256 * h + (kt + 1) * 128],
                                 Dl[1], start=True, stop=True))
            kvmark[b] = cT.n
            cT.wait(vec)
            for kt in range(2):
                vec.tensor_tensor(S_f(h, kt), S_f(h, kt), pb[6][:, kt * 256:(kt + 1) * 256],
                                  ALU.add)
            cV.inc(vec.engine_nop())
            p6m = cV.n
            cV.wait(sc)
            for kt in range(2):
                cS.inc(sc.copy(S_b(h, kt), S_f(h, kt)))
            sbm[h] = cS.n
        cV.wait(gp)
        dW.incd(gp.dma_start(p.scr["s_ot"][tsl, :], och))
        ost = dW.n

    WoSB = [AC[0:128, kt * 1024:(kt + 1) * 1024] for kt in range(8)]
    wmark2 = cT.n
    for kt in range(8):
        gp.wait_ge(cT.s, wmark2)
        gp.wait_ge(cS.s, cS.n)
        dW.incd(gp.dma_start(WoSB[kt], p.W["L1_attn_Wo"][kt * 128:(kt + 1) * 128, :]))
    ovm = [dW.n, dW.n]
    pzm2 = [cS.n] * 2
    for half in range(2):
        sy.wait_ge(cT.s, cT.n)
        sy.wait_ge(dW.s, dW.n)
        for i in range(8):
            dX.incd(sy.dma_start(AA[:, i * 2048:(i + 1) * 2048],
                                 p.scr["s_ot"][half * 2048:(half + 1) * 2048,
                                               i * 128:(i + 1) * 128], transpose=True))
        dX.wait(te)
        dW.wait(te)
        for mo in range(8):
            for cc in range(4):
                po = pb[cc % 2]
                te.wait_ge(cS.s, pzm2[cc % 2])
                for kt in range(8):
                    cT.inc(te.matmul(po[:, :], WoSB[kt][:, mo * 128:(mo + 1) * 128],
                                     AA[0:128, kt * 2048 + cc * 512:kt * 2048 + (cc + 1) * 512],
                                     start=(kt == 0), stop=(kt == 7)))
                cT.wait(sc)
                otile = SM[0:128, (cc % 2) * 512:512 + (cc % 2) * 512]
                sc.wait_ge(dW.s, ovm[cc % 2])
                cS.inc(sc.copy(otile, po[:, :]))
                pzm2[cc % 2] = cS.n
                cS.wait(gp)
                dW.incd(gp.dma_start(
                    p.scr["oc"][mo * 128:(mo + 1) * 128,
                                half * 2048 + cc * 512:half * 2048 + (cc + 1) * 512], otile))
                ovm[cc % 2] = dW.n
    dW.wait(sy)


def _l1_ffn(p):
    nc = p.nc
    pb = p.pb
    vec, sc, te, gp, sy = nc.vector, nc.scalar, nc.tensor, nc.gpsimd, nc.sync
    dW, dX, dS, cT, cV, cS = p.dW, p.dX, p.dS, p.cT, p.cV, p.cS
    SM, sm32, AC, AA, AB, AFX = p.SM, p.sm32, p.AC, p.AA, p.AB, p.AF
    NW = 4096 * 1024

    scrap = AC[0:128, 0:4096]
    lnwrow = SM[0:1, 0:4096]
    lnwbc = AC[0:128, 8192:12288]
    dW.incd(gp.dma_start(lnwrow, p.W["L1_ffn_ln_w"][None, :]))
    dW.wait(te)
    cV.wait(te)
    cS.wait(te)
    for cb2 in range(8):
        cT.inc(te.matmul(pb[3][:, :], p.cst["c_ones_row"][0:1, :],
                         lnwrow[0:1, cb2 * 512:(cb2 + 1) * 512], start=True, stop=True))
        cT.wait(vec)
        cV.inc(vec.tensor_copy(lnwbc[:, cb2 * 512:(cb2 + 1) * 512], pb[3][:, :]))
        cV.wait(te)
    fcb = [AFX[0:128, 0:1024], AFX[0:128, 1024:2048]]
    pjb = [AFX[0:128, 0:2048], AFX[0:128, 2048:4096]]
    ldm = [dX.n, dX.n]
    csm = [cS.n, cS.n]
    for i in range(32):
        b = i % 2
        sy.wait_ge(cS.s, csm[b])
        dX.incd(sy.dma_start(fcb[b], p.W["L1_ffn_W_fc"][i * 128:(i + 1) * 128, :]))
        ldm[b] = dX.n
        sc.wait_ge(dX.s, ldm[b])
        cS.inc(sc.activation(scrap[:, 0:1024], fcb[b], AF.Abs,
                             accum_out=sm32[0:128, i:i + 1]))
        csm[b] = cS.n
    csm = [cS.n, cS.n]
    for i in range(16):
        r, hh = i // 2, i % 2
        b = i % 2
        sy.wait_ge(cS.s, csm[b])
        dX.incd(sy.dma_start(pjb[b], p.W["L1_ffn_W_proj"][r * 128:(r + 1) * 128,
                                                          hh * 2048:(hh + 1) * 2048]))
        ldm[b] = dX.n
        sc.wait_ge(dX.s, ldm[b])
        cS.inc(sc.activation(scrap[:, 0:2048], pjb[b], AF.Abs,
                             accum_out=sm32[0:128, 32 + i:33 + i]))
        csm[b] = cS.n
    cS.wait(vec)
    vec.reduce_sum(sm32[0:128, 56:57], sm32[0:128, 0:32], mybir.AxisListType.X)
    vec.reduce_sum(sm32[0:128, 57:58], sm32[0:128, 32:48], mybir.AxisListType.X)
    vec.memset(sm32[0:128, 58:59], 1.0)
    cV.inc(vec.memset(sm32[0:1, 64:192], 1.0))
    cV.wait(te)
    cT.inc(te.matmul(pb[2][0:1, 0:2], sm32[0:128, 58:59], sm32[0:128, 56:58],
                     start=True, stop=True))
    cT.wait(vec)
    vec.tensor_scalar(sm32[0:1, 60:62], pb[2][0:1, 0:2], 1.0 / NW, 1e-6, ALU.mult, ALU.add)
    vec.drain()
    vec.reciprocal(sm32[0:1, 62:64], sm32[0:1, 60:62])
    vec.drain()
    cV.inc(vec.tensor_scalar(sm32[0:1, 62:64], sm32[0:1, 62:64], 2.0, None, ALU.mult))
    cV.wait(te)
    cT.inc(te.matmul(pb[2][:, 8:10], sm32[0:1, 64:192], sm32[0:1, 62:64],
                     start=True, stop=True))
    cT.wait(vec)
    cV.inc(vec.tensor_copy(sm32[0:128, 48:50], pb[2][:, 8:10]))
    rgfc = sm32[0:128, 48:49]
    rgpj = sm32[0:128, 49:50]
    stm = [dW.n, dW.n]
    vcm = [cV.n, cV.n]
    m1f = [SM[0:128, 0:1024], SM[0:128, 1024:2048]]
    m2f = SM[0:128, 2048:3072]
    wqf = [SM[0:128, 3072:4096], SM[0:128, 4096:5120]]
    for i in range(32):
        b = i % 2
        sy.wait_ge(cV.s, vcm[b])
        dX.incd(sy.dma_start(fcb[b], p.W["L1_ffn_W_fc"][i * 128:(i + 1) * 128, :]))
        ldm[b] = dX.n
        vec.wait_ge(dX.s, ldm[b])
        vec.wait_ge(dW.s, stm[b])
        vec.tensor_scalar(m1f[b], fcb[b], rgfc, 1.0, ALU.mult, ALU.is_gt)
        vec.tensor_scalar(m2f, fcb[b], rgfc, -1.0, ALU.mult, ALU.is_lt)
        cV.inc(vec.tensor_tensor(wqf[b], m1f[b], m2f, ALU.subtract))
        vcm[b] = cV.n
        cV.wait(gp)
        dW.incd(gp.dma_start(p.scr["s_wf"][i * 128:(i + 1) * 128, :], wqf[b]))
        stm[b] = dW.n
    m1p = [SM[0:128, 0:2048], SM[0:128, 2048:4096]]
    dW.wait(vec)
    vcm = [cV.n, cV.n]
    stm = [dW.n, dW.n]
    for i in range(16):
        r, hh = i // 2, i % 2
        b = i % 2
        sy.wait_ge(cV.s, vcm[b])
        dX.incd(sy.dma_start(pjb[b], p.W["L1_ffn_W_proj"][r * 128:(r + 1) * 128,
                                                          hh * 2048:(hh + 1) * 2048]))
        ldm[b] = dX.n
        vec.wait_ge(dX.s, ldm[b])
        vec.wait_ge(dW.s, stm[b])
        vec.tensor_scalar(m1p[b], pjb[b], rgpj, 1.0, ALU.mult, ALU.is_gt)
        vec.tensor_scalar(scrap[:, 0:2048], pjb[b], rgpj, -1.0, ALU.mult, ALU.is_lt)
        cV.inc(vec.tensor_tensor(m1p[b], m1p[b], scrap[:, 0:2048], ALU.subtract))
        vcm[b] = cV.n
        cV.wait(gp)
        dW.incd(gp.dma_start(p.scr["s_wp"][r * 128:(r + 1) * 128,
                                           hh * 2048:(hh + 1) * 2048], m1p[b]))
        stm[b] = dW.n

    sy.wait_ge(cT.s, cT.n)
    sy.wait_ge(dW.s, dW.n)
    WF = [AB[0:128, dt * 4096:(dt + 1) * 4096] for dt in range(8)]
    for dt in range(8):
        dX.incd(sy.dma_start(WF[dt], p.scr["s_wf"][:, dt * 128:(dt + 1) * 128],
                             transpose=True))
    vt = AC[0:128, 4096:8192]
    zt = SM[0:128, 0:4096]
    c = p.sc1
    pzm = [cS.n, cS.n]
    acm = cV.n
    zsm = dW.n
    for half in range(2):
        _load_xlf(p, half)
        XLF = [AA[0:128, i * 2048:(i + 1) * 2048] for i in range(8)]
        dX.wait(te)
        for tl in range(16):
            tt = half * 16 + tl
            for ch in range(8):
                pz = pb[ch % 2]
                te.wait_ge(cS.s, pzm[ch % 2])
                for kt in range(8):
                    cT.inc(te.matmul(pz[:, :], XLF[kt][:, tl * 128:(tl + 1) * 128],
                                     AB[0:128, kt * 4096 + ch * 512:kt * 4096 + (ch + 1) * 512],
                                     start=(kt == 0), stop=(kt == 7)))
                cT.wait(sc)
                if ch == 0:
                    sc.wait_ge(cV.s, acm)
                cS.inc(sc.activation(vt[:, ch * 512:(ch + 1) * 512], pz[:, :], AF.Gelu,
                                     accum_out=sm32[0:128, 70 + ch:71 + ch]))
                pzm[ch % 2] = cS.n
            cS.inc(sc.activation(scrap[:, 0:4096], vt, AF.Square,
                                 accum_out=sm32[0:128, 78:79]))
            cS.wait(vec)
            cV.inc(vec.reduce_sum(c[:, 0:1], sm32[0:128, 70:78], mybir.AxisListType.X))
            cV.wait(sc)
            cS.inc(sc.activation(c[:, 2:3], c[:, 0:1], AF.Copy, scale=1.0 / 4096.0))
            cS.wait(vec)
            cV.inc(vec.engine_nop())
            cV.wait(sc)
            cS.inc(sc.activation(c[:, 3:4], c[:, 2:3], AF.Square))
            cS.wait(vec)
            vec.scalar_tensor_tensor(c[:, 4:5], sm32[0:128, 78:79], 1.0 / 4096.0,
                                     c[:, 3:4], ALU.mult, ALU.subtract)
            vec.drain()
            vec.tensor_scalar(c[:, 5:6], c[:, 4:5], 1e-5, None, ALU.add)
            vec.drain()
            cV.inc(vec.reciprocal(c[:, 6:7], c[:, 5:6]))
            cV.wait(sc)
            cS.inc(sc.activation(c[:, 7:8], c[:, 6:7], AF.Sqrt))
            cS.wait(vec)
            vec.wait_ge(dW.s, zsm)
            vec.tensor_scalar(zt, vt, c[:, 2:3], c[:, 7:8],
                              ALU.subtract, ALU.mult)
            cV.inc(vec.tensor_tensor(zt, zt, lnwbc, ALU.mult))
            acm = cV.n
            cV.wait(gp)
            dW.incd(gp.dma_start(p.scr["s_h1"][tt * 128:(tt + 1) * 128, :], zt))
            zsm = dW.n

    sy.wait_ge(cT.s, cT.n)
    sy.wait_ge(dW.s, dW.n)
    WP = [AB[0:128, kt * 1024:(kt + 1) * 1024] for kt in range(32)]
    for kt in range(32):
        dX.incd(sy.dma_start(WP[kt], p.scr["s_wp"][:, kt * 128:(kt + 1) * 128],
                             transpose=True))
    lnbT = SM[0:128, 4096:4128]
    for kt in range(32):
        dW.incd(gp.dma_start(lnbT[:, kt:kt + 1],
                             p.W["L1_ffn_ln_b"][kt * 128:(kt + 1) * 128][:, None]))
    dX.wait(te)
    dW.wait(te)
    for dt in range(8):
        for kt in range(32):
            cT.inc(te.matmul(pb[2][:, dt:dt + 1], WP[kt][:, dt * 128:(dt + 1) * 128],
                             lnbT[:, kt:kt + 1], start=(kt == 0), stop=(kt == 31)))
    cT.wait(vec)
    cV.inc(vec.tensor_copy(sm32[0:128, 80:88], pb[2][:, 0:8]))
    bvm = cV.n
    ovm = [dW.n, dW.n]
    pzm2 = [cV.n, cV.n]
    for grp in range(8):
        sy.wait_ge(cT.s, cT.n)
        for kt in range(32):
            dX.incd(sy.dma_start(AA[0:128, kt * 512:(kt + 1) * 512],
                                 p.scr["s_h1"][grp * 512:(grp + 1) * 512,
                                               kt * 128:(kt + 1) * 128], transpose=True))
        dX.wait(te)
        for dt in range(8):
            pz = pb[dt % 2]
            te.wait_ge(cV.s, pzm2[dt % 2])
            for kt in range(32):
                cT.inc(te.matmul(pz[:, :], WP[kt][:, dt * 128:(dt + 1) * 128],
                                 AA[0:128, kt * 512:(kt + 1) * 512],
                                 start=(kt == 0), stop=(kt == 31)))
            cT.wait(vec)
            vec.wait_ge(cV.s, bvm)
            otile = SM[0:128, (dt % 2) * 512:512 + (dt % 2) * 512]
            vec.wait_ge(dW.s, ovm[dt % 2])
            cV.inc(vec.tensor_scalar(otile, pz[:, :], sm32[0:128, 80 + dt:81 + dt],
                                     None, ALU.add))
            pzm2[dt % 2] = cV.n
            cV.wait(gp)
            dW.incd(gp.dma_start(p.scr["oc"][dt * 128:(dt + 1) * 128,
                                             grp * 512:(grp + 1) * 512], otile))
            ovm[dt % 2] = dW.n
    dW.wait(sy)


def kernel(**inputs):
    nc = build()
    consts = _consts()
    in_maps = []
    x = np.ascontiguousarray(inputs["L0_attn_x"])
    for core in range(8):
        m = {"x0": x[core]}
        for k, v in inputs.items():
            if k == "L0_attn_x":
                continue
            m[k] = np.ascontiguousarray(v)
        for k, v in consts.items():
            m[k] = v
        in_maps.append(m)
    res = run_bass_kernel_spmd(nc, in_maps, core_ids=list(range(8)))
    out = np.stack([res.results[i]["out"] for i in range(8)], axis=0)
    return out.astype(np.float32)


# version 02
# speedup vs baseline: 1.0517x; 1.0517x over previous
import numpy as np
import ml_dtypes
import concourse.bass as bass
import concourse.mybir as mybir
from concourse.bass_utils import run_bass_kernel_spmd

F32 = mybir.dt.float32
BF16 = mybir.dt.bfloat16
AF = mybir.ActivationFunctionType
ALU = mybir.AluOpType

T = 4096
D = 1024
NT = 32
L = 128
BF = ml_dtypes.bfloat16

_DEV_STOP = None
_DEV_DBG = None


class Ch:
    def __init__(self, ctx, nc, name):
        self.s = ctx.enter_context(nc.semaphore(name))
        self.n = 0

    def inc(self, ins, k=1):
        ins.then_inc(self.s, k)
        self.n += k
        return ins

    def incd(self, ins):
        return self.inc(ins, 16)

    def wait(self, eng, at=None):
        eng.wait_ge(self.s, self.n if at is None else at)


def _consts():
    i = np.arange(L)
    c = {}
    c["c_ident"] = np.eye(L).astype(BF)
    c["c_identf"] = np.eye(L).astype(np.float32)
    c["c_maskb"] = np.where(i[:, None] <= i[None, :], 0.0, -30000.0).astype(np.float32)
    c["c_trilS"] = (i[None, :] < i[:, None]).astype(BF)
    c["c_triuI"] = (i[:, None] <= i[None, :]).astype(BF)
    c["c_ones_row"] = np.ones((1, L), dtype=BF)
    c["c_sel8"] = (np.arange(512)[None, :] // 64 == np.arange(8)[:, None]).astype(BF)
    c["c_oneh"] = (np.arange(1024)[None, :] // 128 == np.arange(8)[:, None]).astype(BF)
    return c


class P:
    pass


def build():
    nc = bass.Bass("TRN2", target_bir_lowering=False, detect_race_conditions=False)
    p = P()
    p.nc = nc
    ap = lambda name, shape, dt, kind: nc.dram_tensor(name, shape, dt, kind=kind).ap()
    p.x0 = ap("x0", [T, D], F32, "ExternalInput")
    p.W = {}
    for name, shape in [
        ("L0_attn_Wz", [D, 2048]), ("L0_attn_Win", [D, 648]), ("L0_attn_Wzg", [2048, 512]),
        ("L0_attn_Wout", [512, D]), ("L0_attn_dt_bias", [8]), ("L0_attn_A_log", [8]),
        ("L0_attn_D_param", [8]),
        ("L0_ffn_w_gate", [2816, D]), ("L0_ffn_w_up", [2816, D]), ("L0_ffn_w_down", [D, 2816]),
        ("L1_attn_Wq", [D, D]), ("L1_attn_Wk", [D, D]), ("L1_attn_Wv", [D, D]),
        ("L1_attn_Wb", [D, 4]), ("L1_attn_conv_q", [4, D]), ("L1_attn_conv_k", [4, D]),
        ("L1_attn_conv_v", [4, D]), ("L1_attn_g_rms", [256]), ("L1_attn_Wo", [D, D]),
        ("L1_attn_ln_g", [D]), ("L1_attn_ln_b", [D]),
        ("L1_ffn_W_fc", [4096, D]), ("L1_ffn_ln_w", [4096]), ("L1_ffn_ln_b", [4096]),
        ("L1_ffn_W_proj", [D, 4096]),
    ]:
        p.W[name] = ap(name, shape, F32, "ExternalInput")
    p.C = {}
    for name, arr in _consts().items():
        p.C[name] = ap(name, list(arr.shape), BF16 if arr.dtype == BF else F32, "ExternalInput")
    p.out = ap("out", [T, D], F32, "ExternalOutput")
    p.dbg = ap("dbg", [4096, 4096], F32, "ExternalOutput") if _DEV_DBG else None
    p.scr = {}
    for name, shape, dt in [
        ("xres", [T, D], F32), ("xln", [T, D], BF16), ("oc", [D, T], BF16),
        ("s_xt", [512, T], BF16), ("s_xx", [512, T], BF16), ("s_y0", [512, T], BF16),
        ("s_xq", [T, D], BF16), ("s_wqg", [2816, D], BF16), ("s_wqu", [2816, D], BF16),
        ("s_wqd", [D, 2816], BF16), ("s_hq", [T, 2816], BF16),
        ("s_kk", [D, T], BF16), ("s_vv", [D, T], BF16), ("s_ot", [T, D], BF16),
        ("s_wf", [4096, D], BF16), ("s_wp", [D, 4096], BF16), ("s_h1", [T, 4096], BF16),
    ]:
        p.scr[name] = ap(name, shape, dt, "Internal")

    sb = lambda name, shape, dt: nc.alloc_sbuf_tensor(name, shape, dt)
    p.AA = sb("AA", [128, 16384], BF16)
    p.AB = sb("AB", [128, 32768], BF16)
    p.AC = sb("AC", [128, 13312], BF16)
    p.AF = sb("AFX", [128, 4096], F32)
    p.SM = sb("SMB", [128, 5120], BF16)
    p.xtm = [sb(f"xtm{i}", [128, D], F32) for i in range(2)]
    p.otm = [sb(f"otm{i}", [128, D], BF16) for i in range(2)]
    p.st48 = sb("st48", [128, 48], F32)
    p.agg2 = sb("agg2", [128, 2], F32)
    p.sc1 = sb("sc1", [128, 8], F32)
    p.lng = sb("lng", [128, D], BF16)
    p.lnb = sb("lnb", [128, D], BF16)
    p.grms = sb("grms", [128, D], BF16)
    p.sm32 = sb("sm32", [128, 1024], F32)
    p.bfm = sb("bfm", [4, 4096], BF16)
    p.cst = {}
    for n, sh, dt in [("c_ident", [L, L], BF16), ("c_identf", [L, L], F32),
                     ("c_maskb", [L, L], F32), ("c_trilS", [L, L], BF16),
                     ("c_triuI", [L, L], BF16), ("c_ones_row", [1, L], BF16),
                     ("c_sel8", [8, 512], BF16), ("c_oneh", [8, 1024], BF16)]:
        p.cst[n] = sb("s" + n, sh, dt)
    p.pb = [nc.alloc_psum_tensor(f"pb{i}", [128, 512], F32) for i in range(7)]
    p.pbt = nc.alloc_psum_tensor("pbt", [128, 1024], BF16)

    from contextlib import ExitStack
    ctx = ExitStack()
    p.dW = Ch(ctx, nc, "dW")
    p.dX = Ch(ctx, nc, "dX")
    p.dS = Ch(ctx, nc, "dS")
    p.cT = Ch(ctx, nc, "cT")
    p.cV = Ch(ctx, nc, "cV")
    p.cS = Ch(ctx, nc, "cS")
    blk = ctx.enter_context(nc.Block())

    @blk.sync
    def _(sync):
        _emit(p)

    ctx.close()
    return nc


def _emit(p):
    _load_consts(p)
    _glue(p, first=True)
    if _DEV_STOP == "ln0":
        _final(p); return
    _l0_attn(p)
    _glue(p)
    if _DEV_STOP == "l0a":
        _final(p); return
    _l0_ffn(p)
    _glue(p)
    if _DEV_STOP == "l0f":
        _final(p); return
    _l1_attn(p)
    _glue(p, inner_ln=True)
    if _DEV_STOP == "l1a":
        _final(p); return
    _l1_ffn(p)
    _glue(p, last=True)
    _final(p)


def _final(p):
    nc = p.nc
    if _DEV_DBG is not None and _DEV_DBG in p.scr:
        s = p.scr[_DEV_DBG]
        rows, cols = s.tensor.shape if hasattr(s, 'tensor') else (None, None)
        sh = list(p.scr[_DEV_DBG].shape)
        stage = p.SM[0:128, 0:4096]
        for r0 in range(0, sh[0], 128):
            p.cV.wait(nc.sync)
            p.dX.incd(nc.sync.dma_start(stage[:, 0:sh[1]], p.scr[_DEV_DBG][r0:r0 + 128, :]))
            p.dX.wait(nc.vector)
            p.cV.inc(nc.vector.tensor_copy(p.AF[:, 0:sh[1]], stage[:, 0:sh[1]]))
            p.cV.wait(nc.gpsimd)
            p.dW.incd(nc.gpsimd.dma_start(p.dbg[r0:r0 + 128, 0:sh[1]], p.AF[:, 0:sh[1]]))
            p.dW.wait(nc.vector)
    if _DEV_STOP is not None:
        src = p.scr["xres"]
        for tt in range(NT):
            sl = slice(tt * 128, (tt + 1) * 128)
            b = p.xtm[tt % 2]
            p.dW.wait(nc.sync)
            p.dX.incd(nc.sync.dma_start(b[:], src[sl, :]))
            p.dX.wait(nc.gpsimd)
            p.dW.incd(nc.gpsimd.dma_start(p.out[sl, :], b[:]))
    p.dW.wait(nc.gpsimd)
    p.dX.wait(nc.gpsimd)
    p.dS.wait(nc.gpsimd)
    p.cV.wait(nc.gpsimd)
    p.cS.wait(nc.gpsimd)
    p.cT.wait(nc.gpsimd)


def _load_consts(p):
    nc = p.nc
    for n in p.cst:
        p.dX.incd(nc.sync.dma_start(p.cst[n][:], p.C[n]))
    s_lg = p.SM[0:1, 0:D]
    s_lb = p.SM[0:1, D:2 * D]
    s_gr = p.SM[0:1, 2 * D:2 * D + 256]
    p.dW.incd(nc.gpsimd.dma_start(s_lg, p.W["L1_attn_ln_g"][None, :]))
    p.dW.incd(nc.gpsimd.dma_start(s_lb, p.W["L1_attn_ln_b"][None, :]))
    p.dW.incd(nc.gpsimd.dma_start(s_gr, p.W["L1_attn_g_rms"][None, :]))
    p.dX.wait(nc.tensor)
    p.dW.wait(nc.tensor)
    orow = p.cst["c_ones_row"]
    for j, src in enumerate([s_lg, s_lb]):
        for c in range(2):
            p.cT.inc(nc.tensor.matmul(p.pb[2 * j + c][:, :], orow[0:1, :],
                                      src[0:1, c * 512:(c + 1) * 512], start=True, stop=True))
    p.cT.inc(nc.tensor.matmul(p.pb[4][:, 0:256], orow[0:1, :], s_gr[0:1, :], start=True, stop=True))
    p.cT.wait(nc.vector)
    for j, dst in enumerate([p.lng, p.lnb]):
        for c in range(2):
            p.cV.inc(nc.vector.tensor_copy(dst[:, c * 512:(c + 1) * 512], p.pb[2 * j + c][:, :]))
    for rep in range(4):
        p.cV.inc(nc.vector.tensor_copy(p.grms[:, rep * 256:(rep + 1) * 256], p.pb[4][:, 0:256]))
    p.cV.wait(nc.tensor)


def _ln_tile(p, src, xl_out, eps=1e-5, sub_mu=True):
    nc = p.nc
    vec, sc = nc.vector, nc.scalar
    N = src.shape[-1]
    scrap = p.AF[0:128, 2 * D:3 * D]
    c = p.sc1
    p.cV.inc(vec.reduce_sum(c[:, 0:1], src, mybir.AxisListType.X))
    p.cV.wait(sc)
    p.cS.inc(sc.activation(scrap[:, 0:N], src, AF.Square, accum_out=c[:, 1:2]))
    p.cS.wait(vec)
    p.cV.inc(vec.tensor_scalar(c[:, 2:3], c[:, 0:1], 1.0 / N, None, ALU.mult))
    p.cV.wait(sc)
    p.cS.inc(sc.activation(c[:, 3:4], c[:, 2:3], AF.Square))
    p.cS.wait(vec)
    vec.scalar_tensor_tensor(c[:, 4:5], c[:, 1:2], 1.0 / N, c[:, 3:4],
                             ALU.mult, ALU.subtract)
    vec.drain()
    vec.tensor_scalar(c[:, 5:6], c[:, 4:5], eps, None, ALU.add)
    vec.drain()
    p.cV.inc(vec.reciprocal(c[:, 6:7], c[:, 5:6]))
    p.cV.wait(sc)
    p.cS.inc(sc.activation(c[:, 7:8], c[:, 6:7], AF.Sqrt))
    p.cS.wait(vec)
    vec.tensor_scalar(xl_out, src, c[:, 2:3], c[:, 7:8], ALU.subtract, ALU.mult)


def _glue(p, first=False, last=False, inner_ln=False):
    nc = p.nc
    vec, sc, gp = nc.vector, nc.scalar, nc.gpsimd
    xin = p.x0 if first else p.scr["xres"]
    oc = p.scr["oc"]
    xl2 = [p.SM[0:128, 3 * D:4 * D], p.SM[0:128, 4 * D:5 * D]]
    yb = p.AF[0:128, 0:D]
    zt = p.AF[0:128, D:2 * D]
    p.dW.wait(nc.scalar)
    p.dW.wait(nc.sync)
    vneed = [p.cV.n, p.cV.n]
    wneed = [p.dW.n, p.dW.n]
    for tt in range(NT):
        sl = slice(tt * 128, (tt + 1) * 128)
        b = tt % 2
        xb, ob, xl = p.xtm[b], p.otm[b], xl2[b]
        nc.sync.wait_ge(p.cV.s, vneed[b])
        nc.sync.wait_ge(p.dW.s, wneed[b])
        p.dX.incd(nc.sync.dma_start(xb[:], xin[sl, :]))
        if not first:
            nc.scalar.wait_ge(p.cV.s, vneed[b])
            nc.scalar.wait_ge(p.dW.s, wneed[b])
            p.dS.incd(nc.scalar.dma_start(ob[:], oc[:, sl], transpose=True))
        if inner_ln:
            nc.scalar.wait_ge(p.cV.s, vneed[b])
            nc.scalar.wait_ge(p.dW.s, wneed[b])
            p.dS.incd(nc.scalar.dma_start(xl, p.scr["xln"][sl, :]))
        p.dX.wait(vec)
        if not first:
            p.dS.wait(vec)
            if inner_ln:
                vec.tensor_tensor(yb, ob[:], xl, ALU.add)
                _ln_tile(p, yb, zt)
                vec.tensor_tensor(zt, zt, p.lng[:], ALU.mult)
                vec.tensor_tensor(zt, zt, p.lnb[:], ALU.add)
                vec.tensor_tensor(xb[:], xb[:], zt, ALU.add)
            else:
                vec.tensor_tensor(xb[:], xb[:], ob[:], ALU.add)
        if not last:
            _ln_tile(p, xb[:], xl)
        p.cV.inc(vec.engine_nop())
        vneed[b] = p.cV.n
        p.cV.wait(gp)
        if not last:
            p.dW.incd(gp.dma_start(p.scr["xln"][sl, :], xl))
        dst = p.out if last else p.scr["xres"]
        p.dW.incd(gp.dma_start(dst[sl, :], xb[:]))
        wneed[b] = p.dW.n
    p.dW.wait(nc.sync)
    p.dW.wait(nc.scalar)
    p.dW.wait(nc.vector)
    p.dW.wait(nc.tensor)


def _load_xlf(p, half, src=None, nrow=D):
    nc = p.nc
    src = src if src is not None else p.scr["xln"]
    t0 = half * 2048
    nc.sync.wait_ge(p.cT.s, p.cT.n)
    for i in range(nrow // 128):
        p.dX.incd(nc.sync.dma_start(p.AA[:, i * 2048:(i + 1) * 2048],
                                    src[t0:t0 + 2048, i * 128:(i + 1) * 128], transpose=True))


def _l0_attn(p):
    nc = p.nc
    pb, pbt = p.pb, p.pbt
    vec, sc, te, gp, sy = nc.vector, nc.scalar, nc.tensor, nc.gpsimd, nc.sync
    dW, dX, cT, cV, cS = p.dW, p.dX, p.cT, p.cV, p.cS
    SM, sm32, AFX = p.SM, p.sm32, p.AF

    db = sm32[0:8, 0:1]
    al = sm32[0:8, 1:2]
    dpc = sm32[0:8, 2:3]
    expa = sm32[0:8, 3:4]
    negA = sm32[0:8, 4:5]
    ones8 = sm32[0:8, 8:136]
    dpcols = sm32[0:128, 136:140]
    Wc = sm32[0:8, 144:272]
    WcT = sm32[0:128, 272:280]
    decT = sm32[0:128, 280:288]
    Dmf = sm32[0:128, 288:416]
    dAL = sm32[0:64, 416:424]
    expw_f = sm32[0:8, 424:552]
    delta = AFX[0:8, 0:2048]
    stF = AFX[0:64, 2048:2560]
    wlo_f = sm32[0:8, 680:808]
    dec_f = sm32[0:8, 552:680]
    w_chunk = sm32[0:8, 808:936]
    xdst = [SM[0:128, j * 512:(j + 1) * 512] for j in range(4)]
    deltab_c = SM[0:8, 2048:2560]
    xdtm = SM[0:128, 0:512]
    CBT = SM[0:128, 512:640]
    Btm = SM[0:128, 640:704]
    GT = SM[0:128, 704:832]
    UT = SM[0:128, 832:960]
    Xdec = SM[0:128, 960:1024]
    ytile = [SM[0:128, 1024 + 128 * j:1152 + 128 * j] for j in range(2)]
    Csc = SM[0:64, 1280:1408]
    whi = SM[0:8, 1408:1536]
    wlo = SM[0:8, 1536:1664]
    expw_b = SM[0:8, 1664:1792]
    stB = p.AC[0:64, 5184:5696]
    dpb = p.AC[0:8, 5696:5697]
    oneh = p.cst["c_oneh"]

    dW.incd(gp.dma_start(db, p.W["L0_attn_dt_bias"][:, None]))
    dW.incd(gp.dma_start(al, p.W["L0_attn_A_log"][:, None]))
    dW.incd(gp.dma_start(dpc, p.W["L0_attn_D_param"][:, None]))
    WinT = [p.AC[0:128, i * 648:(i + 1) * 648] for i in range(8)]
    for i in range(8):
        dW.incd(gp.dma_start(WinT[i], p.W["L0_attn_Win"][i * 128:(i + 1) * 128, :]))

    cV.inc(vec.memset(stF, 0.0))
    cV.inc(vec.memset(stB, 0.0))
    cV.inc(vec.memset(ones8, 1.0))
    dW.wait(sc)
    cS.inc(sc.activation(expa, al, AF.Exp))
    cS.wait(vec)
    cV.inc(vec.tensor_scalar(negA, expa, -1.0, None, ALU.mult))
    cV.inc(vec.tensor_copy(dpb, dpc))
    cV.wait(te)
    dX.wait(te)
    for j in range(4):
        cT.inc(te.matmul(pb[6][:, j:j + 1], p.cst["c_sel8"][:, j * 128:(j + 1) * 128],
                         dpb, start=True, stop=True))
    cT.wait(vec)
    cV.inc(vec.tensor_copy(dpcols, pb[6][:, 0:4]))
    cV.wait(te)

    xd_store = [dW.n] * 4
    yst = [dW.n, dW.n]
    for half in range(2):
        _load_xlf(p, half)
        XLF = [p.AA[0:128, i * 2048:(i + 1) * 2048] for i in range(8)]
        Bt = p.AB[0:64, 0:2048]
        Ct = p.AB[0:64, 2048:4096]
        Xa = [p.AB[0:128, (2 + j) * 2048:(3 + j) * 2048] for j in range(4)]

        for cc in range(4):
            csl = slice(cc * 512, (cc + 1) * 512)
            g0 = half * 2048 + cc * 512
            tsl = slice(g0, g0 + 512)
            dX.wait(te)
            dW.wait(te)
            cV.wait(te)
            cS.wait(te)
            for kt in range(8):
                cT.inc(te.matmul(pb[0][0:64, :], WinT[kt][:, 0:64], XLF[kt][:, csl],
                                 start=(kt == 0), stop=(kt == 7)))
            for kt in range(8):
                cT.inc(te.matmul(pb[6][0:64, :], WinT[kt][:, 64:128], XLF[kt][:, csl],
                                 start=(kt == 0), stop=(kt == 7)))
            for kt in range(8):
                cT.inc(te.matmul(pb[1][0:8, :], WinT[kt][:, 128:136], XLF[kt][:, csl],
                                 start=(kt == 0), stop=(kt == 7)))
            for j in range(4):
                for kt in range(8):
                    cT.inc(te.matmul(pb[2 + j][:, :], WinT[kt][:, 136 + 128 * j:264 + 128 * j],
                                     XLF[kt][:, csl], start=(kt == 0), stop=(kt == 7)))
            cT.wait(vec)
            cT.wait(sc)
            cV.inc(vec.tensor_copy(Bt[:, csl], pb[0][0:64, :]))
            cV.inc(vec.tensor_copy(Ct[:, csl], pb[6][0:64, :]))
            cS.inc(sc.activation(delta[:, csl], pb[1][0:8, :], AF.Exp, bias=db))
            cS.inc(sc.activation(delta[:, csl], delta[:, csl], AF.Ln, bias=1.0))
            cS.wait(vec)
            cV.inc(vec.tensor_copy(deltab_c, delta[:, csl]))
            for j in range(4):
                cV.inc(vec.tensor_copy(Xa[j][:, csl], pb[2 + j][:, :]))
            cV.wait(gp)
            for j in range(4):
                dW.incd(gp.dma_start(p.scr["s_xx"][j * 128:(j + 1) * 128, tsl], Xa[j][:, csl]))
            cV.wait(te)
            for j in range(4):
                cT.inc(te.matmul(pb[6][:, :], p.cst["c_sel8"][:, j * 128:(j + 1) * 128],
                                 deltab_c, start=True, stop=True))
                cT.wait(vec)
                vec.wait_ge(dW.s, xd_store[j])
                cV.inc(vec.tensor_tensor(xdst[j], Xa[j][:, csl], pb[6][:, :], ALU.mult))
                cV.wait(te)
                cV.wait(gp)
                dW.incd(gp.dma_start(p.scr["s_xt"][j * 128:(j + 1) * 128, tsl], xdst[j]))
                xd_store[j] = dW.n
        sy.wait_ge(dW.s, dW.n)
        vec.wait_ge(dW.s, dW.n)
        sc.wait_ge(dW.s, dW.n)
        for scl in range(16):
            t0 = (half * 16 + scl) * 128
            ch_sl = slice(scl * 128, (scl + 1) * 128)
            cS.wait(vec)
            cV.inc(vec.tensor_scalar(w_chunk, delta[:, scl * 128:(scl + 1) * 128],
                                     negA, None, ALU.mult))
            cV.inc(vec.tensor_tensor_scan(Wc, ones8, w_chunk, 0.0, ALU.mult, ALU.add))
            cV.inc(vec.tensor_copy(whi, Wc))
            cV.inc(vec.tensor_tensor(wlo_f, Wc, whi, ALU.subtract))
            cV.inc(vec.tensor_copy(wlo, wlo_f))
            cV.inc(vec.tensor_scalar(dec_f, Wc, Wc[:, 127:128], None, ALU.subtract))
            cV.wait(sc)
            cS.inc(sc.activation(expw_f, Wc, AF.Exp))
            cS.inc(sc.activation(dec_f, dec_f, AF.Exp, scale=-1.0))
            cS.wait(vec)
            cV.inc(vec.tensor_copy(expw_b, expw_f))
            cV.wait(te)
            cS.wait(te)
            cT.inc(te.transpose(pb[5][:, 0:8], Wc, p.cst["c_identf"][0:8, 0:8]))
            cT.inc(te.transpose(pb[5][:, 8:16], dec_f, p.cst["c_identf"][0:8, 0:8]))
            cT.inc(te.matmul(pb[0][:, 0:128], Bt[:, ch_sl], Ct[:, ch_sl],
                             start=True, stop=True))
            cT.inc(te.transpose(pbt[:, 0:64], Bt[:, ch_sl], p.cst["c_ident"][0:64, 0:64]))
            cT.wait(vec)
            cV.inc(vec.tensor_copy(WcT, pb[5][:, 0:8]))
            cV.inc(vec.tensor_copy(decT, pb[5][:, 8:16]))
            cV.inc(vec.tensor_copy(CBT, pb[0][:, 0:128]))
            cV.inc(vec.tensor_copy(Btm, pbt[:, 0:64]))
            sy.wait_ge(cV.s, cV.n)
            sy.wait_ge(cT.s, cT.n)
            dX.incd(sy.dma_start(xdtm, p.scr["s_xt"][0:512, t0:t0 + 128], transpose=True))
            cV.wait(te)
            dX.wait(te)
            dX.wait(vec)
            tprev = cT.n
            for h in range(8):
                pw = pb[1 + (h % 2)]
                pe = pb[3 + (h % 2)]
                py = pb[5 + ((h // 2) % 2)]
                cV.wait(te)
                cS.wait(te)
                lh = oneh[:, h * 128:(h + 1) * 128]
                cT.inc(te.matmul(pw[:, 0:128], lh, whi, start=True, stop=False))
                cT.inc(te.matmul(pw[:, 0:128], lh, wlo, start=False, stop=True))
                cT.inc(te.matmul(pe[0:64, 0:128], lh[:, 0:64], expw_b, start=True, stop=True))
                cT.wait(vec)
                cV.inc(vec.tensor_scalar(Dmf, pw[:, 0:128], WcT[:, h:h + 1],
                                         None, ALU.subtract))
                cV.inc(vec.tensor_tensor(Dmf, Dmf, p.cst["c_maskb"][:, :], ALU.add))
                cV.wait(sc)
                cS.inc(sc.activation(GT, Dmf, AF.Exp))
                vec.wait_ge(cT.s, tprev)
                cV.inc(vec.tensor_tensor(Csc, Ct[:, ch_sl], pe[0:64, 0:128], ALU.mult))
                cV.inc(vec.tensor_copy(dAL[:, h:h + 1], pe[0:64, 127:128]))
                cS.wait(vec)
                cV.inc(vec.tensor_tensor(UT, GT, CBT, ALU.mult))
                if _DEV_DBG == "probe6" and half == 0 and scl == 12 and h == 6:
                    cV.wait(gp)
                    dW.incd(gp.dma_start(p.dbg[2712:2776, 0:512], stB))
                    dW.incd(gp.dma_start(p.dbg[2776:2840, 0:128], Csc))
                    dW.wait(vec)
                if False and _DEV_DBG == "probe6" and half == 0 and scl == 12 and h == 6:
                    cV.wait(gp)
                    cS.wait(gp)
                    dW.incd(gp.dma_start(p.dbg[2048:2176, 0:128], Dmf))
                    dW.incd(gp.dma_start(p.dbg[2176:2304, 0:128], GT))
                    dW.incd(gp.dma_start(p.dbg[2304:2432, 0:128], UT))
                    dW.incd(gp.dma_start(p.dbg[2432:2560, 0:128], CBT))
                    dW.incd(gp.dma_start(p.dbg[2560:2568, 0:128], Wc))
                    dW.incd(gp.dma_start(p.dbg[2568:2696, 0:512], xdtm))
                    dW.incd(gp.dma_start(p.dbg[2696:2704, 0:128], whi))
                    dW.incd(gp.dma_start(p.dbg[2704:2712, 0:128], wlo))
                    dW.wait(vec)
                    dW.wait(sc)
                cV.inc(vec.tensor_scalar(Xdec, xdtm[:, 64 * h:64 * (h + 1)],
                                         decT[:, h:h + 1], None, ALU.mult))
                cV.wait(te)
                ro = 64 * (h % 2)
                cT.inc(te.matmul(py[ro:ro + 64, 0:128], xdtm[:, 64 * h:64 * (h + 1)], UT,
                                 start=True, stop=False))
                cT.inc(te.matmul(py[ro:ro + 64, 0:128], stB[:, 64 * h:64 * (h + 1)], Csc,
                                 start=False, stop=True))
                cT.inc(te.matmul(pb[0][0:64, 128 + 64 * (h % 4):192 + 64 * (h % 4)],
                                 Btm, Xdec, start=True, stop=True))
                tprev = cT.n
                if h % 2 == 1:
                    cT.wait(sc)
                    yt = ytile[(h // 2) % 2]
                    sc.wait_ge(dW.s, yst[(h // 2) % 2])
                    cS.inc(sc.copy(yt, py[:, 0:128]))
                    cS.wait(gp)
                    dW.incd(gp.dma_start(
                        p.scr["s_y0"][128 * (h // 2):128 * (h // 2 + 1), t0:t0 + 128], yt))
                    yst[(h // 2) % 2] = dW.n
                    if _DEV_DBG == "probe6" and half == 0 and scl == 12 and h == 7:
                        dW.incd(gp.dma_start(p.dbg[2840:2968, 0:128], yt))
                        dW.incd(gp.dma_start(p.dbg[2968:3096, 0:64], Xdec))
                        dW.incd(gp.dma_start(p.dbg[3096:3224, 0:8], decT))
                        dW.wait(sc)
                if h % 4 == 3:
                    cT.wait(vec)
                    for hh in range(h - 3, h + 1):
                        cV.inc(vec.scalar_tensor_tensor(
                            stF[:, 64 * hh:64 * (hh + 1)], stF[:, 64 * hh:64 * (hh + 1)],
                            dAL[:, hh:hh + 1],
                            pb[0][0:64, 128 + 64 * (hh % 4):192 + 64 * (hh % 4)],
                            ALU.mult, ALU.add))
            cV.inc(vec.tensor_copy(stB, stF))
            cV.wait(te)

    WzSB = [p.AB[0:128, i * 2048:(i + 1) * 2048] for i in range(8)]
    WzgSB = [p.AB[0:128, 16384 + i * 512:16384 + (i + 1) * 512] for i in range(16)]
    WoutSB = [p.AB[0:128, 24576 + i * 1024:24576 + (i + 1) * 1024] for i in range(4)]
    cT.wait(gp)
    cV.wait(gp)
    for i in range(8):
        dW.incd(gp.dma_start(WzSB[i], p.W["L0_attn_Wz"][i * 128:(i + 1) * 128, :]))
    for i in range(16):
        dW.incd(gp.dma_start(WzgSB[i], p.W["L0_attn_Wzg"][i * 128:(i + 1) * 128, :]))
    for i in range(4):
        dW.incd(gp.dma_start(WoutSB[i], p.W["L0_attn_Wout"][i * 128:(i + 1) * 128, :]))
    zdw = dW.n
    zsil = [p.AC[0:128, i * 512:(i + 1) * 512] for i in range(16)]
    sg2 = [SM[0:128, 0:512], SM[0:128, 4096:4608]]
    ych = [SM[0:128, 512 + j * 512:1024 + j * 512] for j in range(4)]
    t1b = SM[0:128, 2560:3072]
    otile = [SM[0:128, 3072 + j * 512:3584 + j * 512] for j in range(2)]
    xch = [p.AC[0:128, 10240 + j * 512:10752 + j * 512] for j in range(4)]
    prodt = [p.AC[0:128, 8192 + j * 512:8704 + j * 512] for j in range(4)]
    for half in range(2):
        _load_xlf(p, half)
        XLF = [p.AA[0:128, i * 2048:(i + 1) * 2048] for i in range(8)]
        for cc in range(4):
            csl = slice(cc * 512, (cc + 1) * 512)
            g0 = half * 2048 + cc * 512
            tsl = slice(g0, g0 + 512)
            dX.wait(te)
            dW.wait(te)
            zv = [cV.n, cV.n]
            sgv = [cV.n, cV.n]
            for mz in range(16):
                pz = pb[mz % 2]
                sg = sg2[mz % 2]
                te.wait_ge(cV.s, zv[mz % 2])
                for kt in range(8):
                    cT.inc(te.matmul(pz[:, :], WzSB[kt][:, mz * 128:(mz + 1) * 128],
                                     XLF[kt][:, csl], start=(kt == 0), stop=(kt == 7)))
                cT.wait(sc)
                sc.wait_ge(cV.s, sgv[mz % 2])
                cS.inc(sc.activation(sg, pz[:, :], AF.Sigmoid))
                cS.wait(vec)
                cV.inc(vec.tensor_tensor(zsil[mz], sg, pz[:, :], ALU.mult))
                zv[mz % 2] = cV.n
                sgv[mz % 2] = cV.n
            sy.wait_ge(cV.s, cV.n)
            sy.wait_ge(cT.s, cT.n)
            sy.wait_ge(dW.s, zdw)
            for j in range(4):
                dX.incd(sy.dma_start(ych[j], p.scr["s_y0"][j * 128:(j + 1) * 128, tsl]))
                dX.incd(sy.dma_start(xch[j], p.scr["s_xx"][j * 128:(j + 1) * 128, tsl]))
            if _DEV_DBG == "probez" and half == 0 and cc == 0:
                cV.wait(gp)
                cS.wait(gp)
                for j in range(4):
                    dW.incd(gp.dma_start(p.dbg[2048 + 128 * j:2176 + 128 * j, 0:512], ych[j]))
                    dW.incd(gp.dma_start(p.dbg[2560 + 128 * j:2688 + 128 * j, 0:512], xch[j]))
                for mzd in range(4):
                    dW.incd(gp.dma_start(p.dbg[3072 + 128 * mzd:3200 + 128 * mzd, 0:512], zsil[mzd]))
                dW.wait(vec)
                dW.wait(sy)
            pv = [cV.n, cV.n]
            for mg in range(4):
                pzg = pb[2 + mg % 2]
                te.wait_ge(cV.s, pv[mg % 2])
                for kt2 in range(16):
                    cT.inc(te.matmul(pzg[:, :], WzgSB[kt2][:, mg * 128:(mg + 1) * 128],
                                     zsil[kt2], start=(kt2 == 0), stop=(kt2 == 15)))
                dX.wait(vec)
                cT.wait(vec)
                cV.inc(vec.tensor_scalar(t1b, xch[mg], dpcols[:, mg:mg + 1], None, ALU.mult))
                cV.inc(vec.tensor_tensor(t1b, t1b, ych[mg], ALU.add))
                cV.inc(vec.tensor_tensor(prodt[mg], t1b, pzg[:, :], ALU.mult))
                pv[mg % 2] = cV.n
            ov = [dW.n, dW.n]
            for mo in range(8):
                po = pb[4 + mo % 2]
                cS.wait(te)
                cV.wait(te)
                for kt3 in range(4):
                    cT.inc(te.matmul(po[:, :], WoutSB[kt3][:, mo * 128:(mo + 1) * 128],
                                     prodt[kt3], start=(kt3 == 0), stop=(kt3 == 3)))
                cT.wait(sc)
                sc.wait_ge(dW.s, ov[mo % 2])
                cS.inc(sc.copy(otile[mo % 2], po[:, :]))
                cS.wait(gp)
                dW.incd(gp.dma_start(p.scr["oc"][mo * 128:(mo + 1) * 128, tsl], otile[mo % 2]))
                ov[mo % 2] = dW.n
    dW.wait(sy)


def _bc16(handle, row_pitch, col0, nblk):
    return bass.AP(handle, col0, [[row_pitch, 128], [1, nblk], [0, 16]])

def _blk16(handle, row_pitch, col0, nblk):
    return bass.AP(handle, col0, [[row_pitch, 128], [16, nblk], [1, 16]])


def _quant_fp4(p, W, bi, f32_in=False, sgn_wait=None):
    nc = p.nc
    vec, sc = nc.vector, nc.scalar
    nb = W // 16
    AC, AFX, SM, sm32 = p.AC, p.AF, p.SM, p.sm32
    if f32_in:
        xt = AFX[0:128, bi * 1024:bi * 1024 + W]
        xv = _blk16(AFX, 4096, bi * 1024, nb)
        an2 = AFX[0:128, 2048 + bi * 1024:2048 + bi * 1024 + W]
    else:
        xt = AC[0:128, bi * 1408:bi * 1408 + W]
        xv = _blk16(AC, 13312, bi * 1408, nb)
        an2 = AFX[0:128, bi * 2048:bi * 2048 + W]
    sgn = AC[0:128, 8448 + bi * 1408:8448 + bi * 1408 + W]
    tb = AC[0:128, 5632 + bi * 1408:5632 + bi * 1408 + W]
    axq = AC[0:128, 2816 + bi * 1408:2816 + bi * 1408 + W]
    q2 = SM[0:128, bi * 1408:bi * 1408 + W]
    m = sm32[0:128, bi * 88:bi * 88 + nb]
    rcp = sm32[0:128, 256 + bi * 88:256 + bi * 88 + nb]
    rcp36 = sm32[0:128, 432 + bi * 88:432 + bi * 88 + nb]
    shb = SM[0:128, 2816 + bi * 96:2816 + bi * 96 + nb]
    if sgn_wait is not None:
        sc.wait_ge(p.cV.s, sgn_wait)
    p.cS.inc(sc.sign(sgn, xt))
    signm = p.cS.n
    vec.tensor_reduce(m, xv, mybir.AxisListType.X, ALU.max, apply_absolute_value=True)
    vec.drain()
    vec.tensor_scalar(m, m, 1e-12, None, ALU.max)
    vec.drain()
    vec.reciprocal(rcp, m)
    vec.tensor_scalar(shb, m, 1.0 / 12.0, None, ALU.mult)
    vec.drain()
    vec.tensor_scalar(rcp, rcp, 6.0, None, ALU.mult)
    vec.drain()
    vec.tensor_tensor(rcp36, rcp, rcp, ALU.mult)
    vec.tensor_tensor(an2, xt, xt, ALU.mult)
    vec.tensor_tensor(an2, an2, _bc16(sm32, 1024, 432 + bi * 88, nb), ALU.mult)
    vec.tensor_scalar(q2, an2, 0.0625, 1.0, ALU.is_gt, ALU.mult)
    for th2, wgt in [(0.5625, 1.0), (1.5625, 1.0), (3.0625, 1.0),
                     (6.25, 2.0), (12.25, 2.0), (25.0, 4.0)]:
        vec.tensor_scalar(tb, an2, th2, wgt, ALU.is_gt, ALU.mult)
        vec.tensor_tensor(q2, q2, tb, ALU.add)
    anm = p.cV.n + 1
    p.cV.inc(vec.engine_nop())
    vec.wait_ge(p.cS.s, signm)
    vec.tensor_tensor(q2, q2, sgn, ALU.mult)
    p.cV.inc(vec.tensor_tensor(axq, q2, _bc16(SM, 5120, 2816 + bi * 96, nb), ALU.mult))
    return anm


def _l0_ffn(p):
    nc = p.nc
    pb = p.pb
    vec, sc, te, gp, sy = nc.vector, nc.scalar, nc.tensor, nc.gpsimd, nc.sync
    dW, dX, dS, cT, cV, cS = p.dW, p.dX, p.dS, p.cT, p.cV, p.cS
    SM, AC, AFX = p.SM, p.AC, p.AF

    jobs = []
    for tt in range(NT):
        jobs.append((p.scr["xln"][tt * 128:(tt + 1) * 128, :], 1024, "s_xq",
                     (tt * 128, 0), False))
    for wn, sn in [("L0_ffn_w_gate", "s_wqg"), ("L0_ffn_w_up", "s_wqu")]:
        for r in range(22):
            jobs.append((p.W[wn][r * 128:(r + 1) * 128, :], 1024, sn, (r * 128, 0), True))
    for r in range(8):
        for c0, w in [(0, 1024), (1024, 1024), (2048, 768)]:
            jobs.append((p.W["L0_ffn_w_down"][r * 128:(r + 1) * 128, c0:c0 + w], w,
                         "s_wqd", (r * 128, c0), True))
    st2 = [dW.n, dW.n]
    an2m = [cV.n, cV.n]
    sg2m = [cS.n, cS.n]
    ld2 = [dX.n, dX.n]
    vq2 = [cV.n, cV.n]
    prev_path = None
    for idx, (srcap, W, dst, (r0, c0), isw) in enumerate(jobs):
        bi = idx % 2
        if prev_path is not None and isw != prev_path:
            sy.wait_ge(cV.s, cV.n)
            sy.wait_ge(cS.s, cS.n)
        prev_path = isw
        xt = AFX[0:128, bi * 1024:bi * 1024 + W] if isw else AC[0:128, bi * 1408:bi * 1408 + W]
        sy.wait_ge(cV.s, an2m[bi])
        sy.wait_ge(cS.s, sg2m[bi])
        dX.incd(sy.dma_start(xt, srcap))
        ld2[bi] = dX.n
        sc.wait_ge(dX.s, ld2[bi])
        vec.wait_ge(dX.s, ld2[bi])
        vec.wait_ge(dW.s, st2[bi])
        an2m[bi] = _quant_fp4(p, W, bi, f32_in=isw)
        sg2m[bi] = cS.n
        vq2[bi] = cV.n
        cV.wait(gp)
        dW.incd(gp.dma_start(p.scr[dst][r0:r0 + 128, c0:c0 + W],
                             AC[0:128, 2816 + bi * 1408:2816 + bi * 1408 + W]))
        st2[bi] = dW.n

    sgv = [cV.n, cV.n]
    sgb = [AC[0:128, 11264:11776], AC[0:128, 11776:12288]]
    for nh in range(2):
        WG = [p.AB[0:128, kt * 1408:(kt + 1) * 1408] for kt in range(8)]
        WU = [p.AB[0:128, 11264 + kt * 1408:11264 + (kt + 1) * 1408] for kt in range(8)]
        sy.wait_ge(cT.s, cT.n)
        sy.wait_ge(dW.s, dW.n)
        for kt in range(8):
            dX.incd(sy.dma_start(WG[kt], p.scr["s_wqg"][nh * 1408:(nh + 1) * 1408,
                                                        kt * 128:(kt + 1) * 128], transpose=True))
            dX.incd(sy.dma_start(WU[kt], p.scr["s_wqu"][nh * 1408:(nh + 1) * 1408,
                                                        kt * 128:(kt + 1) * 128], transpose=True))
        for half in range(2):
            sy.wait_ge(cT.s, cT.n)
            XQ = [p.AA[0:128, kt * 2048:(kt + 1) * 2048] for kt in range(8)]
            for kt in range(8):
                dX.incd(sy.dma_start(XQ[kt], p.scr["s_xq"][half * 2048:(half + 1) * 2048,
                                                           kt * 128:(kt + 1) * 128], transpose=True))
            dX.wait(te)
            for tl in range(16):
                tt = half * 16 + tl
                bi = tl % 2
                h_row = AC[0:128, bi * 1408:bi * 1408 + 1408]
                hm = 0
                for nci, (nc0, nw) in enumerate([(0, 512), (512, 512), (1024, 384)]):
                    pg = pb[4 + nci % 2]
                    pu = pb[2 + nci % 2]
                    cV.wait(te)
                    cS.wait(te)
                    for kt in range(8):
                        cT.inc(te.matmul(pg[:, 0:nw], XQ[kt][:, tl * 128:(tl + 1) * 128],
                                         WG[kt][:, nc0:nc0 + nw], start=(kt == 0), stop=(kt == 7)))
                    for kt in range(8):
                        cT.inc(te.matmul(pu[:, 0:nw], XQ[kt][:, tl * 128:(tl + 1) * 128],
                                         WU[kt][:, nc0:nc0 + nw], start=(kt == 0), stop=(kt == 7)))
                    cT.wait(sc)
                    sg = sgb[nci % 2]
                    sc.wait_ge(cV.s, sgv[nci % 2])
                    cS.inc(sc.activation(sg[:, 0:nw], pg[:, 0:nw], AF.Sigmoid))
                    cS.wait(vec)
                    if nci == 0:
                        vec.wait_ge(cV.s if False else dW.s, st2[bi])
                        vec.wait_ge(cV.s, an2m[bi]) if False else None
                    vec.tensor_tensor(sg[:, 0:nw], sg[:, 0:nw], pg[:, 0:nw], ALU.mult)
                    p.cV.inc(vec.tensor_tensor(h_row[:, nc0:nc0 + nw], sg[:, 0:nw],
                                               pu[:, 0:nw], ALU.mult))
                    sgv[nci % 2] = cV.n
                hm = cV.n
                an2m[bi] = _quant_fp4(p, 1408, bi, f32_in=False, sgn_wait=hm)
                sg2m[bi] = cS.n
                cV.wait(gp)
                dW.incd(gp.dma_start(p.scr["s_hq"][tt * 128:(tt + 1) * 128,
                                                   nh * 1408:(nh + 1) * 1408],
                                     AC[0:128, 2816 + bi * 1408:2816 + bi * 1408 + 1408]))
                st2[bi] = dW.n
    WD = [p.AB[0:128, kt * 1024:(kt + 1) * 1024] for kt in range(22)]
    sy.wait_ge(cT.s, cT.n)
    sy.wait_ge(dW.s, dW.n)
    for kt in range(22):
        dX.incd(sy.dma_start(WD[kt], p.scr["s_wqd"][:, kt * 128:(kt + 1) * 128], transpose=True))
    ov = [dW.n, dW.n]
    for qt in range(4):
        HQ = []
        sy.wait_ge(cT.s, cT.n)
        for kt in range(22):
            dst = (p.AA[0:128, kt * 1024:(kt + 1) * 1024] if kt < 16
                   else AC[0:128, (kt - 16) * 1024:(kt - 15) * 1024])
            HQ.append(dst)
            dX.incd(sy.dma_start(dst, p.scr["s_hq"][qt * 1024:(qt + 1) * 1024,
                                                    kt * 128:(kt + 1) * 128], transpose=True))
        dX.wait(te)
        for dt in range(8):
            for tch in range(2):
                po = pb[tch % 2]
                cS.wait(te)
                for kt in range(22):
                    cT.inc(te.matmul(po[:, :], WD[kt][:, dt * 128:(dt + 1) * 128],
                                     HQ[kt][:, tch * 512:(tch + 1) * 512],
                                     start=(kt == 0), stop=(kt == 21)))
                cT.wait(sc)
                otile = SM[0:128, 3072 + (tch % 2) * 512:3584 + (tch % 2) * 512]
                sc.wait_ge(dW.s, ov[tch % 2])
                cS.inc(sc.copy(otile, po[:, :]))
                cS.wait(gp)
                dW.incd(gp.dma_start(
                    p.scr["oc"][dt * 128:(dt + 1) * 128,
                                qt * 1024 + tch * 512:qt * 1024 + (tch + 1) * 512], otile))
                ov[tch % 2] = dW.n
    dW.wait(sy)


def _l1_attn(p):
    nc = p.nc
    pb, pbt = p.pb, p.pbt
    vec, sc, te, gp, sy = nc.vector, nc.scalar, nc.tensor, nc.gpsimd, nc.sync
    dW, dX, dS, cT, cV, cS = p.dW, p.dX, p.dS, p.cT, p.cV, p.cS
    SM, sm32, AC, AA, AB, AFX = p.SM, p.sm32, p.AC, p.AA, p.AB, p.AF

    WSB = [AC[0:128, kt * 1024:(kt + 1) * 1024] for kt in range(8)]
    csrc2 = [AC[0:128, 8192 + j * 515:8707 + j * 515] for j in range(2)]
    convT = lambda wi, mt: sm32[0:128, (wi * 8 + mt) * 4:(wi * 8 + mt) * 4 + 4]
    WbSB = [AC[0:128, 9318 + kt * 4:9322 + kt * 4] for kt in range(8)]
    halo = lambda wi, mt: AC[0:128, 9350 + (wi * 8 + mt) * 3:9353 + (wi * 8 + mt) * 3]
    onesc = AC[0:128, 9422:9423]
    acc = SM[0:128, 0:512]
    tmp = SM[0:128, 512:1024]
    sg2 = [SM[0:128, 1024 + j * 512:1536 + j * 512] for j in range(2)]
    stash2 = [SM[0:128, 2048 + j * 512:2560 + j * 512] for j in range(2)]
    q2b2 = [SM[0:128, 3072 + j * 512:3584 + j * 512] for j in range(2)]
    rsqb = SM[0:1, 4096:4608]
    nbetaT = sm32[0:128, 384:512]
    betaT = sm32[0:128, 256:384]
    cvst = SM[0:4, 0:3072]
    bfm = p.bfm
    wnames = ["L1_attn_Wq", "L1_attn_Wk", "L1_attn_Wv"]
    cnames = ["L1_attn_conv_q", "L1_attn_conv_k", "L1_attn_conv_v"]

    for wi in range(3):
        dW.incd(gp.dma_start(cvst[:, wi * 1024:(wi + 1) * 1024], p.W[cnames[wi]]))
    for kt in range(8):
        dW.incd(gp.dma_start(WbSB[kt], p.W["L1_attn_Wb"][kt * 128:(kt + 1) * 128, :]))
    cV.inc(vec.memset(AC[0:128, 9350:9422], 0.0))
    cV.inc(vec.memset(onesc, 1.0))
    dW.wait(te)
    cV.wait(te)
    cS.wait(te)
    for wi in range(3):
        for mt in range(8):
            j4 = (wi * 8 + mt) * 4
            cT.inc(te.transpose(pbt[:, j4:j4 + 4],
                                cvst[0:4, wi * 1024 + mt * 128:wi * 1024 + (mt + 1) * 128],
                                p.cst["c_ident"][0:4, 0:4]))
    cT.wait(vec)
    cV.inc(vec.tensor_copy(sm32[0:128, 0:96], pbt[:, 0:96]))
    cvev = cV.n

    qst = [dW.n] * 2
    sgv = [cV.n] * 2
    csv = [cV.n] * 2
    pzm = [cS.n] * 2
    q2m = [cT.n] * 2
    accm = cS.n
    p2m = cV.n
    p3m = cV.n
    bcm = cT.n
    p4m = cS.n
    for half in range(2):
        _load_xlf(p, half)
        XLF = [AA[0:128, i * 2048:(i + 1) * 2048] for i in range(8)]
        for wi in range(3):
            wmark = cT.n
            for kt in range(8):
                gp.wait_ge(cT.s, wmark)
                dW.incd(gp.dma_start(WSB[kt], p.W[wnames[wi]][kt * 128:(kt + 1) * 128, :]))
            for cc in range(4):
                csl = slice(cc * 512, (cc + 1) * 512)
                g0 = half * 2048 + cc * 512
                dX.wait(te)
                dW.wait(te)
                for h4 in range(4):
                    for mi in range(2):
                        mt = 2 * h4 + mi
                        cb = mt % 2
                        pz = pb[mt % 2]
                        cs = csrc2[cb]
                        te.wait_ge(cS.s, pzm[mt % 2])
                        for kt in range(8):
                            cT.inc(te.matmul(pz[:, :], WSB[kt][:, mt * 128:(mt + 1) * 128],
                                             XLF[kt][:, csl], start=(kt == 0), stop=(kt == 7)))
                        sc.wait_ge(cV.s, csv[cb])
                        cS.inc(sc.copy(cs[:, 0:3], halo(wi, mt)))
                        cT.wait(sc)
                        cS.inc(sc.copy(cs[:, 3:515], pz[:, :]))
                        pzm[mt % 2] = cS.n
                        cS.wait(vec)
                        wc = convT(wi, mt)
                        vec.tensor_scalar(acc, cs[:, 0:512], wc[:, 0:1], None, ALU.mult)
                        for j in (1, 2, 3):
                            vec.tensor_scalar(tmp, cs[:, j:j + 512], wc[:, j:j + 1],
                                              None, ALU.mult)
                            vec.tensor_tensor(acc, acc, tmp, ALU.add)
                        cV.inc(vec.tensor_copy(halo(wi, mt), cs[:, 512:515]))
                        csv[cb] = cV.n
                        sg = sg2[mi]
                        cV.wait(sc)
                        cS.inc(sc.activation(sg, acc, AF.Sigmoid))
                        accm = cS.n
                        cS.wait(vec)
                        st = stash2[mi]
                        vec.wait_ge(dW.s, qst[mi])
                        cV.inc(vec.tensor_tensor(st, sg, acc, ALU.mult))
                        sgv[mi] = cV.n
                        if wi < 2:
                            cV.wait(sc)
                            sc.wait_ge(cT.s, q2m[mi])
                            cS.inc(sc.activation(q2b2[mi], st, AF.Square))
                            cS.wait(te)
                            te.wait_ge(cV.s, p2m)
                            cT.inc(te.matmul(pb[2][0:1, 0:512], onesc, q2b2[mi],
                                             start=(mi == 0), stop=(mi == 1)))
                            q2m[mi] = cT.n
                        else:
                            cV.wait(gp)
                            dW.incd(gp.dma_start(
                                p.scr["s_vv"][mt * 128:(mt + 1) * 128, g0:g0 + 512], st))
                            qst[mi] = dW.n
                    if wi < 2:
                        cT.wait(vec)
                        cV.inc(vec.tensor_scalar(AFX[0:1, 0:512], pb[2][0:1, 0:512],
                                                 1e-6, None, ALU.add))
                        p2m = cV.n
                        vec.drain()
                        vec.reciprocal(AFX[0:1, 512:1024], AFX[0:1, 0:512])
                        cV.inc(vec.engine_nop())
                        cV.wait(sc)
                        sc.wait_ge(cT.s, bcm)
                        cS.inc(sc.activation(rsqb, AFX[0:1, 512:1024], AF.Sqrt))
                        cS.wait(te)
                        te.wait_ge(cV.s, p3m)
                        cT.inc(te.matmul(pb[3][:, :], p.cst["c_ones_row"][0:1, :], rsqb,
                                         start=True, stop=True))
                        bcm = cT.n
                        cT.wait(vec)
                        for mi in range(2):
                            mt = 2 * h4 + mi
                            st = stash2[mi]
                            if wi == 0:
                                cV.inc(vec.tensor_tensor(
                                    AB[0:128, mt * 4096 + g0:mt * 4096 + g0 + 512],
                                    st, pb[3][:, :], ALU.mult))
                            else:
                                cV.inc(vec.tensor_tensor(st, st, pb[3][:, :], ALU.mult))
                                cV.wait(gp)
                                dW.incd(gp.dma_start(
                                    p.scr["s_kk"][mt * 128:(mt + 1) * 128, g0:g0 + 512], st))
                                qst[mi] = dW.n
                        p3m = cV.n
        for cc in range(4):
            csl = slice(cc * 512, (cc + 1) * 512)
            g0 = half * 2048 + cc * 512
            te.wait_ge(cS.s, p4m)
            for kt in range(8):
                cT.inc(te.matmul(pb[4][0:4, :], WbSB[kt], XLF[kt][:, csl],
                                 start=(kt == 0), stop=(kt == 7)))
            cT.wait(sc)
            cS.inc(sc.activation(bfm[0:4, g0:g0 + 512], pb[4][0:4, :], AF.Sigmoid))
            p4m = cS.n
    cS.wait(te)
    cV.wait(te)
    for sco in range(32):
        cT.inc(te.transpose(pbt[:, 4 * sco:4 * sco + 4], bfm[0:4, sco * 128:(sco + 1) * 128],
                            p.cst["c_ident"][0:4, 0:4]))
    cT.wait(vec)
    cV.inc(vec.tensor_copy(betaT, pbt[:, 0:128]))
    vec.tensor_scalar(nbetaT, betaT, -1.0, None, ALU.mult)
    cV.inc(vec.engine_nop())

    S_f = lambda h, kt: AFX[0:128, (2 * h + kt) * 256:(2 * h + kt + 1) * 256]
    S_b = lambda h, kt: AC[0:128, (2 * h + kt) * 256:(2 * h + kt + 1) * 256]
    ktm2 = [AC[0:128, 2048 + j * 1024:3072 + j * 1024] for j in range(2)]
    vtm2 = [AC[0:128, 4096 + j * 1024:5120 + j * 1024] for j in range(2)]
    Dl = [SM[0:128, 0:256], SM[0:128, 256:512]]
    Cp = [(SM[0:128, 512:640], SM[0:128, 640:768]),
          (SM[0:128, 768:896], SM[0:128, 896:1024])]
    Cpair = [SM[0:128, 512:768], SM[0:128, 768:1024]]
    PTb = SM[0:128, 1024:1152]
    scrap = SM[0:128, 1152:1408]
    Rv = SM[0:128, 1408:1664]
    och = SM[0:128, 1664:2688]
    dW.wait(sy)
    dW.wait(te)
    cV.wait(te)
    cS.wait(te)
    cV.inc(vec.memset(AFX[0:128, 0:2048], 0.0))
    cV.inc(vec.memset(AC[0:128, 0:2048], 0.0))
    p0m = cV.n
    ptm = cV.n
    p4m2 = cV.n
    p5mS = cS.n
    p5mV = cV.n
    p6m = cV.n
    pbtm = cS.n
    p3evV = cV.n
    p1mV = cV.n
    p1mS = cS.n
    sbm = [cS.n] * 4
    ost = dW.n
    kvmark = [cT.n, cT.n]

    def twait(dme, dm):
        te.wait_ge(cV.s if dme == "V" else cS.s, dm)

    for sci in range(32):
        half = sci // 16
        if sci % 16 == 0:
            sy.wait_ge(cT.s, cT.n)
            for kt2 in range(8):
                dX.incd(sy.dma_start(AA[:, kt2 * 2048:(kt2 + 1) * 2048],
                                     p.scr["s_kk"][kt2 * 128:(kt2 + 1) * 128,
                                                   half * 2048:(half + 1) * 2048]))
        b = sci % 2
        sy.wait_ge(cT.s, kvmark[b])
        tsl = slice(sci * 128, (sci + 1) * 128)
        dX.incd(sy.dma_start(ktm2[b], p.scr["s_kk"][:, tsl], transpose=True))
        dX.incd(sy.dma_start(vtm2[b], p.scr["s_vv"][:, tsl], transpose=True))
        dX.wait(te)
        dX.wait(vec)
        lsl = (sci % 16) * 128
        for h in range(4):
            kf = [AA[0:128, (2 * h + kt) * 2048 + lsl:(2 * h + kt) * 2048 + lsl + 128]
                  for kt in range(2)]
            qf = [AB[0:128, (2 * h + kt) * 4096 + sci * 128:(2 * h + kt) * 4096 + sci * 128 + 128]
                  for kt in range(2)]
            bcol = betaT[:, 4 * sci + h:4 * sci + h + 1]
            nbcol = nbetaT[:, 4 * sci + h:4 * sci + h + 1]
            te.wait_ge(cV.s, max(p0m, ptm))
            for kt in range(2):
                cT.inc(te.matmul(pb[0][:, 0:128], kf[kt], kf[kt],
                                 start=(kt == 0), stop=(kt == 1)))
            for kt in range(2):
                cT.inc(te.matmul(pb[0][:, 128:256], kf[kt], qf[kt],
                                 start=(kt == 0), stop=(kt == 1)))
            te.wait_ge(cS.s, sbm[h])
            te.wait_ge(cV.s, p4m2)
            for kt in range(2):
                cT.inc(te.matmul(pb[4][:, 0:256], kf[kt], S_b(h, kt),
                                 start=(kt == 0), stop=(kt == 1)))
            cT.wait(vec)
            vec.wait_ge(cS.s, p1mS)
            C0, C0T = Cp[0]
            vec.tensor_scalar(C0, pb[0][:, 0:128], nbcol, None, ALU.mult)
            cV.inc(vec.tensor_tensor(C0, C0, p.cst["c_trilS"][:, :], ALU.mult))
            c1m = cV.n
            cV.inc(vec.tensor_tensor(PTb, pb[0][:, 128:256], p.cst["c_triuI"][:, :], ALU.mult))
            p0m = cV.n
            ptm = cV.n
            vec.tensor_scalar(Rv, vtm2[b][:, 256 * h:256 * h + 256], bcol, None, ALU.mult)
            cV.inc(vec.scalar_tensor_tensor(Dl[0], pb[4][:, 0:256], nbcol, Rv,
                                            ALU.mult, ALU.add))
            p4m2 = cV.n
            dme, dm = "V", cV.n
            te.wait_ge(cV.s, c1m)
            te.wait_ge(cS.s, pbtm)
            cT.inc(te.transpose(pbt[:, 0:128], C0, p.cst["c_ident"][:, :]))
            cT.wait(sc)
            cS.inc(sc.copy(C0T, pbt[:, 0:128]))
            pbtm = cS.n
            ctme, ctm = ["S", None], [cS.n, 0]
            for i in range(7):
                cur = i % 2
                nxt = 1 - cur
                Cc, CcT = Cp[cur]
                te.wait_ge(cV.s, p1mV)
                te.wait_ge(cS.s, p1mS)
                twait(ctme[cur], ctm[cur])
                twait(dme, dm)
                cT.inc(te.matmul(pb[1][:, 0:256], p.cst["c_ident"][:, :], Dl[cur],
                                 start=True, stop=False))
                cT.inc(te.matmul(pb[1][:, 0:256], CcT, Dl[cur], start=False, stop=True))
                am = cT.n
                if i < 6:
                    te.wait_ge(cV.s, p3evV)
                    cT.inc(te.matmul(pb[3][:, 0:128], CcT, Cc, start=True, stop=True))
                    cT.inc(te.matmul(pb[3][:, 128:256], Cc, CcT, start=True, stop=True))
                    sqm = cT.n
                if i % 2 == 0:
                    vec.wait_ge(cT.s, am)
                    cV.inc(vec.tensor_copy(Dl[nxt], pb[1][:, 0:256]))
                    p1mV = cV.n
                    dme, dm = "V", cV.n
                else:
                    sc.wait_ge(cT.s, am)
                    cS.inc(sc.copy(Dl[nxt], pb[1][:, 0:256]))
                    p1mS = cS.n
                    dme, dm = "S", cS.n
                if i < 6:
                    vec.wait_ge(cT.s, sqm)
                    cV.inc(vec.tensor_copy(Cpair[nxt], pb[3][:, 0:256]))
                    p3evV = cV.n
                    ctme[nxt], ctm[nxt] = "V", cV.n
            te.wait_ge(cS.s, p5mS)
            te.wait_ge(cV.s, p5mV)
            twait(dme, dm)
            cT.inc(te.matmul(pb[5][:, 0:256], PTb, Dl[1], start=True, stop=False))
            for kt in range(2):
                cT.inc(te.matmul(pb[5][:, 0:256], qf[kt], S_b(h, kt),
                                 start=False, stop=(kt == 1)))
            om = cT.n
            sc.wait_ge(cT.s, om)
            cS.inc(sc.activation(scrap, pb[5][:, 0:256], AF.Square,
                                 accum_out=sm32[0:128, 16 + h:17 + h]))
            p5mS = cS.n
            cS.wait(vec)
            vec.tensor_scalar(sm32[:, 20 + h:21 + h], sm32[:, 16 + h:17 + h],
                              1.0 / 256.0, 1e-5, ALU.mult, ALU.add)
            vec.drain()
            cV.inc(vec.reciprocal(sm32[:, 24 + h:25 + h], sm32[:, 20 + h:21 + h]))
            cV.wait(sc)
            cS.inc(sc.activation(sm32[:, 28 + h:29 + h], sm32[:, 24 + h:25 + h], AF.Sqrt))
            cS.wait(vec)
            if h == 0:
                vec.wait_ge(dW.s, ost)
            osl = och[:, 256 * h:256 * h + 256]
            vec.tensor_scalar(osl, pb[5][:, 0:256], sm32[:, 28 + h:29 + h], None, ALU.mult)
            cV.inc(vec.tensor_tensor(osl, osl, p.grms[:, 0:256], ALU.mult))
            p5mV = cV.n
            te.wait_ge(cV.s, p6m)
            twait(dme, dm)
            for kt in range(2):
                cT.inc(te.matmul(pb[6][:, kt * 256:(kt + 1) * 256],
                                 ktm2[b][:, 256 * h + kt * 128:256 * h + (kt + 1) * 128],
                                 Dl[1], start=True, stop=True))
            kvmark[b] = cT.n
            cT.wait(vec)
            for kt in range(2):
                vec.tensor_tensor(S_f(h, kt), S_f(h, kt), pb[6][:, kt * 256:(kt + 1) * 256],
                                  ALU.add)
            cV.inc(vec.engine_nop())
            p6m = cV.n
            cV.wait(sc)
            for kt in range(2):
                cS.inc(sc.copy(S_b(h, kt), S_f(h, kt)))
            sbm[h] = cS.n
        cV.wait(gp)
        dW.incd(gp.dma_start(p.scr["s_ot"][tsl, :], och))
        ost = dW.n

    WoSB = [AC[0:128, kt * 1024:(kt + 1) * 1024] for kt in range(8)]
    wmark2 = cT.n
    for kt in range(8):
        gp.wait_ge(cT.s, wmark2)
        gp.wait_ge(cS.s, cS.n)
        dW.incd(gp.dma_start(WoSB[kt], p.W["L1_attn_Wo"][kt * 128:(kt + 1) * 128, :]))
    ovm = [dW.n, dW.n]
    pzm2 = [cS.n] * 2
    for half in range(2):
        sy.wait_ge(cT.s, cT.n)
        sy.wait_ge(dW.s, dW.n)
        for i in range(8):
            dX.incd(sy.dma_start(AA[:, i * 2048:(i + 1) * 2048],
                                 p.scr["s_ot"][half * 2048:(half + 1) * 2048,
                                               i * 128:(i + 1) * 128], transpose=True))
        dX.wait(te)
        dW.wait(te)
        for mo in range(8):
            for cc in range(4):
                po = pb[cc % 2]
                te.wait_ge(cS.s, pzm2[cc % 2])
                for kt in range(8):
                    cT.inc(te.matmul(po[:, :], WoSB[kt][:, mo * 128:(mo + 1) * 128],
                                     AA[0:128, kt * 2048 + cc * 512:kt * 2048 + (cc + 1) * 512],
                                     start=(kt == 0), stop=(kt == 7)))
                cT.wait(sc)
                otile = SM[0:128, (cc % 2) * 512:512 + (cc % 2) * 512]
                sc.wait_ge(dW.s, ovm[cc % 2])
                cS.inc(sc.copy(otile, po[:, :]))
                pzm2[cc % 2] = cS.n
                cS.wait(gp)
                dW.incd(gp.dma_start(
                    p.scr["oc"][mo * 128:(mo + 1) * 128,
                                half * 2048 + cc * 512:half * 2048 + (cc + 1) * 512], otile))
                ovm[cc % 2] = dW.n
    dW.wait(sy)


def _l1_ffn(p):
    nc = p.nc
    pb = p.pb
    vec, sc, te, gp, sy = nc.vector, nc.scalar, nc.tensor, nc.gpsimd, nc.sync
    dW, dX, dS, cT, cV, cS = p.dW, p.dX, p.dS, p.cT, p.cV, p.cS
    SM, sm32, AC, AA, AB, AFX = p.SM, p.sm32, p.AC, p.AA, p.AB, p.AF
    NW = 4096 * 1024

    scrap = AC[0:128, 0:4096]
    lnwrow = SM[0:1, 0:4096]
    lnwbc = AC[0:128, 8192:12288]
    dW.incd(gp.dma_start(lnwrow, p.W["L1_ffn_ln_w"][None, :]))
    dW.wait(te)
    cV.wait(te)
    cS.wait(te)
    for cb2 in range(8):
        cT.inc(te.matmul(pb[3][:, :], p.cst["c_ones_row"][0:1, :],
                         lnwrow[0:1, cb2 * 512:(cb2 + 1) * 512], start=True, stop=True))
        cT.wait(vec)
        cV.inc(vec.tensor_copy(lnwbc[:, cb2 * 512:(cb2 + 1) * 512], pb[3][:, :]))
        cV.wait(te)
    fcb = [AFX[0:128, 0:1024], AFX[0:128, 1024:2048]]
    pjb = [AFX[0:128, 0:2048], AFX[0:128, 2048:4096]]
    ldm = [dX.n, dX.n]
    csm = [cS.n, cS.n]
    for i in range(32):
        b = i % 2
        sy.wait_ge(cS.s, csm[b])
        dX.incd(sy.dma_start(fcb[b], p.W["L1_ffn_W_fc"][i * 128:(i + 1) * 128, :]))
        ldm[b] = dX.n
        sc.wait_ge(dX.s, ldm[b])
        cS.inc(sc.activation(scrap[:, 0:1024], fcb[b], AF.Abs,
                             accum_out=sm32[0:128, i:i + 1]))
        csm[b] = cS.n
    csm = [cS.n, cS.n]
    for i in range(16):
        r, hh = i // 2, i % 2
        b = i % 2
        sy.wait_ge(cS.s, csm[b])
        dX.incd(sy.dma_start(pjb[b], p.W["L1_ffn_W_proj"][r * 128:(r + 1) * 128,
                                                          hh * 2048:(hh + 1) * 2048]))
        ldm[b] = dX.n
        sc.wait_ge(dX.s, ldm[b])
        cS.inc(sc.activation(scrap[:, 0:2048], pjb[b], AF.Abs,
                             accum_out=sm32[0:128, 32 + i:33 + i]))
        csm[b] = cS.n
    cS.wait(vec)
    vec.reduce_sum(sm32[0:128, 56:57], sm32[0:128, 0:32], mybir.AxisListType.X)
    vec.reduce_sum(sm32[0:128, 57:58], sm32[0:128, 32:48], mybir.AxisListType.X)
    vec.memset(sm32[0:128, 58:59], 1.0)
    cV.inc(vec.memset(sm32[0:1, 64:192], 1.0))
    cV.wait(te)
    cT.inc(te.matmul(pb[2][0:1, 0:2], sm32[0:128, 58:59], sm32[0:128, 56:58],
                     start=True, stop=True))
    cT.wait(vec)
    vec.tensor_scalar(sm32[0:1, 60:62], pb[2][0:1, 0:2], 1.0 / NW, 1e-6, ALU.mult, ALU.add)
    vec.drain()
    vec.reciprocal(sm32[0:1, 62:64], sm32[0:1, 60:62])
    vec.drain()
    cV.inc(vec.tensor_scalar(sm32[0:1, 62:64], sm32[0:1, 62:64], 2.0, None, ALU.mult))
    cV.wait(te)
    cT.inc(te.matmul(pb[2][:, 8:10], sm32[0:1, 64:192], sm32[0:1, 62:64],
                     start=True, stop=True))
    cT.wait(vec)
    cV.inc(vec.tensor_copy(sm32[0:128, 48:50], pb[2][:, 8:10]))
    rgfc = sm32[0:128, 48:49]
    rgpj = sm32[0:128, 49:50]
    stm = [dW.n, dW.n]
    vcm = [cV.n, cV.n]
    m1f = [SM[0:128, 0:1024], SM[0:128, 1024:2048]]
    m2f = SM[0:128, 2048:3072]
    wqf = [SM[0:128, 3072:4096], SM[0:128, 4096:5120]]
    for i in range(32):
        b = i % 2
        sy.wait_ge(cV.s, vcm[b])
        dX.incd(sy.dma_start(fcb[b], p.W["L1_ffn_W_fc"][i * 128:(i + 1) * 128, :]))
        ldm[b] = dX.n
        vec.wait_ge(dX.s, ldm[b])
        vec.wait_ge(dW.s, stm[b])
        vec.tensor_scalar(m1f[b], fcb[b], rgfc, 1.0, ALU.mult, ALU.is_gt)
        vec.tensor_scalar(m2f, fcb[b], rgfc, -1.0, ALU.mult, ALU.is_lt)
        cV.inc(vec.tensor_tensor(wqf[b], m1f[b], m2f, ALU.subtract))
        vcm[b] = cV.n
        cV.wait(gp)
        dW.incd(gp.dma_start(p.scr["s_wf"][i * 128:(i + 1) * 128, :], wqf[b]))
        stm[b] = dW.n
    m1p = [SM[0:128, 0:2048], SM[0:128, 2048:4096]]
    dW.wait(vec)
    vcm = [cV.n, cV.n]
    stm = [dW.n, dW.n]
    for i in range(16):
        r, hh = i // 2, i % 2
        b = i % 2
        sy.wait_ge(cV.s, vcm[b])
        dX.incd(sy.dma_start(pjb[b], p.W["L1_ffn_W_proj"][r * 128:(r + 1) * 128,
                                                          hh * 2048:(hh + 1) * 2048]))
        ldm[b] = dX.n
        vec.wait_ge(dX.s, ldm[b])
        vec.wait_ge(dW.s, stm[b])
        vec.tensor_scalar(m1p[b], pjb[b], rgpj, 1.0, ALU.mult, ALU.is_gt)
        vec.tensor_scalar(scrap[:, 0:2048], pjb[b], rgpj, -1.0, ALU.mult, ALU.is_lt)
        cV.inc(vec.tensor_tensor(m1p[b], m1p[b], scrap[:, 0:2048], ALU.subtract))
        vcm[b] = cV.n
        cV.wait(gp)
        dW.incd(gp.dma_start(p.scr["s_wp"][r * 128:(r + 1) * 128,
                                           hh * 2048:(hh + 1) * 2048], m1p[b]))
        stm[b] = dW.n

    sy.wait_ge(cT.s, cT.n)
    sy.wait_ge(dW.s, dW.n)
    WF = [AB[0:128, dt * 4096:(dt + 1) * 4096] for dt in range(8)]
    for dt in range(8):
        dX.incd(sy.dma_start(WF[dt], p.scr["s_wf"][:, dt * 128:(dt + 1) * 128],
                             transpose=True))
    vt = AC[0:128, 4096:8192]
    zt = SM[0:128, 0:4096]
    c = p.sc1
    pzm = [cS.n, cS.n]
    acm = cV.n
    zsm = dW.n
    for half in range(2):
        _load_xlf(p, half)
        XLF = [AA[0:128, i * 2048:(i + 1) * 2048] for i in range(8)]
        dX.wait(te)
        for tl in range(16):
            tt = half * 16 + tl
            for ch in range(8):
                pz = pb[ch % 2]
                te.wait_ge(cS.s, pzm[ch % 2])
                for kt in range(8):
                    cT.inc(te.matmul(pz[:, :], XLF[kt][:, tl * 128:(tl + 1) * 128],
                                     AB[0:128, kt * 4096 + ch * 512:kt * 4096 + (ch + 1) * 512],
                                     start=(kt == 0), stop=(kt == 7)))
                cT.wait(sc)
                if ch == 0:
                    sc.wait_ge(cV.s, acm)
                cS.inc(sc.activation(vt[:, ch * 512:(ch + 1) * 512], pz[:, :], AF.Gelu,
                                     accum_out=sm32[0:128, 70 + ch:71 + ch]))
                pzm[ch % 2] = cS.n
            cS.inc(sc.activation(scrap[:, 0:4096], vt, AF.Square,
                                 accum_out=sm32[0:128, 78:79]))
            cS.wait(vec)
            cV.inc(vec.reduce_sum(c[:, 0:1], sm32[0:128, 70:78], mybir.AxisListType.X))
            cV.wait(sc)
            cS.inc(sc.activation(c[:, 2:3], c[:, 0:1], AF.Copy, scale=1.0 / 4096.0))
            cS.wait(vec)
            cV.inc(vec.engine_nop())
            cV.wait(sc)
            cS.inc(sc.activation(c[:, 3:4], c[:, 2:3], AF.Square))
            cS.wait(vec)
            vec.scalar_tensor_tensor(c[:, 4:5], sm32[0:128, 78:79], 1.0 / 4096.0,
                                     c[:, 3:4], ALU.mult, ALU.subtract)
            vec.drain()
            vec.tensor_scalar(c[:, 5:6], c[:, 4:5], 1e-5, None, ALU.add)
            vec.drain()
            cV.inc(vec.reciprocal(c[:, 6:7], c[:, 5:6]))
            cV.wait(sc)
            cS.inc(sc.activation(c[:, 7:8], c[:, 6:7], AF.Sqrt))
            cS.wait(vec)
            vec.wait_ge(dW.s, zsm)
            vec.tensor_scalar(zt, vt, c[:, 2:3], c[:, 7:8],
                              ALU.subtract, ALU.mult)
            cV.inc(vec.tensor_tensor(zt, zt, lnwbc, ALU.mult))
            acm = cV.n
            cV.wait(gp)
            dW.incd(gp.dma_start(p.scr["s_h1"][tt * 128:(tt + 1) * 128, :], zt))
            zsm = dW.n

    sy.wait_ge(cT.s, cT.n)
    sy.wait_ge(dW.s, dW.n)
    WP = [AB[0:128, kt * 1024:(kt + 1) * 1024] for kt in range(32)]
    for kt in range(32):
        dX.incd(sy.dma_start(WP[kt], p.scr["s_wp"][:, kt * 128:(kt + 1) * 128],
                             transpose=True))
    lnbT = SM[0:128, 4096:4128]
    for kt in range(32):
        dW.incd(gp.dma_start(lnbT[:, kt:kt + 1],
                             p.W["L1_ffn_ln_b"][kt * 128:(kt + 1) * 128][:, None]))
    dX.wait(te)
    dW.wait(te)
    for dt in range(8):
        for kt in range(32):
            cT.inc(te.matmul(pb[2][:, dt:dt + 1], WP[kt][:, dt * 128:(dt + 1) * 128],
                             lnbT[:, kt:kt + 1], start=(kt == 0), stop=(kt == 31)))
    cT.wait(vec)
    cV.inc(vec.tensor_copy(sm32[0:128, 80:88], pb[2][:, 0:8]))
    bvm = cV.n
    ovm = [dW.n, dW.n]
    pzm2 = [cV.n, cV.n]
    for grp in range(8):
        sy.wait_ge(cT.s, cT.n)
        for kt in range(32):
            dX.incd(sy.dma_start(AA[0:128, kt * 512:(kt + 1) * 512],
                                 p.scr["s_h1"][grp * 512:(grp + 1) * 512,
                                               kt * 128:(kt + 1) * 128], transpose=True))
        dX.wait(te)
        for dt in range(8):
            pz = pb[dt % 2]
            te.wait_ge(cV.s, pzm2[dt % 2])
            for kt in range(32):
                cT.inc(te.matmul(pz[:, :], WP[kt][:, dt * 128:(dt + 1) * 128],
                                 AA[0:128, kt * 512:(kt + 1) * 512],
                                 start=(kt == 0), stop=(kt == 31)))
            cT.wait(vec)
            vec.wait_ge(cV.s, bvm)
            otile = SM[0:128, (dt % 2) * 512:512 + (dt % 2) * 512]
            vec.wait_ge(dW.s, ovm[dt % 2])
            cV.inc(vec.tensor_scalar(otile, pz[:, :], sm32[0:128, 80 + dt:81 + dt],
                                     None, ALU.add))
            pzm2[dt % 2] = cV.n
            cV.wait(gp)
            dW.incd(gp.dma_start(p.scr["oc"][dt * 128:(dt + 1) * 128,
                                             grp * 512:(grp + 1) * 512], otile))
            ovm[dt % 2] = dW.n
    dW.wait(sy)


def kernel(**inputs):
    nc = build()
    consts = _consts()
    in_maps = []
    x = np.ascontiguousarray(inputs["L0_attn_x"])
    for core in range(8):
        m = {"x0": x[core]}
        for k, v in inputs.items():
            if k == "L0_attn_x":
                continue
            m[k] = np.ascontiguousarray(v)
        for k, v in consts.items():
            m[k] = v
        in_maps.append(m)
    res = run_bass_kernel_spmd(nc, in_maps, core_ids=list(range(8)))
    out = np.stack([res.results[i]["out"] for i in range(8)], axis=0)
    return out.astype(np.float32)


# version 05
# speedup vs baseline: 1.1207x; 1.0656x over previous
import numpy as np
import ml_dtypes
import concourse.bass as bass
import concourse.mybir as mybir
from concourse.bass_utils import run_bass_kernel_spmd

F32 = mybir.dt.float32
BF16 = mybir.dt.bfloat16
AF = mybir.ActivationFunctionType
ALU = mybir.AluOpType

T = 4096
D = 1024
NT = 32
L = 128
BF = ml_dtypes.bfloat16

_DEV_STOP = None
_DEV_DBG = None


class Ch:
    def __init__(self, ctx, nc, name):
        self.s = ctx.enter_context(nc.semaphore(name))
        self.n = 0

    def inc(self, ins, k=1):
        ins.then_inc(self.s, k)
        self.n += k
        return ins

    def incd(self, ins):
        return self.inc(ins, 16)

    def wait(self, eng, at=None):
        eng.wait_ge(self.s, self.n if at is None else at)


def _consts():
    i = np.arange(L)
    c = {}
    c["c_ident"] = np.eye(L).astype(BF)
    c["c_identf"] = np.eye(L).astype(np.float32)
    c["c_maskb"] = np.where(i[:, None] <= i[None, :], 0.0, -30000.0).astype(np.float32)
    c["c_trilS"] = (i[None, :] < i[:, None]).astype(BF)
    c["c_triuI"] = (i[:, None] <= i[None, :]).astype(BF)
    c["c_ones_row"] = np.ones((1, L), dtype=BF)
    c["c_sel8"] = (np.arange(512)[None, :] // 64 == np.arange(8)[:, None]).astype(BF)
    c["c_oneh"] = (np.arange(1024)[None, :] // 128 == np.arange(8)[:, None]).astype(BF)
    return c


class P:
    pass


def build():
    nc = bass.Bass("TRN2", target_bir_lowering=False, detect_race_conditions=False)
    p = P()
    p.nc = nc
    ap = lambda name, shape, dt, kind: nc.dram_tensor(name, shape, dt, kind=kind).ap()
    p.x0 = ap("x0", [T, D], F32, "ExternalInput")
    p.W = {}
    for name, shape in [
        ("L0_attn_Wz", [D, 2048]), ("L0_attn_Win", [D, 648]), ("L0_attn_Wzg", [2048, 512]),
        ("L0_attn_Wout", [512, D]), ("L0_attn_dt_bias", [8]), ("L0_attn_A_log", [8]),
        ("L0_attn_D_param", [8]),
        ("L0_ffn_w_gate", [2816, D]), ("L0_ffn_w_up", [2816, D]), ("L0_ffn_w_down", [D, 2816]),
        ("L1_attn_Wq", [D, D]), ("L1_attn_Wk", [D, D]), ("L1_attn_Wv", [D, D]),
        ("L1_attn_Wb", [D, 4]), ("L1_attn_conv_q", [4, D]), ("L1_attn_conv_k", [4, D]),
        ("L1_attn_conv_v", [4, D]), ("L1_attn_g_rms", [256]), ("L1_attn_Wo", [D, D]),
        ("L1_attn_ln_g", [D]), ("L1_attn_ln_b", [D]),
        ("L1_ffn_W_fc", [4096, D]), ("L1_ffn_ln_w", [4096]), ("L1_ffn_ln_b", [4096]),
        ("L1_ffn_W_proj", [D, 4096]),
    ]:
        p.W[name] = ap(name, shape, F32, "ExternalInput")
    p.C = {}
    for name, arr in _consts().items():
        p.C[name] = ap(name, list(arr.shape), BF16 if arr.dtype == BF else F32, "ExternalInput")
    p.out = ap("out", [T, D], F32, "ExternalOutput")
    p.dbg = ap("dbg", [4096, 4096], F32, "ExternalOutput") if _DEV_DBG else None
    p.scr = {}
    for name, shape, dt in [
        ("xres", [T, D], F32), ("xln", [T, D], BF16), ("oc", [D, T], BF16),
        ("s_xt", [512, T], BF16), ("s_xx", [512, T], BF16), ("s_y0", [512, T], BF16),
        ("s_xq", [T, D], BF16), ("s_wqg", [2816, D], BF16), ("s_wqu", [2816, D], BF16),
        ("s_wqd", [D, 2816], BF16), ("s_hq", [T, 2816], BF16),
        ("s_kk", [D, T], BF16), ("s_vv", [D, T], BF16), ("s_ot", [T, D], BF16),
        ("s_wf", [4096, D], BF16), ("s_wp", [D, 4096], BF16), ("s_h1", [T, 4096], BF16),
    ]:
        p.scr[name] = ap(name, shape, dt, "Internal")

    sb = lambda name, shape, dt: nc.alloc_sbuf_tensor(name, shape, dt)
    p.AA = sb("AA", [128, 16384], BF16)
    p.AB = sb("AB", [128, 32768], BF16)
    p.AC = sb("AC", [128, 13312], BF16)
    p.AF = sb("AFX", [128, 4096], F32)
    p.SM = sb("SMB", [128, 5120], BF16)
    p.xtm = [sb(f"xtm{i}", [128, D], F32) for i in range(2)]
    p.otm = [sb(f"otm{i}", [128, D], BF16) for i in range(2)]
    p.st48 = sb("st48", [128, 48], F32)
    p.agg2 = sb("agg2", [128, 2], F32)
    p.sc1 = sb("sc1", [128, 8], F32)
    p.lng = sb("lng", [128, D], BF16)
    p.lnb = sb("lnb", [128, D], BF16)
    p.grms = sb("grms", [128, D], BF16)
    p.sm32 = sb("sm32", [128, 1024], F32)
    p.bfm = sb("bfm", [4, 4096], BF16)
    p.cst = {}
    for n, sh, dt in [("c_ident", [L, L], BF16), ("c_identf", [L, L], F32),
                     ("c_maskb", [L, L], F32), ("c_trilS", [L, L], BF16),
                     ("c_triuI", [L, L], BF16), ("c_ones_row", [1, L], BF16),
                     ("c_sel8", [8, 512], BF16), ("c_oneh", [8, 1024], BF16)]:
        p.cst[n] = sb("s" + n, sh, dt)
    p.pb = [nc.alloc_psum_tensor(f"pb{i}", [128, 512], F32) for i in range(7)]
    p.pbt = nc.alloc_psum_tensor("pbt", [128, 1024], BF16)

    from contextlib import ExitStack
    ctx = ExitStack()
    p.dW = Ch(ctx, nc, "dW")
    p.dX = Ch(ctx, nc, "dX")
    p.dS = Ch(ctx, nc, "dS")
    p.cT = Ch(ctx, nc, "cT")
    p.cV = Ch(ctx, nc, "cV")
    p.cS = Ch(ctx, nc, "cS")
    blk = ctx.enter_context(nc.Block())

    @blk.sync
    def _(sync):
        _emit(p)

    ctx.close()
    return nc


def _emit(p):
    _load_consts(p)
    _glue(p, first=True)
    if _DEV_STOP == "ln0":
        _final(p); return
    _l0_attn(p)
    _glue(p)
    if _DEV_STOP == "l0a":
        _final(p); return
    _l0_ffn(p)
    _glue(p)
    if _DEV_STOP == "l0f":
        _final(p); return
    _l1_attn(p)
    _glue(p, inner_ln=True)
    if _DEV_STOP == "l1a":
        _final(p); return
    _l1_ffn(p)
    _glue(p, last=True)
    _final(p)


def _final(p):
    nc = p.nc
    if _DEV_DBG is not None and _DEV_DBG in p.scr:
        s = p.scr[_DEV_DBG]
        rows, cols = s.tensor.shape if hasattr(s, 'tensor') else (None, None)
        sh = list(p.scr[_DEV_DBG].shape)
        stage = p.SM[0:128, 0:4096]
        for r0 in range(0, sh[0], 128):
            p.cV.wait(nc.sync)
            p.dX.incd(nc.sync.dma_start(stage[:, 0:sh[1]], p.scr[_DEV_DBG][r0:r0 + 128, :]))
            p.dX.wait(nc.vector)
            p.cV.inc(nc.vector.tensor_copy(p.AF[:, 0:sh[1]], stage[:, 0:sh[1]]))
            p.cV.wait(nc.gpsimd)
            p.dW.incd(nc.gpsimd.dma_start(p.dbg[r0:r0 + 128, 0:sh[1]], p.AF[:, 0:sh[1]]))
            p.dW.wait(nc.vector)
    if _DEV_STOP is not None:
        src = p.scr["xres"]
        for tt in range(NT):
            sl = slice(tt * 128, (tt + 1) * 128)
            b = p.xtm[tt % 2]
            p.dW.wait(nc.sync)
            p.dX.incd(nc.sync.dma_start(b[:], src[sl, :]))
            p.dX.wait(nc.gpsimd)
            p.dW.incd(nc.gpsimd.dma_start(p.out[sl, :], b[:]))
    p.dW.wait(nc.gpsimd)
    p.dX.wait(nc.gpsimd)
    p.dS.wait(nc.gpsimd)
    p.cV.wait(nc.gpsimd)
    p.cS.wait(nc.gpsimd)
    p.cT.wait(nc.gpsimd)


def _load_consts(p):
    nc = p.nc
    for n in p.cst:
        p.dX.incd(nc.sync.dma_start(p.cst[n][:], p.C[n]))
    s_lg = p.SM[0:1, 0:D]
    s_lb = p.SM[0:1, D:2 * D]
    s_gr = p.SM[0:1, 2 * D:2 * D + 256]
    p.dW.incd(nc.gpsimd.dma_start(s_lg, p.W["L1_attn_ln_g"][None, :]))
    p.dW.incd(nc.gpsimd.dma_start(s_lb, p.W["L1_attn_ln_b"][None, :]))
    p.dW.incd(nc.gpsimd.dma_start(s_gr, p.W["L1_attn_g_rms"][None, :]))
    p.dX.wait(nc.tensor)
    p.dW.wait(nc.tensor)
    orow = p.cst["c_ones_row"]
    for j, src in enumerate([s_lg, s_lb]):
        for c in range(2):
            p.cT.inc(nc.tensor.matmul(p.pb[2 * j + c][:, :], orow[0:1, :],
                                      src[0:1, c * 512:(c + 1) * 512], start=True, stop=True))
    p.cT.inc(nc.tensor.matmul(p.pb[4][:, 0:256], orow[0:1, :], s_gr[0:1, :], start=True, stop=True))
    p.cT.wait(nc.vector)
    for j, dst in enumerate([p.lng, p.lnb]):
        for c in range(2):
            p.cV.inc(nc.vector.tensor_copy(dst[:, c * 512:(c + 1) * 512], p.pb[2 * j + c][:, :]))
    for rep in range(4):
        p.cV.inc(nc.vector.tensor_copy(p.grms[:, rep * 256:(rep + 1) * 256], p.pb[4][:, 0:256]))
    p.cV.wait(nc.tensor)


def _ln_tile(p, src, xl_out, eps=1e-5, sub_mu=True):
    nc = p.nc
    vec, sc = nc.vector, nc.scalar
    N = src.shape[-1]
    scrap = p.AF[0:128, 2 * D:3 * D]
    c = p.sc1
    p.cV.inc(vec.reduce_sum(c[:, 0:1], src, mybir.AxisListType.X))
    p.cV.wait(sc)
    p.cS.inc(sc.activation(scrap[:, 0:N], src, AF.Square, accum_out=c[:, 1:2]))
    p.cS.wait(vec)
    p.cV.inc(vec.tensor_scalar(c[:, 2:3], c[:, 0:1], 1.0 / N, None, ALU.mult))
    p.cV.wait(sc)
    p.cS.inc(sc.activation(c[:, 3:4], c[:, 2:3], AF.Square))
    p.cS.wait(vec)
    vec.scalar_tensor_tensor(c[:, 4:5], c[:, 1:2], 1.0 / N, c[:, 3:4],
                             ALU.mult, ALU.subtract)
    vec.drain()
    vec.tensor_scalar(c[:, 5:6], c[:, 4:5], eps, None, ALU.add)
    vec.drain()
    p.cV.inc(vec.reciprocal(c[:, 6:7], c[:, 5:6]))
    p.cV.wait(sc)
    p.cS.inc(sc.activation(c[:, 7:8], c[:, 6:7], AF.Sqrt))
    p.cS.wait(vec)
    vec.tensor_scalar(xl_out, src, c[:, 2:3], c[:, 7:8], ALU.subtract, ALU.mult)


def _glue(p, first=False, last=False, inner_ln=False):
    nc = p.nc
    vec, sc, gp = nc.vector, nc.scalar, nc.gpsimd
    xin = p.x0 if first else p.scr["xres"]
    oc = p.scr["oc"]
    xl2 = [p.SM[0:128, 3 * D:4 * D], p.SM[0:128, 4 * D:5 * D]]
    yb = p.AF[0:128, 0:D]
    zt = p.AF[0:128, D:2 * D]
    p.dW.wait(nc.scalar)
    p.dW.wait(nc.sync)
    vneed = [p.cV.n, p.cV.n]
    wneed = [p.dW.n, p.dW.n]
    for tt in range(NT):
        sl = slice(tt * 128, (tt + 1) * 128)
        b = tt % 2
        xb, ob, xl = p.xtm[b], p.otm[b], xl2[b]
        nc.sync.wait_ge(p.cV.s, vneed[b])
        nc.sync.wait_ge(p.dW.s, wneed[b])
        p.dX.incd(nc.sync.dma_start(xb[:], xin[sl, :]))
        if not first:
            nc.scalar.wait_ge(p.cV.s, vneed[b])
            nc.scalar.wait_ge(p.dW.s, wneed[b])
            p.dS.incd(nc.scalar.dma_start(ob[:], oc[:, sl], transpose=True))
        if inner_ln:
            nc.scalar.wait_ge(p.cV.s, vneed[b])
            nc.scalar.wait_ge(p.dW.s, wneed[b])
            p.dS.incd(nc.scalar.dma_start(xl, p.scr["xln"][sl, :]))
        p.dX.wait(vec)
        if not first:
            p.dS.wait(vec)
            if inner_ln:
                vec.tensor_tensor(yb, ob[:], xl, ALU.add)
                _ln_tile(p, yb, zt)
                vec.tensor_tensor(zt, zt, p.lng[:], ALU.mult)
                vec.tensor_tensor(zt, zt, p.lnb[:], ALU.add)
                vec.tensor_tensor(xb[:], xb[:], zt, ALU.add)
            else:
                vec.tensor_tensor(xb[:], xb[:], ob[:], ALU.add)
        if not last:
            _ln_tile(p, xb[:], xl)
        p.cV.inc(vec.engine_nop())
        vneed[b] = p.cV.n
        p.cV.wait(gp)
        if not last:
            p.dW.incd(gp.dma_start(p.scr["xln"][sl, :], xl))
        dst = p.out if last else p.scr["xres"]
        p.dW.incd(gp.dma_start(dst[sl, :], xb[:]))
        wneed[b] = p.dW.n
    p.dW.wait(nc.sync)
    p.dW.wait(nc.scalar)
    p.dW.wait(nc.vector)
    p.dW.wait(nc.tensor)


def _load_xlf(p, half, src=None, nrow=D):
    nc = p.nc
    src = src if src is not None else p.scr["xln"]
    t0 = half * 2048
    nc.sync.wait_ge(p.cT.s, p.cT.n)
    for i in range(nrow // 128):
        p.dX.incd(nc.sync.dma_start(p.AA[:, i * 2048:(i + 1) * 2048],
                                    src[t0:t0 + 2048, i * 128:(i + 1) * 128], transpose=True))


def _l0_attn(p):
    nc = p.nc
    pb, pbt = p.pb, p.pbt
    vec, sc, te, gp, sy = nc.vector, nc.scalar, nc.tensor, nc.gpsimd, nc.sync
    dW, dX, cT, cV, cS = p.dW, p.dX, p.cT, p.cV, p.cS
    SM, sm32, AFX = p.SM, p.sm32, p.AF

    db = sm32[0:8, 0:1]
    al = sm32[0:8, 1:2]
    dpc = sm32[0:8, 2:3]
    expa = sm32[0:8, 3:4]
    negA = sm32[0:8, 4:5]
    ones8 = sm32[0:8, 8:136]
    dpcols = sm32[0:128, 136:140]
    Wc = sm32[0:8, 144:272]
    WcT = sm32[0:128, 272:280]
    decT = sm32[0:128, 280:288]
    Dmf = sm32[0:128, 288:416]
    dAL = sm32[0:64, 416:424]
    expw_f = sm32[0:8, 424:552]
    delta = AFX[0:8, 0:2048]
    stF = AFX[0:64, 2048:2560]
    wlo_f = sm32[0:8, 680:808]
    dec_f = sm32[0:8, 552:680]
    w_chunk = sm32[0:8, 808:936]
    xdst = [SM[0:128, j * 512:(j + 1) * 512] for j in range(4)]
    deltab_c = SM[0:8, 2048:2560]
    xdtm = SM[0:128, 0:512]
    CBT = SM[0:128, 512:640]
    Btm = SM[0:128, 640:704]
    GT = SM[0:128, 704:832]
    UT = SM[0:128, 832:960]
    Xdec = SM[0:128, 960:1024]
    ytile = [SM[0:128, 1024 + 128 * j:1152 + 128 * j] for j in range(2)]
    Csc = SM[0:64, 1280:1408]
    whi = SM[0:8, 1408:1536]
    wlo = SM[0:8, 1536:1664]
    expw_b = SM[0:8, 1664:1792]
    stB = p.AC[0:64, 5184:5696]
    dpb = p.AC[0:8, 5696:5697]
    oneh = p.cst["c_oneh"]

    dW.incd(gp.dma_start(db, p.W["L0_attn_dt_bias"][:, None]))
    dW.incd(gp.dma_start(al, p.W["L0_attn_A_log"][:, None]))
    dW.incd(gp.dma_start(dpc, p.W["L0_attn_D_param"][:, None]))
    WinT = [p.AC[0:128, i * 648:(i + 1) * 648] for i in range(8)]
    for i in range(8):
        dW.incd(gp.dma_start(WinT[i], p.W["L0_attn_Win"][i * 128:(i + 1) * 128, :]))

    cV.inc(vec.memset(stF, 0.0))
    cV.inc(vec.memset(stB, 0.0))
    cV.inc(vec.memset(ones8, 1.0))
    dW.wait(sc)
    cS.inc(sc.activation(expa, al, AF.Exp))
    cS.wait(vec)
    cV.inc(vec.tensor_scalar(negA, expa, -1.0, None, ALU.mult))
    cV.inc(vec.tensor_copy(dpb, dpc))
    cV.wait(te)
    dX.wait(te)
    for j in range(4):
        cT.inc(te.matmul(pb[6][:, j:j + 1], p.cst["c_sel8"][:, j * 128:(j + 1) * 128],
                         dpb, start=True, stop=True))
    cT.wait(vec)
    cV.inc(vec.tensor_copy(dpcols, pb[6][:, 0:4]))
    cV.wait(te)

    xd_store = [dW.n] * 4
    yst = [dW.n, dW.n]
    for half in range(2):
        _load_xlf(p, half)
        XLF = [p.AA[0:128, i * 2048:(i + 1) * 2048] for i in range(8)]
        Bt = p.AB[0:64, 0:2048]
        Ct = p.AB[0:64, 2048:4096]
        Xa = [p.AB[0:128, (2 + j) * 2048:(3 + j) * 2048] for j in range(4)]

        for cc in range(4):
            csl = slice(cc * 512, (cc + 1) * 512)
            g0 = half * 2048 + cc * 512
            tsl = slice(g0, g0 + 512)
            dX.wait(te)
            dW.wait(te)
            cV.wait(te)
            cS.wait(te)
            for kt in range(8):
                cT.inc(te.matmul(pb[0][0:64, :], WinT[kt][:, 0:64], XLF[kt][:, csl],
                                 start=(kt == 0), stop=(kt == 7)))
            for kt in range(8):
                cT.inc(te.matmul(pb[6][0:64, :], WinT[kt][:, 64:128], XLF[kt][:, csl],
                                 start=(kt == 0), stop=(kt == 7)))
            for kt in range(8):
                cT.inc(te.matmul(pb[1][0:8, :], WinT[kt][:, 128:136], XLF[kt][:, csl],
                                 start=(kt == 0), stop=(kt == 7)))
            for j in range(4):
                for kt in range(8):
                    cT.inc(te.matmul(pb[2 + j][:, :], WinT[kt][:, 136 + 128 * j:264 + 128 * j],
                                     XLF[kt][:, csl], start=(kt == 0), stop=(kt == 7)))
            cT.wait(vec)
            cT.wait(sc)
            cV.inc(vec.tensor_copy(Bt[:, csl], pb[0][0:64, :]))
            cV.inc(vec.tensor_copy(Ct[:, csl], pb[6][0:64, :]))
            cS.inc(sc.activation(delta[:, csl], pb[1][0:8, :], AF.Exp, bias=db))
            cS.inc(sc.activation(delta[:, csl], delta[:, csl], AF.Ln, bias=1.0))
            cS.wait(vec)
            cV.inc(vec.tensor_copy(deltab_c, delta[:, csl]))
            for j in range(4):
                cV.inc(vec.tensor_copy(Xa[j][:, csl], pb[2 + j][:, :]))
            cV.wait(gp)
            for j in range(4):
                dW.incd(gp.dma_start(p.scr["s_xx"][j * 128:(j + 1) * 128, tsl], Xa[j][:, csl]))
            cV.wait(te)
            for j in range(4):
                cT.inc(te.matmul(pb[6][:, :], p.cst["c_sel8"][:, j * 128:(j + 1) * 128],
                                 deltab_c, start=True, stop=True))
                cT.wait(vec)
                vec.wait_ge(dW.s, xd_store[j])
                cV.inc(vec.tensor_tensor(xdst[j], Xa[j][:, csl], pb[6][:, :], ALU.mult))
                cV.wait(te)
                cV.wait(gp)
                dW.incd(gp.dma_start(p.scr["s_xt"][j * 128:(j + 1) * 128, tsl], xdst[j]))
                xd_store[j] = dW.n
        sy.wait_ge(dW.s, dW.n)
        vec.wait_ge(dW.s, dW.n)
        sc.wait_ge(dW.s, dW.n)
        for scl in range(16):
            t0 = (half * 16 + scl) * 128
            ch_sl = slice(scl * 128, (scl + 1) * 128)
            cS.wait(vec)
            cV.inc(vec.tensor_scalar(w_chunk, delta[:, scl * 128:(scl + 1) * 128],
                                     negA, None, ALU.mult))
            cV.inc(vec.tensor_tensor_scan(Wc, ones8, w_chunk, 0.0, ALU.mult, ALU.add))
            cV.inc(vec.tensor_copy(whi, Wc))
            cV.inc(vec.tensor_tensor(wlo_f, Wc, whi, ALU.subtract))
            cV.inc(vec.tensor_copy(wlo, wlo_f))
            cV.inc(vec.tensor_scalar(dec_f, Wc, Wc[:, 127:128], None, ALU.subtract))
            cV.wait(sc)
            cS.inc(sc.activation(expw_f, Wc, AF.Exp))
            cS.inc(sc.activation(dec_f, dec_f, AF.Exp, scale=-1.0))
            cS.wait(vec)
            cV.inc(vec.tensor_copy(expw_b, expw_f))
            cV.wait(te)
            cS.wait(te)
            cT.inc(te.transpose(pb[5][:, 0:8], Wc, p.cst["c_identf"][0:8, 0:8]))
            cT.inc(te.transpose(pb[5][:, 8:16], dec_f, p.cst["c_identf"][0:8, 0:8]))
            cT.inc(te.matmul(pb[0][:, 0:128], Bt[:, ch_sl], Ct[:, ch_sl],
                             start=True, stop=True))
            cT.inc(te.transpose(pbt[:, 0:64], Bt[:, ch_sl], p.cst["c_ident"][0:64, 0:64]))
            cT.wait(vec)
            cV.inc(vec.tensor_copy(WcT, pb[5][:, 0:8]))
            cV.inc(vec.tensor_copy(decT, pb[5][:, 8:16]))
            cV.inc(vec.tensor_copy(CBT, pb[0][:, 0:128]))
            cV.inc(vec.tensor_copy(Btm, pbt[:, 0:64]))
            sy.wait_ge(cV.s, cV.n)
            sy.wait_ge(cT.s, cT.n)
            dX.incd(sy.dma_start(xdtm, p.scr["s_xt"][0:512, t0:t0 + 128], transpose=True))
            cV.wait(te)
            dX.wait(te)
            dX.wait(vec)
            tprev = cT.n
            for h in range(8):
                pw = pb[1 + (h % 2)]
                pe = pb[3 + (h % 2)]
                py = pb[5 + ((h // 2) % 2)]
                cV.wait(te)
                cS.wait(te)
                lh = oneh[:, h * 128:(h + 1) * 128]
                cT.inc(te.matmul(pw[:, 0:128], lh, whi, start=True, stop=False))
                cT.inc(te.matmul(pw[:, 0:128], lh, wlo, start=False, stop=True))
                cT.inc(te.matmul(pe[0:64, 0:128], lh[:, 0:64], expw_b, start=True, stop=True))
                cT.wait(vec)
                cV.inc(vec.tensor_scalar(Dmf, pw[:, 0:128], WcT[:, h:h + 1],
                                         None, ALU.subtract))
                cV.inc(vec.tensor_tensor(Dmf, Dmf, p.cst["c_maskb"][:, :], ALU.add))
                cV.wait(sc)
                cS.inc(sc.activation(GT, Dmf, AF.Exp))
                vec.wait_ge(cT.s, tprev)
                cV.inc(vec.tensor_tensor(Csc, Ct[:, ch_sl], pe[0:64, 0:128], ALU.mult))
                cV.inc(vec.tensor_copy(dAL[:, h:h + 1], pe[0:64, 127:128]))
                cS.wait(vec)
                cV.inc(vec.tensor_tensor(UT, GT, CBT, ALU.mult))
                if _DEV_DBG == "probe6" and half == 0 and scl == 12 and h == 6:
                    cV.wait(gp)
                    dW.incd(gp.dma_start(p.dbg[2712:2776, 0:512], stB))
                    dW.incd(gp.dma_start(p.dbg[2776:2840, 0:128], Csc))
                    dW.wait(vec)
                if False and _DEV_DBG == "probe6" and half == 0 and scl == 12 and h == 6:
                    cV.wait(gp)
                    cS.wait(gp)
                    dW.incd(gp.dma_start(p.dbg[2048:2176, 0:128], Dmf))
                    dW.incd(gp.dma_start(p.dbg[2176:2304, 0:128], GT))
                    dW.incd(gp.dma_start(p.dbg[2304:2432, 0:128], UT))
                    dW.incd(gp.dma_start(p.dbg[2432:2560, 0:128], CBT))
                    dW.incd(gp.dma_start(p.dbg[2560:2568, 0:128], Wc))
                    dW.incd(gp.dma_start(p.dbg[2568:2696, 0:512], xdtm))
                    dW.incd(gp.dma_start(p.dbg[2696:2704, 0:128], whi))
                    dW.incd(gp.dma_start(p.dbg[2704:2712, 0:128], wlo))
                    dW.wait(vec)
                    dW.wait(sc)
                cV.inc(vec.tensor_scalar(Xdec, xdtm[:, 64 * h:64 * (h + 1)],
                                         decT[:, h:h + 1], None, ALU.mult))
                cV.wait(te)
                ro = 64 * (h % 2)
                cT.inc(te.matmul(py[ro:ro + 64, 0:128], xdtm[:, 64 * h:64 * (h + 1)], UT,
                                 start=True, stop=False))
                cT.inc(te.matmul(py[ro:ro + 64, 0:128], stB[:, 64 * h:64 * (h + 1)], Csc,
                                 start=False, stop=True))
                cT.inc(te.matmul(pb[0][0:64, 128 + 64 * (h % 4):192 + 64 * (h % 4)],
                                 Btm, Xdec, start=True, stop=True))
                tprev = cT.n
                if h % 2 == 1:
                    cT.wait(sc)
                    yt = ytile[(h // 2) % 2]
                    sc.wait_ge(dW.s, yst[(h // 2) % 2])
                    cS.inc(sc.copy(yt, py[:, 0:128]))
                    cS.wait(gp)
                    dW.incd(gp.dma_start(
                        p.scr["s_y0"][128 * (h // 2):128 * (h // 2 + 1), t0:t0 + 128], yt))
                    yst[(h // 2) % 2] = dW.n
                    if _DEV_DBG == "probe6" and half == 0 and scl == 12 and h == 7:
                        dW.incd(gp.dma_start(p.dbg[2840:2968, 0:128], yt))
                        dW.incd(gp.dma_start(p.dbg[2968:3096, 0:64], Xdec))
                        dW.incd(gp.dma_start(p.dbg[3096:3224, 0:8], decT))
                        dW.wait(sc)
                if h % 4 == 3:
                    cT.wait(vec)
                    for hh in range(h - 3, h + 1):
                        cV.inc(vec.scalar_tensor_tensor(
                            stF[:, 64 * hh:64 * (hh + 1)], stF[:, 64 * hh:64 * (hh + 1)],
                            dAL[:, hh:hh + 1],
                            pb[0][0:64, 128 + 64 * (hh % 4):192 + 64 * (hh % 4)],
                            ALU.mult, ALU.add))
            cV.inc(vec.tensor_copy(stB, stF))
            cV.wait(te)

    WzSB = [p.AB[0:128, i * 2048:(i + 1) * 2048] for i in range(8)]
    WzgSB = [p.AB[0:128, 16384 + i * 512:16384 + (i + 1) * 512] for i in range(16)]
    WoutSB = [p.AB[0:128, 24576 + i * 1024:24576 + (i + 1) * 1024] for i in range(4)]
    cT.wait(gp)
    cV.wait(gp)
    for i in range(8):
        dW.incd(gp.dma_start(WzSB[i], p.W["L0_attn_Wz"][i * 128:(i + 1) * 128, :]))
    for i in range(16):
        dW.incd(gp.dma_start(WzgSB[i], p.W["L0_attn_Wzg"][i * 128:(i + 1) * 128, :]))
    for i in range(4):
        dW.incd(gp.dma_start(WoutSB[i], p.W["L0_attn_Wout"][i * 128:(i + 1) * 128, :]))
    zdw = dW.n
    zsil = [p.AC[0:128, i * 512:(i + 1) * 512] for i in range(16)]
    sg2 = [SM[0:128, 0:512], SM[0:128, 4096:4608]]
    ych = [SM[0:128, 512 + j * 512:1024 + j * 512] for j in range(4)]
    t1b = SM[0:128, 2560:3072]
    otile = [SM[0:128, 3072 + j * 512:3584 + j * 512] for j in range(2)]
    xch = [p.AC[0:128, 10240 + j * 512:10752 + j * 512] for j in range(4)]
    prodt = [p.AC[0:128, 8192 + j * 512:8704 + j * 512] for j in range(4)]
    for half in range(2):
        _load_xlf(p, half)
        XLF = [p.AA[0:128, i * 2048:(i + 1) * 2048] for i in range(8)]
        for cc in range(4):
            csl = slice(cc * 512, (cc + 1) * 512)
            g0 = half * 2048 + cc * 512
            tsl = slice(g0, g0 + 512)
            dX.wait(te)
            dW.wait(te)
            zv = [cV.n, cV.n]
            sgv = [cV.n, cV.n]
            for mz in range(16):
                pz = pb[mz % 2]
                sg = sg2[mz % 2]
                te.wait_ge(cV.s, zv[mz % 2])
                for kt in range(8):
                    cT.inc(te.matmul(pz[:, :], WzSB[kt][:, mz * 128:(mz + 1) * 128],
                                     XLF[kt][:, csl], start=(kt == 0), stop=(kt == 7)))
                cT.wait(sc)
                sc.wait_ge(cV.s, sgv[mz % 2])
                cS.inc(sc.activation(sg, pz[:, :], AF.Sigmoid))
                cS.wait(vec)
                cV.inc(vec.tensor_tensor(zsil[mz], sg, pz[:, :], ALU.mult))
                zv[mz % 2] = cV.n
                sgv[mz % 2] = cV.n
            sy.wait_ge(cV.s, cV.n)
            sy.wait_ge(cT.s, cT.n)
            sy.wait_ge(dW.s, zdw)
            for j in range(4):
                dX.incd(sy.dma_start(ych[j], p.scr["s_y0"][j * 128:(j + 1) * 128, tsl]))
                dX.incd(sy.dma_start(xch[j], p.scr["s_xx"][j * 128:(j + 1) * 128, tsl]))
            if _DEV_DBG == "probez" and half == 0 and cc == 0:
                cV.wait(gp)
                cS.wait(gp)
                for j in range(4):
                    dW.incd(gp.dma_start(p.dbg[2048 + 128 * j:2176 + 128 * j, 0:512], ych[j]))
                    dW.incd(gp.dma_start(p.dbg[2560 + 128 * j:2688 + 128 * j, 0:512], xch[j]))
                for mzd in range(4):
                    dW.incd(gp.dma_start(p.dbg[3072 + 128 * mzd:3200 + 128 * mzd, 0:512], zsil[mzd]))
                dW.wait(vec)
                dW.wait(sy)
            pv = [cV.n, cV.n]
            for mg in range(4):
                pzg = pb[2 + mg % 2]
                te.wait_ge(cV.s, pv[mg % 2])
                for kt2 in range(16):
                    cT.inc(te.matmul(pzg[:, :], WzgSB[kt2][:, mg * 128:(mg + 1) * 128],
                                     zsil[kt2], start=(kt2 == 0), stop=(kt2 == 15)))
                dX.wait(vec)
                cT.wait(vec)
                cV.inc(vec.tensor_scalar(t1b, xch[mg], dpcols[:, mg:mg + 1], None, ALU.mult))
                cV.inc(vec.tensor_tensor(t1b, t1b, ych[mg], ALU.add))
                cV.inc(vec.tensor_tensor(prodt[mg], t1b, pzg[:, :], ALU.mult))
                pv[mg % 2] = cV.n
            ov = [dW.n, dW.n]
            for mo in range(8):
                po = pb[4 + mo % 2]
                cS.wait(te)
                cV.wait(te)
                for kt3 in range(4):
                    cT.inc(te.matmul(po[:, :], WoutSB[kt3][:, mo * 128:(mo + 1) * 128],
                                     prodt[kt3], start=(kt3 == 0), stop=(kt3 == 3)))
                cT.wait(sc)
                sc.wait_ge(dW.s, ov[mo % 2])
                cS.inc(sc.copy(otile[mo % 2], po[:, :]))
                cS.wait(gp)
                dW.incd(gp.dma_start(p.scr["oc"][mo * 128:(mo + 1) * 128, tsl], otile[mo % 2]))
                ov[mo % 2] = dW.n
    dW.wait(sy)


def _bc16(handle, row_pitch, col0, nblk):
    return bass.AP(handle, col0, [[row_pitch, 128], [1, nblk], [0, 16]])

def _blk16(handle, row_pitch, col0, nblk):
    return bass.AP(handle, col0, [[row_pitch, 128], [16, nblk], [1, 16]])


def _quant_fp4(p, W, bi, f32_in=False, sgn_wait=None):
    nc = p.nc
    vec, sc = nc.vector, nc.scalar
    nb = W // 16
    AC, AFX, SM, sm32 = p.AC, p.AF, p.SM, p.sm32
    if f32_in:
        xt = AFX[0:128, bi * 1024:bi * 1024 + W]
        xv = _blk16(AFX, 4096, bi * 1024, nb)
        an2 = AFX[0:128, 2048 + bi * 1024:2048 + bi * 1024 + W]
    else:
        xt = AC[0:128, bi * 1408:bi * 1408 + W]
        xv = _blk16(AC, 13312, bi * 1408, nb)
        an2 = AFX[0:128, bi * 2048:bi * 2048 + W]
    sgn = AC[0:128, 8448 + bi * 1408:8448 + bi * 1408 + W]
    tb = AC[0:128, 5632 + bi * 1408:5632 + bi * 1408 + W]
    axq = AC[0:128, 2816 + bi * 1408:2816 + bi * 1408 + W]
    q2 = SM[0:128, bi * 1408:bi * 1408 + W]
    m = sm32[0:128, bi * 88:bi * 88 + nb]
    rcp = sm32[0:128, 256 + bi * 88:256 + bi * 88 + nb]
    rcp36 = sm32[0:128, 432 + bi * 88:432 + bi * 88 + nb]
    shb = SM[0:128, 2816 + bi * 96:2816 + bi * 96 + nb]
    if sgn_wait is not None:
        sc.wait_ge(p.cV.s, sgn_wait)
    p.cS.inc(sc.sign(sgn, xt))
    signm = p.cS.n
    vec.tensor_reduce(m, xv, mybir.AxisListType.X, ALU.max, apply_absolute_value=True)
    vec.drain()
    vec.tensor_scalar(m, m, 1e-12, None, ALU.max)
    vec.drain()
    vec.reciprocal(rcp, m)
    vec.tensor_scalar(shb, m, 1.0 / 12.0, None, ALU.mult)
    vec.drain()
    vec.tensor_scalar(rcp, rcp, 6.0, None, ALU.mult)
    vec.drain()
    vec.tensor_tensor(rcp36, rcp, rcp, ALU.mult)
    vec.tensor_tensor(an2, xt, xt, ALU.mult)
    vec.tensor_tensor(an2, an2, _bc16(sm32, 1024, 432 + bi * 88, nb), ALU.mult)
    vec.tensor_scalar(q2, an2, 0.0625, 1.0, ALU.is_gt, ALU.mult)
    for th2, wgt in [(0.5625, 1.0), (1.5625, 1.0), (3.0625, 1.0),
                     (6.25, 2.0), (12.25, 2.0), (25.0, 4.0)]:
        vec.tensor_scalar(tb, an2, th2, wgt, ALU.is_gt, ALU.mult)
        vec.tensor_tensor(q2, q2, tb, ALU.add)
    anm = p.cV.n + 1
    p.cV.inc(vec.engine_nop())
    vec.wait_ge(p.cS.s, signm)
    vec.tensor_tensor(q2, q2, sgn, ALU.mult)
    p.cV.inc(vec.tensor_tensor(axq, q2, _bc16(SM, 5120, 2816 + bi * 96, nb), ALU.mult))
    return anm


def _l0_ffn(p):
    nc = p.nc
    pb = p.pb
    vec, sc, te, gp, sy = nc.vector, nc.scalar, nc.tensor, nc.gpsimd, nc.sync
    dW, dX, dS, cT, cV, cS = p.dW, p.dX, p.dS, p.cT, p.cV, p.cS
    SM, AC, AFX = p.SM, p.AC, p.AF

    jobs = []
    for tt in range(NT):
        jobs.append((p.scr["xln"][tt * 128:(tt + 1) * 128, :], 1024, "s_xq",
                     (tt * 128, 0), False))
    for wn, sn in [("L0_ffn_w_gate", "s_wqg"), ("L0_ffn_w_up", "s_wqu")]:
        for r in range(22):
            jobs.append((p.W[wn][r * 128:(r + 1) * 128, :], 1024, sn, (r * 128, 0), True))
    for r in range(8):
        for c0, w in [(0, 1024), (1024, 1024), (2048, 768)]:
            jobs.append((p.W["L0_ffn_w_down"][r * 128:(r + 1) * 128, c0:c0 + w], w,
                         "s_wqd", (r * 128, c0), True))
    st2 = [dW.n, dW.n]
    an2m = [cV.n, cV.n]
    sg2m = [cS.n, cS.n]
    ld2 = [dX.n, dX.n]
    vq2 = [cV.n, cV.n]
    prev_path = None
    for idx, (srcap, W, dst, (r0, c0), isw) in enumerate(jobs):
        bi = idx % 2
        if prev_path is not None and isw != prev_path:
            sy.wait_ge(cV.s, cV.n)
            sy.wait_ge(cS.s, cS.n)
        prev_path = isw
        xt = AFX[0:128, bi * 1024:bi * 1024 + W] if isw else AC[0:128, bi * 1408:bi * 1408 + W]
        sy.wait_ge(cV.s, an2m[bi])
        sy.wait_ge(cS.s, sg2m[bi])
        dX.incd(sy.dma_start(xt, srcap))
        ld2[bi] = dX.n
        sc.wait_ge(dX.s, ld2[bi])
        vec.wait_ge(dX.s, ld2[bi])
        vec.wait_ge(dW.s, st2[bi])
        an2m[bi] = _quant_fp4(p, W, bi, f32_in=isw)
        sg2m[bi] = cS.n
        vq2[bi] = cV.n
        cV.wait(gp)
        dW.incd(gp.dma_start(p.scr[dst][r0:r0 + 128, c0:c0 + W],
                             AC[0:128, 2816 + bi * 1408:2816 + bi * 1408 + W]))
        st2[bi] = dW.n

    sgv = [cV.n, cV.n]
    sgb = [AC[0:128, 11264:11776], AC[0:128, 11776:12288]]
    for nh in range(2):
        WG = [p.AB[0:128, kt * 1408:(kt + 1) * 1408] for kt in range(8)]
        WU = [p.AB[0:128, 11264 + kt * 1408:11264 + (kt + 1) * 1408] for kt in range(8)]
        sy.wait_ge(cT.s, cT.n)
        sy.wait_ge(dW.s, dW.n)
        for kt in range(8):
            dX.incd(sy.dma_start(WG[kt], p.scr["s_wqg"][nh * 1408:(nh + 1) * 1408,
                                                        kt * 128:(kt + 1) * 128], transpose=True))
            dX.incd(sy.dma_start(WU[kt], p.scr["s_wqu"][nh * 1408:(nh + 1) * 1408,
                                                        kt * 128:(kt + 1) * 128], transpose=True))
        for half in range(2):
            sy.wait_ge(cT.s, cT.n)
            XQ = [p.AA[0:128, kt * 2048:(kt + 1) * 2048] for kt in range(8)]
            for kt in range(8):
                dX.incd(sy.dma_start(XQ[kt], p.scr["s_xq"][half * 2048:(half + 1) * 2048,
                                                           kt * 128:(kt + 1) * 128], transpose=True))
            dX.wait(te)
            for tl in range(16):
                tt = half * 16 + tl
                bi = tl % 2
                h_row = AC[0:128, bi * 1408:bi * 1408 + 1408]
                hm = 0
                for nci, (nc0, nw) in enumerate([(0, 512), (512, 512), (1024, 384)]):
                    pg = pb[4 + nci % 2]
                    pu = pb[2 + nci % 2]
                    cV.wait(te)
                    cS.wait(te)
                    for kt in range(8):
                        cT.inc(te.matmul(pg[:, 0:nw], XQ[kt][:, tl * 128:(tl + 1) * 128],
                                         WG[kt][:, nc0:nc0 + nw], start=(kt == 0), stop=(kt == 7)))
                    for kt in range(8):
                        cT.inc(te.matmul(pu[:, 0:nw], XQ[kt][:, tl * 128:(tl + 1) * 128],
                                         WU[kt][:, nc0:nc0 + nw], start=(kt == 0), stop=(kt == 7)))
                    cT.wait(sc)
                    sg = sgb[nci % 2]
                    sc.wait_ge(cV.s, sgv[nci % 2])
                    cS.inc(sc.activation(sg[:, 0:nw], pg[:, 0:nw], AF.Sigmoid))
                    cS.wait(vec)
                    if nci == 0:
                        vec.wait_ge(cV.s if False else dW.s, st2[bi])
                        vec.wait_ge(cV.s, an2m[bi]) if False else None
                    vec.tensor_tensor(sg[:, 0:nw], sg[:, 0:nw], pg[:, 0:nw], ALU.mult)
                    p.cV.inc(vec.tensor_tensor(h_row[:, nc0:nc0 + nw], sg[:, 0:nw],
                                               pu[:, 0:nw], ALU.mult))
                    sgv[nci % 2] = cV.n
                hm = cV.n
                an2m[bi] = _quant_fp4(p, 1408, bi, f32_in=False, sgn_wait=hm)
                sg2m[bi] = cS.n
                cV.wait(gp)
                dW.incd(gp.dma_start(p.scr["s_hq"][tt * 128:(tt + 1) * 128,
                                                   nh * 1408:(nh + 1) * 1408],
                                     AC[0:128, 2816 + bi * 1408:2816 + bi * 1408 + 1408]))
                st2[bi] = dW.n
    WD = [p.AB[0:128, kt * 1024:(kt + 1) * 1024] for kt in range(22)]
    sy.wait_ge(cT.s, cT.n)
    sy.wait_ge(dW.s, dW.n)
    for kt in range(22):
        dX.incd(sy.dma_start(WD[kt], p.scr["s_wqd"][:, kt * 128:(kt + 1) * 128], transpose=True))
    ov = [dW.n, dW.n]
    for qt in range(4):
        HQ = []
        sy.wait_ge(cT.s, cT.n)
        for kt in range(22):
            dst = (p.AA[0:128, kt * 1024:(kt + 1) * 1024] if kt < 16
                   else AC[0:128, (kt - 16) * 1024:(kt - 15) * 1024])
            HQ.append(dst)
            dX.incd(sy.dma_start(dst, p.scr["s_hq"][qt * 1024:(qt + 1) * 1024,
                                                    kt * 128:(kt + 1) * 128], transpose=True))
        dX.wait(te)
        for dt in range(8):
            for tch in range(2):
                po = pb[tch % 2]
                cS.wait(te)
                for kt in range(22):
                    cT.inc(te.matmul(po[:, :], WD[kt][:, dt * 128:(dt + 1) * 128],
                                     HQ[kt][:, tch * 512:(tch + 1) * 512],
                                     start=(kt == 0), stop=(kt == 21)))
                cT.wait(sc)
                otile = SM[0:128, 3072 + (tch % 2) * 512:3584 + (tch % 2) * 512]
                sc.wait_ge(dW.s, ov[tch % 2])
                cS.inc(sc.copy(otile, po[:, :]))
                cS.wait(gp)
                dW.incd(gp.dma_start(
                    p.scr["oc"][dt * 128:(dt + 1) * 128,
                                qt * 1024 + tch * 512:qt * 1024 + (tch + 1) * 512], otile))
                ov[tch % 2] = dW.n
    dW.wait(sy)


def _l1_attn(p):
    nc = p.nc
    pb, pbt = p.pb, p.pbt
    vec, sc, te, gp, sy = nc.vector, nc.scalar, nc.tensor, nc.gpsimd, nc.sync
    dW, dX, dS, cT, cV, cS = p.dW, p.dX, p.dS, p.cT, p.cV, p.cS
    SM, sm32, AC, AA, AB, AFX = p.SM, p.sm32, p.AC, p.AA, p.AB, p.AF

    WSB = [AC[0:128, kt * 1024:(kt + 1) * 1024] for kt in range(8)]
    csrc2 = [AC[0:128, 8192 + j * 515:8707 + j * 515] for j in range(2)]
    convT = lambda wi, mt: sm32[0:128, (wi * 8 + mt) * 4:(wi * 8 + mt) * 4 + 4]
    WbSB = [AC[0:128, 9318 + kt * 4:9322 + kt * 4] for kt in range(8)]
    halo = lambda wi, mt: AC[0:128, 9350 + (wi * 8 + mt) * 3:9353 + (wi * 8 + mt) * 3]
    onesc = AC[0:128, 9422:9423]
    acc = SM[0:128, 0:512]
    tmp = SM[0:128, 512:1024]
    sg2 = [SM[0:128, 1024 + j * 512:1536 + j * 512] for j in range(2)]
    stash2 = [SM[0:128, 2048 + j * 512:2560 + j * 512] for j in range(2)]
    q2b2 = [SM[0:128, 3072 + j * 512:3584 + j * 512] for j in range(2)]
    rsqb = SM[0:1, 4096:4608]
    nbetaT = sm32[0:128, 384:512]
    betaT = sm32[0:128, 256:384]
    cvst = SM[0:4, 0:3072]
    bfm = p.bfm
    wnames = ["L1_attn_Wq", "L1_attn_Wk", "L1_attn_Wv"]
    cnames = ["L1_attn_conv_q", "L1_attn_conv_k", "L1_attn_conv_v"]

    for wi in range(3):
        dW.incd(gp.dma_start(cvst[:, wi * 1024:(wi + 1) * 1024], p.W[cnames[wi]]))
    for kt in range(8):
        dW.incd(gp.dma_start(WbSB[kt], p.W["L1_attn_Wb"][kt * 128:(kt + 1) * 128, :]))
    cV.inc(vec.memset(AC[0:128, 9350:9422], 0.0))
    cV.inc(vec.memset(onesc, 1.0))
    dW.wait(te)
    cV.wait(te)
    cS.wait(te)
    for wi in range(3):
        for mt in range(8):
            j4 = (wi * 8 + mt) * 4
            cT.inc(te.transpose(pbt[:, j4:j4 + 4],
                                cvst[0:4, wi * 1024 + mt * 128:wi * 1024 + (mt + 1) * 128],
                                p.cst["c_ident"][0:4, 0:4]))
    cT.wait(vec)
    cV.inc(vec.tensor_copy(sm32[0:128, 0:96], pbt[:, 0:96]))
    cvev = cV.n

    qst = [dW.n] * 2
    sgv = [cV.n] * 2
    csv = [cV.n] * 2
    pzm = [cS.n] * 2
    q2m = [cT.n] * 2
    accm = cS.n
    p2m = cV.n
    p3m = cV.n
    bcm = cT.n
    p4m = cS.n
    for half in range(2):
        _load_xlf(p, half)
        XLF = [AA[0:128, i * 2048:(i + 1) * 2048] for i in range(8)]
        for wi in range(3):
            wmark = cT.n
            for kt in range(8):
                gp.wait_ge(cT.s, wmark)
                dW.incd(gp.dma_start(WSB[kt], p.W[wnames[wi]][kt * 128:(kt + 1) * 128, :]))
            for cc in range(4):
                csl = slice(cc * 512, (cc + 1) * 512)
                g0 = half * 2048 + cc * 512
                dX.wait(te)
                dW.wait(te)
                for h4 in range(4):
                    for mi in range(2):
                        mt = 2 * h4 + mi
                        cb = mt % 2
                        pz = pb[mt % 2]
                        cs = csrc2[cb]
                        te.wait_ge(cS.s, pzm[mt % 2])
                        for kt in range(8):
                            cT.inc(te.matmul(pz[:, :], WSB[kt][:, mt * 128:(mt + 1) * 128],
                                             XLF[kt][:, csl], start=(kt == 0), stop=(kt == 7)))
                        sc.wait_ge(cV.s, csv[cb])
                        cS.inc(sc.copy(cs[:, 0:3], halo(wi, mt)))
                        cT.wait(sc)
                        cS.inc(sc.copy(cs[:, 3:515], pz[:, :]))
                        pzm[mt % 2] = cS.n
                        cS.wait(vec)
                        wc = convT(wi, mt)
                        vec.tensor_scalar(acc, cs[:, 0:512], wc[:, 0:1], None, ALU.mult)
                        for j in (1, 2, 3):
                            vec.tensor_scalar(tmp, cs[:, j:j + 512], wc[:, j:j + 1],
                                              None, ALU.mult)
                            vec.tensor_tensor(acc, acc, tmp, ALU.add)
                        cV.inc(vec.tensor_copy(halo(wi, mt), cs[:, 512:515]))
                        csv[cb] = cV.n
                        sg = sg2[mi]
                        cV.wait(sc)
                        cS.inc(sc.activation(sg, acc, AF.Sigmoid))
                        accm = cS.n
                        cS.wait(vec)
                        st = stash2[mi]
                        vec.wait_ge(dW.s, qst[mi])
                        cV.inc(vec.tensor_tensor(st, sg, acc, ALU.mult))
                        sgv[mi] = cV.n
                        if wi < 2:
                            cV.wait(sc)
                            sc.wait_ge(cT.s, q2m[mi])
                            cS.inc(sc.activation(q2b2[mi], st, AF.Square))
                            cS.wait(te)
                            te.wait_ge(cV.s, p2m)
                            cT.inc(te.matmul(pb[2][0:1, 0:512], onesc, q2b2[mi],
                                             start=(mi == 0), stop=(mi == 1)))
                            q2m[mi] = cT.n
                        else:
                            cV.wait(gp)
                            dW.incd(gp.dma_start(
                                p.scr["s_vv"][mt * 128:(mt + 1) * 128, g0:g0 + 512], st))
                            qst[mi] = dW.n
                    if wi < 2:
                        cT.wait(vec)
                        cV.inc(vec.tensor_scalar(AFX[0:1, 0:512], pb[2][0:1, 0:512],
                                                 1e-6, None, ALU.add))
                        p2m = cV.n
                        vec.drain()
                        vec.reciprocal(AFX[0:1, 512:1024], AFX[0:1, 0:512])
                        cV.inc(vec.engine_nop())
                        cV.wait(sc)
                        sc.wait_ge(cT.s, bcm)
                        cS.inc(sc.activation(rsqb, AFX[0:1, 512:1024], AF.Sqrt))
                        cS.wait(te)
                        te.wait_ge(cV.s, p3m)
                        cT.inc(te.matmul(pb[3][:, :], p.cst["c_ones_row"][0:1, :], rsqb,
                                         start=True, stop=True))
                        bcm = cT.n
                        cT.wait(vec)
                        for mi in range(2):
                            mt = 2 * h4 + mi
                            st = stash2[mi]
                            if wi == 0:
                                cV.inc(vec.tensor_tensor(
                                    AB[0:128, mt * 4096 + g0:mt * 4096 + g0 + 512],
                                    st, pb[3][:, :], ALU.mult))
                            else:
                                cV.inc(vec.tensor_tensor(st, st, pb[3][:, :], ALU.mult))
                                cV.wait(gp)
                                dW.incd(gp.dma_start(
                                    p.scr["s_kk"][mt * 128:(mt + 1) * 128, g0:g0 + 512], st))
                                qst[mi] = dW.n
                        p3m = cV.n
        for cc in range(4):
            csl = slice(cc * 512, (cc + 1) * 512)
            g0 = half * 2048 + cc * 512
            te.wait_ge(cS.s, p4m)
            for kt in range(8):
                cT.inc(te.matmul(pb[4][0:4, :], WbSB[kt], XLF[kt][:, csl],
                                 start=(kt == 0), stop=(kt == 7)))
            cT.wait(sc)
            cS.inc(sc.activation(bfm[0:4, g0:g0 + 512], pb[4][0:4, :], AF.Sigmoid))
            p4m = cS.n
    cS.wait(te)
    cV.wait(te)
    for sco in range(32):
        cT.inc(te.transpose(pbt[:, 4 * sco:4 * sco + 4], bfm[0:4, sco * 128:(sco + 1) * 128],
                            p.cst["c_ident"][0:4, 0:4]))
    cT.wait(vec)
    cV.inc(vec.tensor_copy(betaT, pbt[:, 0:128]))
    vec.tensor_scalar(nbetaT, betaT, -1.0, None, ALU.mult)
    cV.inc(vec.engine_nop())

    S_f = lambda h, kt: AFX[0:128, (2 * h + kt) * 256:(2 * h + kt + 1) * 256]
    S_b = lambda h, kt: AC[0:128, (2 * h + kt) * 256:(2 * h + kt + 1) * 256]
    ktm2 = [AC[0:128, 2048 + j * 1024:3072 + j * 1024] for j in range(2)]
    vtm2 = [AC[0:128, 4096 + j * 1024:5120 + j * 1024] for j in range(2)]
    Dl = [[SM[0:128, s * 512 + j * 256:s * 512 + (j + 1) * 256] for j in range(2)]
          for s in range(2)]
    Cp = [[(SM[0:128, 1024 + s * 512 + j * 256:1152 + s * 512 + j * 256],
            SM[0:128, 1152 + s * 512 + j * 256:1280 + s * 512 + j * 256])
           for j in range(2)] for s in range(2)]
    Cpair = [[SM[0:128, 1024 + s * 512 + j * 256:1280 + s * 512 + j * 256]
              for j in range(2)] for s in range(2)]
    PTb = [SM[0:128, 2048 + s * 128:2176 + s * 128] for s in range(2)]
    scrap2 = [SM[0:128, 2304 + s * 256:2560 + s * 256] for s in range(2)]
    Rv2 = [SM[0:128, 2816 + s * 256:3072 + s * 256] for s in range(2)]
    och = SM[0:128, 3328:4352]
    dW.wait(sy)
    dW.wait(te)
    cV.wait(te)
    cS.wait(te)
    cV.inc(vec.memset(AFX[0:128, 0:2048], 0.0))
    cV.inc(vec.memset(AC[0:128, 0:2048], 0.0))
    mk = {"p0m": [cV.n] * 2, "p4m2": [cV.n] * 2, "p5mS": [cS.n] * 2,
          "p5mV": [cV.n] * 2, "p1mV": [cV.n] * 2, "p1mS": [cS.n] * 2,
          "p3ev": [cV.n] * 2, "pbtm": [cS.n] * 2, "ost": dW.n}
    sbm = [cS.n] * 4
    kvmark = [cT.n, cT.n]

    def head_prog(h, s, sci, b, lsl):
        kf = [AA[0:128, (2 * h + kt) * 2048 + lsl:(2 * h + kt) * 2048 + lsl + 128]
              for kt in range(2)]
        qf = [AB[0:128, (2 * h + kt) * 4096 + sci * 128:(2 * h + kt) * 4096 + sci * 128 + 128]
              for kt in range(2)]
        bcol = betaT[:, 4 * sci + h:4 * sci + h + 1]
        nbcol = nbetaT[:, 4 * sci + h:4 * sci + h + 1]
        bank = pb[0] if s == 0 else pb[4]
        pKK = bank[:, 0:128]
        pPT = bank[:, 128:256]
        pR = bank[:, 256:512]
        pO = bank[:, 256:512]
        pA = (pb[1] if s == 0 else pb[2])[:, 0:256]
        pSq = (pb[3] if s == 0 else pb[6])[:, 0:256]
        pC1T = pbt[:, s * 128:(s + 1) * 128]
        te.wait_ge(cV.s, mk["p0m"][s])
        te.wait_ge(cV.s, mk["p5mV"][s])
        te.wait_ge(cS.s, mk["p5mS"][s])
        for kt in range(2):
            cT.inc(te.matmul(pKK, kf[kt], kf[kt], start=(kt == 0), stop=(kt == 1)))
        for kt in range(2):
            cT.inc(te.matmul(pPT, kf[kt], qf[kt], start=(kt == 0), stop=(kt == 1)))
        te.wait_ge(cS.s, sbm[h])
        te.wait_ge(cV.s, mk["p4m2"][s])
        for kt in range(2):
            cT.inc(te.matmul(pR, kf[kt], S_b(h, kt), start=(kt == 0), stop=(kt == 1)))
        mmk = cT.n
        yield
        vec.wait_ge(cT.s, mmk)
        vec.wait_ge(cS.s, mk["p1mS"][s])
        C0, C0T = Cp[s][0]
        vec.tensor_scalar(C0, pKK, nbcol, None, ALU.mult)
        cV.inc(vec.tensor_tensor(C0, C0, p.cst["c_trilS"][:, :], ALU.mult))
        c1m = cV.n
        cV.inc(vec.tensor_tensor(PTb[s], pPT, p.cst["c_triuI"][:, :], ALU.mult))
        mk["p0m"][s] = cV.n
        vec.tensor_scalar(Rv2[s], vtm2[b][:, 256 * h:256 * h + 256], bcol, None, ALU.mult)
        cV.inc(vec.scalar_tensor_tensor(Dl[s][0], pR, nbcol, Rv2[s], ALU.mult, ALU.add))
        mk["p4m2"][s] = cV.n
        dme, dm = "V", cV.n
        te.wait_ge(cV.s, c1m)
        te.wait_ge(cS.s, mk["pbtm"][s])
        cT.inc(te.transpose(pC1T, C0, p.cst["c_ident"][:, :]))
        cT.wait(sc)
        cS.inc(sc.copy(C0T, pC1T))
        mk["pbtm"][s] = cS.n
        ctme, ctm = ["S", None], [cS.n, 0]
        yield
        for i in range(7):
            cur = i % 2
            nxt = 1 - cur
            Cc, CcT = Cp[s][cur]
            te.wait_ge(cV.s, mk["p1mV"][s])
            te.wait_ge(cS.s, mk["p1mS"][s])
            te.wait_ge(cV.s if ctme[cur] == "V" else cS.s, ctm[cur]) \
                if ctm[cur] is not None else None
            te.wait_ge(cV.s if dme == "V" else cS.s, dm)
            cT.inc(te.matmul(pA, p.cst["c_ident"][:, :], Dl[s][cur],
                             start=True, stop=False))
            cT.inc(te.matmul(pA, CcT, Dl[s][cur], start=False, stop=True))
            am = cT.n
            if i < 6:
                te.wait_ge(cV.s, mk["p3ev"][s])
                cT.inc(te.matmul(pSq[:, 0:128], CcT, Cc, start=True, stop=True))
                cT.inc(te.matmul(pSq[:, 128:256], Cc, CcT, start=True, stop=True))
                sqm = cT.n
            if i % 2 == 0:
                vec.wait_ge(cT.s, am)
                cV.inc(vec.tensor_copy(Dl[s][nxt], pA))
                mk["p1mV"][s] = cV.n
                dme, dm = "V", cV.n
            else:
                sc.wait_ge(cT.s, am)
                cS.inc(sc.copy(Dl[s][nxt], pA))
                mk["p1mS"][s] = cS.n
                dme, dm = "S", cS.n
            if i < 6:
                vec.wait_ge(cT.s, sqm)
                cV.inc(vec.tensor_copy(Cpair[s][nxt], pSq))
                mk["p3ev"][s] = cV.n
                ctme[nxt], ctm[nxt] = "V", cV.n
            yield
        te.wait_ge(cS.s, mk["p5mS"][s])
        te.wait_ge(cV.s, mk["p5mV"][s])
        te.wait_ge(cV.s if dme == "V" else cS.s, dm)
        cT.inc(te.matmul(pO, PTb[s], Dl[s][1], start=True, stop=False))
        for kt in range(2):
            cT.inc(te.matmul(pO, qf[kt], S_b(h, kt), start=False, stop=(kt == 1)))
        om = cT.n
        yield
        sc.wait_ge(cT.s, om)
        cS.inc(sc.activation(scrap2[s], pO, AF.Square,
                             accum_out=sm32[0:128, 16 + h:17 + h]))
        mk["p5mS"][s] = cS.n
        cS.wait(vec)
        vec.tensor_scalar(sm32[:, 20 + h:21 + h], sm32[:, 16 + h:17 + h],
                          1.0 / 256.0, 1e-5, ALU.mult, ALU.add)
        vec.drain()
        cV.inc(vec.reciprocal(sm32[:, 24 + h:25 + h], sm32[:, 20 + h:21 + h]))
        cV.wait(sc)
        cS.inc(sc.activation(sm32[:, 28 + h:29 + h], sm32[:, 24 + h:25 + h], AF.Sqrt))
        cS.wait(vec)
        if h == 0:
            vec.wait_ge(dW.s, mk["ost"])
        osl = och[:, 256 * h:256 * h + 256]
        vec.tensor_scalar(osl, pO, sm32[:, 28 + h:29 + h], None, ALU.mult)
        cV.inc(vec.tensor_tensor(osl, osl, p.grms[:, 0:256], ALU.mult))
        mk["p5mV"][s] = cV.n
        pSup = pb[1] if s == 0 else pb[2]
        te.wait_ge(cV.s if dme == "V" else cS.s, dm)
        te.wait_ge(cV.s, mk["p1mV"][s])
        for kt in range(2):
            cT.inc(te.matmul(pSup[:, kt * 256:(kt + 1) * 256],
                             ktm2[b][:, 256 * h + kt * 128:256 * h + (kt + 1) * 128],
                             Dl[s][1], start=True, stop=True))
        kvmark[b] = cT.n
        smk = cT.n
        yield
        vec.wait_ge(cT.s, smk)
        for kt in range(2):
            vec.tensor_tensor(S_f(h, kt), S_f(h, kt), pSup[:, kt * 256:(kt + 1) * 256],
                              ALU.add)
        cV.inc(vec.engine_nop())
        mk["p1mV"][s] = cV.n
        cV.wait(sc)
        for kt in range(2):
            cS.inc(sc.copy(S_b(h, kt), S_f(h, kt)))
        sbm[h] = cS.n

    for sci in range(32):
        half = sci // 16
        if sci % 16 == 0:
            sy.wait_ge(cT.s, cT.n)
            for kt2 in range(8):
                dX.incd(sy.dma_start(AA[:, kt2 * 2048:(kt2 + 1) * 2048],
                                     p.scr["s_kk"][kt2 * 128:(kt2 + 1) * 128,
                                                   half * 2048:(half + 1) * 2048]))
        b = sci % 2
        sy.wait_ge(cT.s, kvmark[b])
        tsl = slice(sci * 128, (sci + 1) * 128)
        dX.incd(sy.dma_start(ktm2[b], p.scr["s_kk"][:, tsl], transpose=True))
        dX.incd(sy.dma_start(vtm2[b], p.scr["s_vv"][:, tsl], transpose=True))
        dX.wait(te)
        dX.wait(vec)
        lsl = (sci % 16) * 128
        for pair in range(2):
            gens = [head_prog(2 * pair, 0, sci, b, lsl),
                    head_prog(2 * pair + 1, 1, sci, b, lsl)]
            alive = [True, True]
            while alive[0] or alive[1]:
                for gi in range(2):
                    if alive[gi]:
                        try:
                            next(gens[gi])
                        except StopIteration:
                            alive[gi] = False
        cV.wait(gp)
        dW.incd(gp.dma_start(p.scr["s_ot"][tsl, :], och))
        mk["ost"] = dW.n

    WoSB = [AC[0:128, kt * 1024:(kt + 1) * 1024] for kt in range(8)]
    wmark2 = cT.n
    for kt in range(8):
        gp.wait_ge(cT.s, wmark2)
        gp.wait_ge(cS.s, cS.n)
        dW.incd(gp.dma_start(WoSB[kt], p.W["L1_attn_Wo"][kt * 128:(kt + 1) * 128, :]))
    ovm = [dW.n, dW.n]
    pzm2 = [cS.n] * 2
    for half in range(2):
        sy.wait_ge(cT.s, cT.n)
        sy.wait_ge(dW.s, dW.n)
        for i in range(8):
            dX.incd(sy.dma_start(AA[:, i * 2048:(i + 1) * 2048],
                                 p.scr["s_ot"][half * 2048:(half + 1) * 2048,
                                               i * 128:(i + 1) * 128], transpose=True))
        dX.wait(te)
        dW.wait(te)
        for mo in range(8):
            for cc in range(4):
                po = pb[cc % 2]
                te.wait_ge(cS.s, pzm2[cc % 2])
                for kt in range(8):
                    cT.inc(te.matmul(po[:, :], WoSB[kt][:, mo * 128:(mo + 1) * 128],
                                     AA[0:128, kt * 2048 + cc * 512:kt * 2048 + (cc + 1) * 512],
                                     start=(kt == 0), stop=(kt == 7)))
                cT.wait(sc)
                otile = SM[0:128, (cc % 2) * 512:512 + (cc % 2) * 512]
                sc.wait_ge(dW.s, ovm[cc % 2])
                cS.inc(sc.copy(otile, po[:, :]))
                pzm2[cc % 2] = cS.n
                cS.wait(gp)
                dW.incd(gp.dma_start(
                    p.scr["oc"][mo * 128:(mo + 1) * 128,
                                half * 2048 + cc * 512:half * 2048 + (cc + 1) * 512], otile))
                ovm[cc % 2] = dW.n
    dW.wait(sy)


def _l1_ffn(p):
    nc = p.nc
    pb = p.pb
    vec, sc, te, gp, sy = nc.vector, nc.scalar, nc.tensor, nc.gpsimd, nc.sync
    dW, dX, dS, cT, cV, cS = p.dW, p.dX, p.dS, p.cT, p.cV, p.cS
    SM, sm32, AC, AA, AB, AFX = p.SM, p.sm32, p.AC, p.AA, p.AB, p.AF
    NW = 4096 * 1024

    scrap = AC[0:128, 0:4096]
    lnwrow = SM[0:1, 0:4096]
    lnwbc = AC[0:128, 8192:12288]
    dW.incd(gp.dma_start(lnwrow, p.W["L1_ffn_ln_w"][None, :]))
    dW.wait(te)
    cV.wait(te)
    cS.wait(te)
    for cb2 in range(8):
        cT.inc(te.matmul(pb[3][:, :], p.cst["c_ones_row"][0:1, :],
                         lnwrow[0:1, cb2 * 512:(cb2 + 1) * 512], start=True, stop=True))
        cT.wait(vec)
        cV.inc(vec.tensor_copy(lnwbc[:, cb2 * 512:(cb2 + 1) * 512], pb[3][:, :]))
        cV.wait(te)
    fcb = [AFX[0:128, 0:1024], AFX[0:128, 1024:2048]]
    pjb = [AFX[0:128, 0:2048], AFX[0:128, 2048:4096]]
    ldm = [dX.n, dX.n]
    csm = [cS.n, cS.n]
    for i in range(32):
        b = i % 2
        sy.wait_ge(cS.s, csm[b])
        dX.incd(sy.dma_start(fcb[b], p.W["L1_ffn_W_fc"][i * 128:(i + 1) * 128, :]))
        ldm[b] = dX.n
        sc.wait_ge(dX.s, ldm[b])
        cS.inc(sc.activation(scrap[:, 0:1024], fcb[b], AF.Abs,
                             accum_out=sm32[0:128, i:i + 1]))
        csm[b] = cS.n
    csm = [cS.n, cS.n]
    for i in range(16):
        r, hh = i // 2, i % 2
        b = i % 2
        sy.wait_ge(cS.s, csm[b])
        dX.incd(sy.dma_start(pjb[b], p.W["L1_ffn_W_proj"][r * 128:(r + 1) * 128,
                                                          hh * 2048:(hh + 1) * 2048]))
        ldm[b] = dX.n
        sc.wait_ge(dX.s, ldm[b])
        cS.inc(sc.activation(scrap[:, 0:2048], pjb[b], AF.Abs,
                             accum_out=sm32[0:128, 32 + i:33 + i]))
        csm[b] = cS.n
    cS.wait(vec)
    vec.reduce_sum(sm32[0:128, 56:57], sm32[0:128, 0:32], mybir.AxisListType.X)
    vec.reduce_sum(sm32[0:128, 57:58], sm32[0:128, 32:48], mybir.AxisListType.X)
    vec.memset(sm32[0:128, 58:59], 1.0)
    cV.inc(vec.memset(sm32[0:1, 64:192], 1.0))
    cV.wait(te)
    cT.inc(te.matmul(pb[2][0:1, 0:2], sm32[0:128, 58:59], sm32[0:128, 56:58],
                     start=True, stop=True))
    cT.wait(vec)
    vec.tensor_scalar(sm32[0:1, 60:62], pb[2][0:1, 0:2], 1.0 / NW, 1e-6, ALU.mult, ALU.add)
    vec.drain()
    vec.reciprocal(sm32[0:1, 62:64], sm32[0:1, 60:62])
    vec.drain()
    cV.inc(vec.tensor_scalar(sm32[0:1, 62:64], sm32[0:1, 62:64], 2.0, None, ALU.mult))
    cV.wait(te)
    cT.inc(te.matmul(pb[2][:, 8:10], sm32[0:1, 64:192], sm32[0:1, 62:64],
                     start=True, stop=True))
    cT.wait(vec)
    cV.inc(vec.tensor_copy(sm32[0:128, 48:50], pb[2][:, 8:10]))
    rgfc = sm32[0:128, 48:49]
    rgpj = sm32[0:128, 49:50]
    stm = [dW.n, dW.n]
    vcm = [cV.n, cV.n]
    m1f = [SM[0:128, 0:1024], SM[0:128, 1024:2048]]
    m2f = SM[0:128, 2048:3072]
    wqf = [SM[0:128, 3072:4096], SM[0:128, 4096:5120]]
    for i in range(32):
        b = i % 2
        sy.wait_ge(cV.s, vcm[b])
        dX.incd(sy.dma_start(fcb[b], p.W["L1_ffn_W_fc"][i * 128:(i + 1) * 128, :]))
        ldm[b] = dX.n
        vec.wait_ge(dX.s, ldm[b])
        vec.wait_ge(dW.s, stm[b])
        vec.tensor_scalar(m1f[b], fcb[b], rgfc, 1.0, ALU.mult, ALU.is_gt)
        vec.tensor_scalar(m2f, fcb[b], rgfc, -1.0, ALU.mult, ALU.is_lt)
        cV.inc(vec.tensor_tensor(wqf[b], m1f[b], m2f, ALU.subtract))
        vcm[b] = cV.n
        cV.wait(gp)
        dW.incd(gp.dma_start(p.scr["s_wf"][i * 128:(i + 1) * 128, :], wqf[b]))
        stm[b] = dW.n
    m1p = [SM[0:128, 0:2048], SM[0:128, 2048:4096]]
    dW.wait(vec)
    vcm = [cV.n, cV.n]
    stm = [dW.n, dW.n]
    for i in range(16):
        r, hh = i // 2, i % 2
        b = i % 2
        sy.wait_ge(cV.s, vcm[b])
        dX.incd(sy.dma_start(pjb[b], p.W["L1_ffn_W_proj"][r * 128:(r + 1) * 128,
                                                          hh * 2048:(hh + 1) * 2048]))
        ldm[b] = dX.n
        vec.wait_ge(dX.s, ldm[b])
        vec.wait_ge(dW.s, stm[b])
        vec.tensor_scalar(m1p[b], pjb[b], rgpj, 1.0, ALU.mult, ALU.is_gt)
        vec.tensor_scalar(scrap[:, 0:2048], pjb[b], rgpj, -1.0, ALU.mult, ALU.is_lt)
        cV.inc(vec.tensor_tensor(m1p[b], m1p[b], scrap[:, 0:2048], ALU.subtract))
        vcm[b] = cV.n
        cV.wait(gp)
        dW.incd(gp.dma_start(p.scr["s_wp"][r * 128:(r + 1) * 128,
                                           hh * 2048:(hh + 1) * 2048], m1p[b]))
        stm[b] = dW.n

    sy.wait_ge(cT.s, cT.n)
    sy.wait_ge(dW.s, dW.n)
    WF = [AB[0:128, dt * 4096:(dt + 1) * 4096] for dt in range(8)]
    for dt in range(8):
        dX.incd(sy.dma_start(WF[dt], p.scr["s_wf"][:, dt * 128:(dt + 1) * 128],
                             transpose=True))
    vt = AC[0:128, 4096:8192]
    zt = SM[0:128, 0:4096]
    c = p.sc1
    pzm = [cS.n, cS.n]
    acm = cV.n
    zsm = dW.n
    for half in range(2):
        _load_xlf(p, half)
        XLF = [AA[0:128, i * 2048:(i + 1) * 2048] for i in range(8)]
        dX.wait(te)
        for tl in range(16):
            tt = half * 16 + tl
            for ch in range(8):
                pz = pb[ch % 2]
                te.wait_ge(cS.s, pzm[ch % 2])
                for kt in range(8):
                    cT.inc(te.matmul(pz[:, :], XLF[kt][:, tl * 128:(tl + 1) * 128],
                                     AB[0:128, kt * 4096 + ch * 512:kt * 4096 + (ch + 1) * 512],
                                     start=(kt == 0), stop=(kt == 7)))
                cT.wait(sc)
                if ch == 0:
                    sc.wait_ge(cV.s, acm)
                cS.inc(sc.activation(vt[:, ch * 512:(ch + 1) * 512], pz[:, :], AF.Gelu,
                                     accum_out=sm32[0:128, 70 + ch:71 + ch]))
                pzm[ch % 2] = cS.n
            cS.inc(sc.activation(scrap[:, 0:4096], vt, AF.Square,
                                 accum_out=sm32[0:128, 78:79]))
            cS.wait(vec)
            cV.inc(vec.reduce_sum(c[:, 0:1], sm32[0:128, 70:78], mybir.AxisListType.X))
            cV.wait(sc)
            cS.inc(sc.activation(c[:, 2:3], c[:, 0:1], AF.Copy, scale=1.0 / 4096.0))
            cS.wait(vec)
            cV.inc(vec.engine_nop())
            cV.wait(sc)
            cS.inc(sc.activation(c[:, 3:4], c[:, 2:3], AF.Square))
            cS.wait(vec)
            vec.scalar_tensor_tensor(c[:, 4:5], sm32[0:128, 78:79], 1.0 / 4096.0,
                                     c[:, 3:4], ALU.mult, ALU.subtract)
            vec.drain()
            vec.tensor_scalar(c[:, 5:6], c[:, 4:5], 1e-5, None, ALU.add)
            vec.drain()
            cV.inc(vec.reciprocal(c[:, 6:7], c[:, 5:6]))
            cV.wait(sc)
            cS.inc(sc.activation(c[:, 7:8], c[:, 6:7], AF.Sqrt))
            cS.wait(vec)
            vec.wait_ge(dW.s, zsm)
            vec.tensor_scalar(zt, vt, c[:, 2:3], c[:, 7:8],
                              ALU.subtract, ALU.mult)
            cV.inc(vec.tensor_tensor(zt, zt, lnwbc, ALU.mult))
            acm = cV.n
            cV.wait(gp)
            dW.incd(gp.dma_start(p.scr["s_h1"][tt * 128:(tt + 1) * 128, :], zt))
            zsm = dW.n

    sy.wait_ge(cT.s, cT.n)
    sy.wait_ge(dW.s, dW.n)
    WP = [AB[0:128, kt * 1024:(kt + 1) * 1024] for kt in range(32)]
    for kt in range(32):
        dX.incd(sy.dma_start(WP[kt], p.scr["s_wp"][:, kt * 128:(kt + 1) * 128],
                             transpose=True))
    lnbT = SM[0:128, 4096:4128]
    for kt in range(32):
        dW.incd(gp.dma_start(lnbT[:, kt:kt + 1],
                             p.W["L1_ffn_ln_b"][kt * 128:(kt + 1) * 128][:, None]))
    dX.wait(te)
    dW.wait(te)
    for dt in range(8):
        for kt in range(32):
            cT.inc(te.matmul(pb[2][:, dt:dt + 1], WP[kt][:, dt * 128:(dt + 1) * 128],
                             lnbT[:, kt:kt + 1], start=(kt == 0), stop=(kt == 31)))
    cT.wait(vec)
    cV.inc(vec.tensor_copy(sm32[0:128, 80:88], pb[2][:, 0:8]))
    bvm = cV.n
    ovm = [dW.n, dW.n]
    pzm2 = [cV.n, cV.n]
    for grp in range(8):
        sy.wait_ge(cT.s, cT.n)
        for kt in range(32):
            dX.incd(sy.dma_start(AA[0:128, kt * 512:(kt + 1) * 512],
                                 p.scr["s_h1"][grp * 512:(grp + 1) * 512,
                                               kt * 128:(kt + 1) * 128], transpose=True))
        dX.wait(te)
        for dt in range(8):
            pz = pb[dt % 2]
            te.wait_ge(cV.s, pzm2[dt % 2])
            for kt in range(32):
                cT.inc(te.matmul(pz[:, :], WP[kt][:, dt * 128:(dt + 1) * 128],
                                 AA[0:128, kt * 512:(kt + 1) * 512],
                                 start=(kt == 0), stop=(kt == 31)))
            cT.wait(vec)
            vec.wait_ge(cV.s, bvm)
            otile = SM[0:128, (dt % 2) * 512:512 + (dt % 2) * 512]
            vec.wait_ge(dW.s, ovm[dt % 2])
            cV.inc(vec.tensor_scalar(otile, pz[:, :], sm32[0:128, 80 + dt:81 + dt],
                                     None, ALU.add))
            pzm2[dt % 2] = cV.n
            cV.wait(gp)
            dW.incd(gp.dma_start(p.scr["oc"][dt * 128:(dt + 1) * 128,
                                             grp * 512:(grp + 1) * 512], otile))
            ovm[dt % 2] = dW.n
    dW.wait(sy)


def kernel(**inputs):
    nc = build()
    consts = _consts()
    in_maps = []
    x = np.ascontiguousarray(inputs["L0_attn_x"])
    for core in range(8):
        m = {"x0": x[core]}
        for k, v in inputs.items():
            if k == "L0_attn_x":
                continue
            m[k] = np.ascontiguousarray(v)
        for k, v in consts.items():
            m[k] = v
        in_maps.append(m)
    res = run_bass_kernel_spmd(nc, in_maps, core_ids=list(range(8)))
    out = np.stack([res.results[i]["out"] for i in range(8)], axis=0)
    return out.astype(np.float32)


# version 07 (final)
# speedup vs baseline: 1.1568x; 1.0322x over previous
import numpy as np
import ml_dtypes
import concourse.bass as bass
import concourse.mybir as mybir
from concourse.bass_utils import run_bass_kernel_spmd

F32 = mybir.dt.float32
BF16 = mybir.dt.bfloat16
AF = mybir.ActivationFunctionType
ALU = mybir.AluOpType

T = 4096
D = 1024
NT = 32
L = 128
BF = ml_dtypes.bfloat16

_DEV_STOP = None
_DEV_DBG = None


class Ch:
    def __init__(self, ctx, nc, name):
        self.s = ctx.enter_context(nc.semaphore(name))
        self.n = 0

    def inc(self, ins, k=1):
        ins.then_inc(self.s, k)
        self.n += k
        return ins

    def incd(self, ins):
        return self.inc(ins, 16)

    def wait(self, eng, at=None):
        eng.wait_ge(self.s, self.n if at is None else at)


def _consts():
    i = np.arange(L)
    c = {}
    c["c_ident"] = np.eye(L).astype(BF)
    c["c_identf"] = np.eye(L).astype(np.float32)
    c["c_maskb"] = np.where(i[:, None] <= i[None, :], 0.0, -30000.0).astype(np.float32)
    c["c_trilS"] = (i[None, :] < i[:, None]).astype(BF)
    c["c_triuI"] = (i[:, None] <= i[None, :]).astype(BF)
    c["c_ones_row"] = np.ones((1, L), dtype=BF)
    c["c_sel8"] = (np.arange(512)[None, :] // 64 == np.arange(8)[:, None]).astype(BF)
    c["c_oneh"] = (np.arange(1024)[None, :] // 128 == np.arange(8)[:, None]).astype(BF)
    return c


class P:
    pass


def build():
    nc = bass.Bass("TRN2", target_bir_lowering=False, detect_race_conditions=False)
    p = P()
    p.nc = nc
    ap = lambda name, shape, dt, kind: nc.dram_tensor(name, shape, dt, kind=kind).ap()
    p.x0 = ap("x0", [T, D], F32, "ExternalInput")
    p.W = {}
    for name, shape in [
        ("L0_attn_Wz", [D, 2048]), ("L0_attn_Win", [D, 648]), ("L0_attn_Wzg", [2048, 512]),
        ("L0_attn_Wout", [512, D]), ("L0_attn_dt_bias", [8]), ("L0_attn_A_log", [8]),
        ("L0_attn_D_param", [8]),
        ("L0_ffn_w_gate", [2816, D]), ("L0_ffn_w_up", [2816, D]), ("L0_ffn_w_down", [D, 2816]),
        ("L1_attn_Wq", [D, D]), ("L1_attn_Wk", [D, D]), ("L1_attn_Wv", [D, D]),
        ("L1_attn_Wb", [D, 4]), ("L1_attn_conv_q", [4, D]), ("L1_attn_conv_k", [4, D]),
        ("L1_attn_conv_v", [4, D]), ("L1_attn_g_rms", [256]), ("L1_attn_Wo", [D, D]),
        ("L1_attn_ln_g", [D]), ("L1_attn_ln_b", [D]),
        ("L1_ffn_W_fc", [4096, D]), ("L1_ffn_ln_w", [4096]), ("L1_ffn_ln_b", [4096]),
        ("L1_ffn_W_proj", [D, 4096]),
    ]:
        p.W[name] = ap(name, shape, F32, "ExternalInput")
    p.C = {}
    for name, arr in _consts().items():
        p.C[name] = ap(name, list(arr.shape), BF16 if arr.dtype == BF else F32, "ExternalInput")
    p.out = ap("out", [T, D], F32, "ExternalOutput")
    p.dbg = ap("dbg", [4096, 4096], F32, "ExternalOutput") if _DEV_DBG else None
    p.scr = {}
    for name, shape, dt in [
        ("xres", [T, D], F32), ("xln", [T, D], BF16), ("oc", [D, T], BF16),
        ("s_xt", [512, T], BF16), ("s_xx", [512, T], BF16), ("s_y0", [512, T], BF16),
        ("s_xq", [T, D], BF16), ("s_wqg", [2816, D], BF16), ("s_wqu", [2816, D], BF16),
        ("s_wqd", [D, 2816], BF16), ("s_hq", [T, 2816], BF16),
        ("s_kk", [D, T], BF16), ("s_vv", [D, T], BF16), ("s_ot", [T, D], BF16),
        ("s_wf", [4096, D], BF16), ("s_wp", [D, 4096], BF16), ("s_h1", [T, 4096], BF16),
    ]:
        p.scr[name] = ap(name, shape, dt, "Internal")

    sb = lambda name, shape, dt: nc.alloc_sbuf_tensor(name, shape, dt)
    p.AA = sb("AA", [128, 16384], BF16)
    p.AB = sb("AB", [128, 32768], BF16)
    p.AC = sb("AC", [128, 13312], BF16)
    p.AF = sb("AFX", [128, 4096], F32)
    p.SM = sb("SMB", [128, 5120], BF16)
    p.xtm = [sb(f"xtm{i}", [128, D], F32) for i in range(2)]
    p.otm = [sb(f"otm{i}", [128, D], BF16) for i in range(2)]
    p.st48 = sb("st48", [128, 48], F32)
    p.agg2 = sb("agg2", [128, 2], F32)
    p.sc1 = sb("sc1", [128, 8], F32)
    p.lng = sb("lng", [128, D], BF16)
    p.lnb = sb("lnb", [128, D], BF16)
    p.grms = sb("grms", [128, D], BF16)
    p.sm32 = sb("sm32", [128, 1024], F32)
    p.bfm = sb("bfm", [4, 4096], BF16)
    p.cst = {}
    for n, sh, dt in [("c_ident", [L, L], BF16), ("c_identf", [L, L], F32),
                     ("c_maskb", [L, L], F32), ("c_trilS", [L, L], BF16),
                     ("c_triuI", [L, L], BF16), ("c_ones_row", [1, L], BF16),
                     ("c_sel8", [8, 512], BF16), ("c_oneh", [8, 1024], BF16)]:
        p.cst[n] = sb("s" + n, sh, dt)
    p.pb = [nc.alloc_psum_tensor(f"pb{i}", [128, 512], F32) for i in range(7)]
    p.pbt = nc.alloc_psum_tensor("pbt", [128, 1024], BF16)

    from contextlib import ExitStack
    ctx = ExitStack()
    p.dW = Ch(ctx, nc, "dW")
    p.dX = Ch(ctx, nc, "dX")
    p.dS = Ch(ctx, nc, "dS")
    p.cT = Ch(ctx, nc, "cT")
    p.cV = Ch(ctx, nc, "cV")
    p.cS = Ch(ctx, nc, "cS")
    blk = ctx.enter_context(nc.Block())

    @blk.sync
    def _(sync):
        _emit(p)

    ctx.close()
    return nc


def _emit(p):
    _load_consts(p)
    _glue(p, first=True)
    if _DEV_STOP == "ln0":
        _final(p); return
    _l0_attn(p)
    _glue(p)
    if _DEV_STOP == "l0a":
        _final(p); return
    _l0_ffn(p)
    _glue(p)
    if _DEV_STOP == "l0f":
        _final(p); return
    _l1_attn(p)
    _glue(p, inner_ln=True)
    if _DEV_STOP == "l1a":
        _final(p); return
    _l1_ffn(p)
    _glue(p, last=True)
    _final(p)


def _final(p):
    nc = p.nc
    if _DEV_DBG is not None and _DEV_DBG in p.scr:
        s = p.scr[_DEV_DBG]
        rows, cols = s.tensor.shape if hasattr(s, 'tensor') else (None, None)
        sh = list(p.scr[_DEV_DBG].shape)
        stage = p.SM[0:128, 0:4096]
        for r0 in range(0, sh[0], 128):
            p.cV.wait(nc.sync)
            p.dX.incd(nc.sync.dma_start(stage[:, 0:sh[1]], p.scr[_DEV_DBG][r0:r0 + 128, :]))
            p.dX.wait(nc.vector)
            p.cV.inc(nc.vector.tensor_copy(p.AF[:, 0:sh[1]], stage[:, 0:sh[1]]))
            p.cV.wait(nc.gpsimd)
            p.dW.incd(nc.gpsimd.dma_start(p.dbg[r0:r0 + 128, 0:sh[1]], p.AF[:, 0:sh[1]]))
            p.dW.wait(nc.vector)
    if _DEV_STOP is not None:
        src = p.scr["xres"]
        for tt in range(NT):
            sl = slice(tt * 128, (tt + 1) * 128)
            b = p.xtm[tt % 2]
            p.dW.wait(nc.sync)
            p.dX.incd(nc.sync.dma_start(b[:], src[sl, :]))
            p.dX.wait(nc.gpsimd)
            p.dW.incd(nc.gpsimd.dma_start(p.out[sl, :], b[:]))
    p.dW.wait(nc.gpsimd)
    p.dX.wait(nc.gpsimd)
    p.dS.wait(nc.gpsimd)
    p.cV.wait(nc.gpsimd)
    p.cS.wait(nc.gpsimd)
    p.cT.wait(nc.gpsimd)


def _load_consts(p):
    nc = p.nc
    for n in p.cst:
        p.dX.incd(nc.sync.dma_start(p.cst[n][:], p.C[n]))
    s_lg = p.SM[0:1, 0:D]
    s_lb = p.SM[0:1, D:2 * D]
    s_gr = p.SM[0:1, 2 * D:2 * D + 256]
    p.dW.incd(nc.gpsimd.dma_start(s_lg, p.W["L1_attn_ln_g"][None, :]))
    p.dW.incd(nc.gpsimd.dma_start(s_lb, p.W["L1_attn_ln_b"][None, :]))
    p.dW.incd(nc.gpsimd.dma_start(s_gr, p.W["L1_attn_g_rms"][None, :]))
    p.dX.wait(nc.tensor)
    p.dW.wait(nc.tensor)
    orow = p.cst["c_ones_row"]
    for j, src in enumerate([s_lg, s_lb]):
        for c in range(2):
            p.cT.inc(nc.tensor.matmul(p.pb[2 * j + c][:, :], orow[0:1, :],
                                      src[0:1, c * 512:(c + 1) * 512], start=True, stop=True))
    p.cT.inc(nc.tensor.matmul(p.pb[4][:, 0:256], orow[0:1, :], s_gr[0:1, :], start=True, stop=True))
    p.cT.wait(nc.vector)
    for j, dst in enumerate([p.lng, p.lnb]):
        for c in range(2):
            p.cV.inc(nc.vector.tensor_copy(dst[:, c * 512:(c + 1) * 512], p.pb[2 * j + c][:, :]))
    for rep in range(4):
        p.cV.inc(nc.vector.tensor_copy(p.grms[:, rep * 256:(rep + 1) * 256], p.pb[4][:, 0:256]))
    p.cV.wait(nc.tensor)


def _ln_tile(p, src, xl_out, eps=1e-5, sub_mu=True):
    nc = p.nc
    vec, sc = nc.vector, nc.scalar
    N = src.shape[-1]
    scrap = p.AF[0:128, 2 * D:3 * D]
    c = p.sc1
    p.cV.inc(vec.reduce_sum(c[:, 0:1], src, mybir.AxisListType.X))
    p.cV.wait(sc)
    p.cS.inc(sc.activation(scrap[:, 0:N], src, AF.Square, accum_out=c[:, 1:2]))
    p.cS.wait(vec)
    p.cV.inc(vec.tensor_scalar(c[:, 2:3], c[:, 0:1], 1.0 / N, None, ALU.mult))
    p.cV.wait(sc)
    p.cS.inc(sc.activation(c[:, 3:4], c[:, 2:3], AF.Square))
    p.cS.wait(vec)
    vec.scalar_tensor_tensor(c[:, 4:5], c[:, 1:2], 1.0 / N, c[:, 3:4],
                             ALU.mult, ALU.subtract)
    vec.drain()
    vec.tensor_scalar(c[:, 5:6], c[:, 4:5], eps, None, ALU.add)
    vec.drain()
    p.cV.inc(vec.reciprocal(c[:, 6:7], c[:, 5:6]))
    p.cV.wait(sc)
    p.cS.inc(sc.activation(c[:, 7:8], c[:, 6:7], AF.Sqrt))
    p.cS.wait(vec)
    vec.tensor_scalar(xl_out, src, c[:, 2:3], c[:, 7:8], ALU.subtract, ALU.mult)


def _glue(p, first=False, last=False, inner_ln=False):
    nc = p.nc
    vec, sc, gp = nc.vector, nc.scalar, nc.gpsimd
    xin = p.x0 if first else p.scr["xres"]
    oc = p.scr["oc"]
    xl2 = [p.SM[0:128, 3 * D:4 * D], p.SM[0:128, 4 * D:5 * D]]
    yb = p.AF[0:128, 0:D]
    zt = p.AF[0:128, D:2 * D]
    p.dW.wait(nc.scalar)
    p.dW.wait(nc.sync)
    vneed = [p.cV.n, p.cV.n]
    wneed = [p.dW.n, p.dW.n]
    for tt in range(NT):
        sl = slice(tt * 128, (tt + 1) * 128)
        b = tt % 2
        xb, ob, xl = p.xtm[b], p.otm[b], xl2[b]
        nc.sync.wait_ge(p.cV.s, vneed[b])
        nc.sync.wait_ge(p.dW.s, wneed[b])
        p.dX.incd(nc.sync.dma_start(xb[:], xin[sl, :]))
        if not first:
            nc.scalar.wait_ge(p.cV.s, vneed[b])
            nc.scalar.wait_ge(p.dW.s, wneed[b])
            p.dS.incd(nc.scalar.dma_start(ob[:], oc[:, sl], transpose=True))
        if inner_ln:
            nc.scalar.wait_ge(p.cV.s, vneed[b])
            nc.scalar.wait_ge(p.dW.s, wneed[b])
            p.dS.incd(nc.scalar.dma_start(xl, p.scr["xln"][sl, :]))
        p.dX.wait(vec)
        if not first:
            p.dS.wait(vec)
            if inner_ln:
                vec.tensor_tensor(yb, ob[:], xl, ALU.add)
                _ln_tile(p, yb, zt)
                vec.tensor_tensor(zt, zt, p.lng[:], ALU.mult)
                vec.tensor_tensor(zt, zt, p.lnb[:], ALU.add)
                vec.tensor_tensor(xb[:], xb[:], zt, ALU.add)
            else:
                vec.tensor_tensor(xb[:], xb[:], ob[:], ALU.add)
        if not last:
            _ln_tile(p, xb[:], xl)
        p.cV.inc(vec.engine_nop())
        vneed[b] = p.cV.n
        p.cV.wait(gp)
        if not last:
            p.dW.incd(gp.dma_start(p.scr["xln"][sl, :], xl))
        dst = p.out if last else p.scr["xres"]
        p.dW.incd(gp.dma_start(dst[sl, :], xb[:]))
        wneed[b] = p.dW.n
    p.dW.wait(nc.sync)
    p.dW.wait(nc.scalar)
    p.dW.wait(nc.vector)
    p.dW.wait(nc.tensor)


def _load_xlf(p, half, src=None, nrow=D):
    nc = p.nc
    src = src if src is not None else p.scr["xln"]
    t0 = half * 2048
    nc.sync.wait_ge(p.cT.s, p.cT.n)
    for i in range(nrow // 128):
        p.dX.incd(nc.sync.dma_start(p.AA[:, i * 2048:(i + 1) * 2048],
                                    src[t0:t0 + 2048, i * 128:(i + 1) * 128], transpose=True))


def _l0_attn(p):
    nc = p.nc
    pb, pbt = p.pb, p.pbt
    vec, sc, te, gp, sy = nc.vector, nc.scalar, nc.tensor, nc.gpsimd, nc.sync
    dW, dX, cT, cV, cS = p.dW, p.dX, p.cT, p.cV, p.cS
    SM, sm32, AFX = p.SM, p.sm32, p.AF

    db = sm32[0:8, 0:1]
    al = sm32[0:8, 1:2]
    dpc = sm32[0:8, 2:3]
    expa = sm32[0:8, 3:4]
    negA = sm32[0:8, 4:5]
    ones8 = sm32[0:8, 8:136]
    dpcols = sm32[0:128, 136:140]
    Wc = sm32[0:8, 144:272]
    WcT = sm32[0:128, 272:280]
    decT = sm32[0:128, 280:288]
    Dmf = sm32[0:128, 288:416]
    dAL = sm32[0:64, 416:424]
    expw_f = sm32[0:8, 424:552]
    delta = AFX[0:8, 0:2048]
    stF = AFX[0:64, 2048:2560]
    wlo_f = sm32[0:8, 680:808]
    dec_f = sm32[0:8, 552:680]
    w_chunk = sm32[0:8, 808:936]
    xdst = [SM[0:128, j * 512:(j + 1) * 512] for j in range(4)]
    deltab_c = SM[0:8, 2048:2560]
    xdtm = SM[0:128, 0:512]
    CBT = SM[0:128, 512:640]
    Btm = SM[0:128, 640:704]
    GT = SM[0:128, 704:832]
    UT = SM[0:128, 832:960]
    Xdec = SM[0:128, 960:1024]
    ytile = [SM[0:128, 1024 + 128 * j:1152 + 128 * j] for j in range(2)]
    Csc = SM[0:64, 1280:1408]
    whi = SM[0:8, 1408:1536]
    wlo = SM[0:8, 1536:1664]
    expw_b = SM[0:8, 1664:1792]
    stB = p.AC[0:64, 5184:5696]
    dpb = p.AC[0:8, 5696:5697]
    oneh = p.cst["c_oneh"]

    dW.incd(gp.dma_start(db, p.W["L0_attn_dt_bias"][:, None]))
    dW.incd(gp.dma_start(al, p.W["L0_attn_A_log"][:, None]))
    dW.incd(gp.dma_start(dpc, p.W["L0_attn_D_param"][:, None]))
    WinT = [p.AC[0:128, i * 648:(i + 1) * 648] for i in range(8)]
    for i in range(8):
        dW.incd(gp.dma_start(WinT[i], p.W["L0_attn_Win"][i * 128:(i + 1) * 128, :]))

    cV.inc(vec.memset(stF, 0.0))
    cV.inc(vec.memset(stB, 0.0))
    cV.inc(vec.memset(ones8, 1.0))
    dW.wait(sc)
    cS.inc(sc.activation(expa, al, AF.Exp))
    cS.wait(vec)
    cV.inc(vec.tensor_scalar(negA, expa, -1.0, None, ALU.mult))
    cV.inc(vec.tensor_copy(dpb, dpc))
    cV.wait(te)
    dX.wait(te)
    for j in range(4):
        cT.inc(te.matmul(pb[6][:, j:j + 1], p.cst["c_sel8"][:, j * 128:(j + 1) * 128],
                         dpb, start=True, stop=True))
    cT.wait(vec)
    cV.inc(vec.tensor_copy(dpcols, pb[6][:, 0:4]))
    cV.wait(te)

    xd_store = [dW.n] * 4
    yst = [dW.n, dW.n]
    for half in range(2):
        _load_xlf(p, half)
        XLF = [p.AA[0:128, i * 2048:(i + 1) * 2048] for i in range(8)]
        Bt = p.AB[0:64, 0:2048]
        Ct = p.AB[0:64, 2048:4096]
        Xa = [p.AB[0:128, (2 + j) * 2048:(3 + j) * 2048] for j in range(4)]

        for cc in range(4):
            csl = slice(cc * 512, (cc + 1) * 512)
            g0 = half * 2048 + cc * 512
            tsl = slice(g0, g0 + 512)
            dX.wait(te)
            dW.wait(te)
            cV.wait(te)
            cS.wait(te)
            for kt in range(8):
                cT.inc(te.matmul(pb[0][0:64, :], WinT[kt][:, 0:64], XLF[kt][:, csl],
                                 start=(kt == 0), stop=(kt == 7)))
            for kt in range(8):
                cT.inc(te.matmul(pb[6][0:64, :], WinT[kt][:, 64:128], XLF[kt][:, csl],
                                 start=(kt == 0), stop=(kt == 7)))
            for kt in range(8):
                cT.inc(te.matmul(pb[1][0:8, :], WinT[kt][:, 128:136], XLF[kt][:, csl],
                                 start=(kt == 0), stop=(kt == 7)))
            for j in range(4):
                for kt in range(8):
                    cT.inc(te.matmul(pb[2 + j][:, :], WinT[kt][:, 136 + 128 * j:264 + 128 * j],
                                     XLF[kt][:, csl], start=(kt == 0), stop=(kt == 7)))
            cT.wait(vec)
            cT.wait(sc)
            cV.inc(vec.tensor_copy(Bt[:, csl], pb[0][0:64, :]))
            cV.inc(vec.tensor_copy(Ct[:, csl], pb[6][0:64, :]))
            cS.inc(sc.activation(delta[:, csl], pb[1][0:8, :], AF.Exp, bias=db))
            cS.inc(sc.activation(delta[:, csl], delta[:, csl], AF.Ln, bias=1.0))
            cS.wait(vec)
            cV.inc(vec.tensor_copy(deltab_c, delta[:, csl]))
            for j in range(4):
                cV.inc(vec.tensor_copy(Xa[j][:, csl], pb[2 + j][:, :]))
            cV.wait(gp)
            for j in range(4):
                dW.incd(gp.dma_start(p.scr["s_xx"][j * 128:(j + 1) * 128, tsl], Xa[j][:, csl]))
            cV.wait(te)
            for j in range(4):
                cT.inc(te.matmul(pb[6][:, :], p.cst["c_sel8"][:, j * 128:(j + 1) * 128],
                                 deltab_c, start=True, stop=True))
                cT.wait(vec)
                vec.wait_ge(dW.s, xd_store[j])
                cV.inc(vec.tensor_tensor(xdst[j], Xa[j][:, csl], pb[6][:, :], ALU.mult))
                cV.wait(te)
                cV.wait(gp)
                dW.incd(gp.dma_start(p.scr["s_xt"][j * 128:(j + 1) * 128, tsl], xdst[j]))
                xd_store[j] = dW.n
        sy.wait_ge(dW.s, dW.n)
        vec.wait_ge(dW.s, dW.n)
        sc.wait_ge(dW.s, dW.n)
        for scl in range(16):
            t0 = (half * 16 + scl) * 128
            ch_sl = slice(scl * 128, (scl + 1) * 128)
            cS.wait(vec)
            cV.inc(vec.tensor_scalar(w_chunk, delta[:, scl * 128:(scl + 1) * 128],
                                     negA, None, ALU.mult))
            cV.inc(vec.tensor_tensor_scan(Wc, ones8, w_chunk, 0.0, ALU.mult, ALU.add))
            cV.inc(vec.tensor_copy(whi, Wc))
            cV.inc(vec.tensor_tensor(wlo_f, Wc, whi, ALU.subtract))
            cV.inc(vec.tensor_copy(wlo, wlo_f))
            cV.inc(vec.tensor_scalar(dec_f, Wc, Wc[:, 127:128], None, ALU.subtract))
            cV.wait(sc)
            cS.inc(sc.activation(expw_f, Wc, AF.Exp))
            cS.inc(sc.activation(dec_f, dec_f, AF.Exp, scale=-1.0))
            cS.wait(vec)
            cV.inc(vec.tensor_copy(expw_b, expw_f))
            cV.wait(te)
            cS.wait(te)
            cT.inc(te.transpose(pb[5][:, 0:8], Wc, p.cst["c_identf"][0:8, 0:8]))
            cT.inc(te.transpose(pb[5][:, 8:16], dec_f, p.cst["c_identf"][0:8, 0:8]))
            cT.inc(te.matmul(pb[0][:, 0:128], Bt[:, ch_sl], Ct[:, ch_sl],
                             start=True, stop=True))
            cT.inc(te.transpose(pbt[:, 0:64], Bt[:, ch_sl], p.cst["c_ident"][0:64, 0:64]))
            cT.wait(vec)
            cV.inc(vec.tensor_copy(WcT, pb[5][:, 0:8]))
            cV.inc(vec.tensor_copy(decT, pb[5][:, 8:16]))
            cV.inc(vec.tensor_copy(CBT, pb[0][:, 0:128]))
            cV.inc(vec.tensor_copy(Btm, pbt[:, 0:64]))
            sy.wait_ge(cV.s, cV.n)
            sy.wait_ge(cT.s, cT.n)
            dX.incd(sy.dma_start(xdtm, p.scr["s_xt"][0:512, t0:t0 + 128], transpose=True))
            cV.wait(te)
            dX.wait(te)
            dX.wait(vec)
            tprev = cT.n
            for h in range(8):
                pw = pb[1 + (h % 2)]
                pe = pb[3 + (h % 2)]
                py = pb[5 + ((h // 2) % 2)]
                cV.wait(te)
                cS.wait(te)
                lh = oneh[:, h * 128:(h + 1) * 128]
                cT.inc(te.matmul(pw[:, 0:128], lh, whi, start=True, stop=False))
                cT.inc(te.matmul(pw[:, 0:128], lh, wlo, start=False, stop=True))
                cT.inc(te.matmul(pe[0:64, 0:128], lh[:, 0:64], expw_b, start=True, stop=True))
                cT.wait(vec)
                cV.inc(vec.tensor_scalar(Dmf, pw[:, 0:128], WcT[:, h:h + 1],
                                         None, ALU.subtract))
                cV.inc(vec.tensor_tensor(Dmf, Dmf, p.cst["c_maskb"][:, :], ALU.add))
                cV.wait(sc)
                cS.inc(sc.activation(GT, Dmf, AF.Exp))
                vec.wait_ge(cT.s, tprev)
                cV.inc(vec.tensor_tensor(Csc, Ct[:, ch_sl], pe[0:64, 0:128], ALU.mult))
                cV.inc(vec.tensor_copy(dAL[:, h:h + 1], pe[0:64, 127:128]))
                cS.wait(vec)
                cV.inc(vec.tensor_tensor(UT, GT, CBT, ALU.mult))
                if _DEV_DBG == "probe6" and half == 0 and scl == 12 and h == 6:
                    cV.wait(gp)
                    dW.incd(gp.dma_start(p.dbg[2712:2776, 0:512], stB))
                    dW.incd(gp.dma_start(p.dbg[2776:2840, 0:128], Csc))
                    dW.wait(vec)
                if False and _DEV_DBG == "probe6" and half == 0 and scl == 12 and h == 6:
                    cV.wait(gp)
                    cS.wait(gp)
                    dW.incd(gp.dma_start(p.dbg[2048:2176, 0:128], Dmf))
                    dW.incd(gp.dma_start(p.dbg[2176:2304, 0:128], GT))
                    dW.incd(gp.dma_start(p.dbg[2304:2432, 0:128], UT))
                    dW.incd(gp.dma_start(p.dbg[2432:2560, 0:128], CBT))
                    dW.incd(gp.dma_start(p.dbg[2560:2568, 0:128], Wc))
                    dW.incd(gp.dma_start(p.dbg[2568:2696, 0:512], xdtm))
                    dW.incd(gp.dma_start(p.dbg[2696:2704, 0:128], whi))
                    dW.incd(gp.dma_start(p.dbg[2704:2712, 0:128], wlo))
                    dW.wait(vec)
                    dW.wait(sc)
                cV.inc(vec.tensor_scalar(Xdec, xdtm[:, 64 * h:64 * (h + 1)],
                                         decT[:, h:h + 1], None, ALU.mult))
                cV.wait(te)
                ro = 64 * (h % 2)
                cT.inc(te.matmul(py[ro:ro + 64, 0:128], xdtm[:, 64 * h:64 * (h + 1)], UT,
                                 start=True, stop=False))
                cT.inc(te.matmul(py[ro:ro + 64, 0:128], stB[:, 64 * h:64 * (h + 1)], Csc,
                                 start=False, stop=True))
                cT.inc(te.matmul(pb[0][0:64, 128 + 64 * (h % 4):192 + 64 * (h % 4)],
                                 Btm, Xdec, start=True, stop=True))
                tprev = cT.n
                if h % 2 == 1:
                    cT.wait(sc)
                    yt = ytile[(h // 2) % 2]
                    sc.wait_ge(dW.s, yst[(h // 2) % 2])
                    cS.inc(sc.copy(yt, py[:, 0:128]))
                    cS.wait(gp)
                    dW.incd(gp.dma_start(
                        p.scr["s_y0"][128 * (h // 2):128 * (h // 2 + 1), t0:t0 + 128], yt))
                    yst[(h // 2) % 2] = dW.n
                    if _DEV_DBG == "probe6" and half == 0 and scl == 12 and h == 7:
                        dW.incd(gp.dma_start(p.dbg[2840:2968, 0:128], yt))
                        dW.incd(gp.dma_start(p.dbg[2968:3096, 0:64], Xdec))
                        dW.incd(gp.dma_start(p.dbg[3096:3224, 0:8], decT))
                        dW.wait(sc)
                if h % 4 == 3:
                    cT.wait(vec)
                    for hh in range(h - 3, h + 1):
                        cV.inc(vec.scalar_tensor_tensor(
                            stF[:, 64 * hh:64 * (hh + 1)], stF[:, 64 * hh:64 * (hh + 1)],
                            dAL[:, hh:hh + 1],
                            pb[0][0:64, 128 + 64 * (hh % 4):192 + 64 * (hh % 4)],
                            ALU.mult, ALU.add))
            cV.inc(vec.tensor_copy(stB, stF))
            cV.wait(te)

    WzSB = [p.AB[0:128, i * 2048:(i + 1) * 2048] for i in range(8)]
    WzgSB = [p.AB[0:128, 16384 + i * 512:16384 + (i + 1) * 512] for i in range(16)]
    WoutSB = [p.AB[0:128, 24576 + i * 1024:24576 + (i + 1) * 1024] for i in range(4)]
    cT.wait(gp)
    cV.wait(gp)
    for i in range(8):
        dW.incd(gp.dma_start(WzSB[i], p.W["L0_attn_Wz"][i * 128:(i + 1) * 128, :]))
    for i in range(16):
        dW.incd(gp.dma_start(WzgSB[i], p.W["L0_attn_Wzg"][i * 128:(i + 1) * 128, :]))
    for i in range(4):
        dW.incd(gp.dma_start(WoutSB[i], p.W["L0_attn_Wout"][i * 128:(i + 1) * 128, :]))
    zdw = dW.n
    zsil = [p.AC[0:128, i * 512:(i + 1) * 512] for i in range(16)]
    sg2 = [SM[0:128, 0:512], SM[0:128, 4096:4608]]
    ych = [SM[0:128, 512 + j * 512:1024 + j * 512] for j in range(4)]
    t1b = SM[0:128, 2560:3072]
    otile = [SM[0:128, 3072 + j * 512:3584 + j * 512] for j in range(2)]
    xch = [p.AC[0:128, 10240 + j * 512:10752 + j * 512] for j in range(4)]
    prodt = [p.AC[0:128, 8192 + j * 512:8704 + j * 512] for j in range(4)]
    for half in range(2):
        _load_xlf(p, half)
        XLF = [p.AA[0:128, i * 2048:(i + 1) * 2048] for i in range(8)]
        for cc in range(4):
            csl = slice(cc * 512, (cc + 1) * 512)
            g0 = half * 2048 + cc * 512
            tsl = slice(g0, g0 + 512)
            dX.wait(te)
            dW.wait(te)
            zv = [cV.n, cV.n]
            sgv = [cV.n, cV.n]
            for mz in range(16):
                pz = pb[mz % 2]
                sg = sg2[mz % 2]
                te.wait_ge(cV.s, zv[mz % 2])
                for kt in range(8):
                    cT.inc(te.matmul(pz[:, :], WzSB[kt][:, mz * 128:(mz + 1) * 128],
                                     XLF[kt][:, csl], start=(kt == 0), stop=(kt == 7)))
                cT.wait(sc)
                sc.wait_ge(cV.s, sgv[mz % 2])
                cS.inc(sc.activation(sg, pz[:, :], AF.Sigmoid))
                cS.wait(vec)
                cV.inc(vec.tensor_tensor(zsil[mz], sg, pz[:, :], ALU.mult))
                zv[mz % 2] = cV.n
                sgv[mz % 2] = cV.n
            sy.wait_ge(cV.s, cV.n)
            sy.wait_ge(cT.s, cT.n)
            sy.wait_ge(dW.s, zdw)
            for j in range(4):
                dX.incd(sy.dma_start(ych[j], p.scr["s_y0"][j * 128:(j + 1) * 128, tsl]))
                dX.incd(sy.dma_start(xch[j], p.scr["s_xx"][j * 128:(j + 1) * 128, tsl]))
            if _DEV_DBG == "probez" and half == 0 and cc == 0:
                cV.wait(gp)
                cS.wait(gp)
                for j in range(4):
                    dW.incd(gp.dma_start(p.dbg[2048 + 128 * j:2176 + 128 * j, 0:512], ych[j]))
                    dW.incd(gp.dma_start(p.dbg[2560 + 128 * j:2688 + 128 * j, 0:512], xch[j]))
                for mzd in range(4):
                    dW.incd(gp.dma_start(p.dbg[3072 + 128 * mzd:3200 + 128 * mzd, 0:512], zsil[mzd]))
                dW.wait(vec)
                dW.wait(sy)
            pv = [cV.n, cV.n]
            for mg in range(4):
                pzg = pb[2 + mg % 2]
                te.wait_ge(cV.s, pv[mg % 2])
                for kt2 in range(16):
                    cT.inc(te.matmul(pzg[:, :], WzgSB[kt2][:, mg * 128:(mg + 1) * 128],
                                     zsil[kt2], start=(kt2 == 0), stop=(kt2 == 15)))
                dX.wait(vec)
                cT.wait(vec)
                cV.inc(vec.tensor_scalar(t1b, xch[mg], dpcols[:, mg:mg + 1], None, ALU.mult))
                cV.inc(vec.tensor_tensor(t1b, t1b, ych[mg], ALU.add))
                cV.inc(vec.tensor_tensor(prodt[mg], t1b, pzg[:, :], ALU.mult))
                pv[mg % 2] = cV.n
            ov = [dW.n, dW.n]
            for mo in range(8):
                po = pb[4 + mo % 2]
                cS.wait(te)
                cV.wait(te)
                for kt3 in range(4):
                    cT.inc(te.matmul(po[:, :], WoutSB[kt3][:, mo * 128:(mo + 1) * 128],
                                     prodt[kt3], start=(kt3 == 0), stop=(kt3 == 3)))
                cT.wait(sc)
                sc.wait_ge(dW.s, ov[mo % 2])
                cS.inc(sc.copy(otile[mo % 2], po[:, :]))
                cS.wait(gp)
                dW.incd(gp.dma_start(p.scr["oc"][mo * 128:(mo + 1) * 128, tsl], otile[mo % 2]))
                ov[mo % 2] = dW.n
    dW.wait(sy)


def _bc16(handle, row_pitch, col0, nblk):
    return bass.AP(handle, col0, [[row_pitch, 128], [1, nblk], [0, 16]])

def _blk16(handle, row_pitch, col0, nblk):
    return bass.AP(handle, col0, [[row_pitch, 128], [16, nblk], [1, 16]])


def _quant_fp4(p, W, bi, f32_in=False, sgn_wait=None):
    nc = p.nc
    vec, sc = nc.vector, nc.scalar
    nb = W // 16
    AC, AFX, SM, sm32 = p.AC, p.AF, p.SM, p.sm32
    if f32_in:
        xt = AFX[0:128, bi * 1024:bi * 1024 + W]
        xv = _blk16(AFX, 4096, bi * 1024, nb)
        an2 = AFX[0:128, 2048 + bi * 1024:2048 + bi * 1024 + W]
    else:
        xt = AC[0:128, bi * 1408:bi * 1408 + W]
        xv = _blk16(AC, 13312, bi * 1408, nb)
        an2 = AFX[0:128, bi * 2048:bi * 2048 + W]
    sgn = AC[0:128, 8448 + bi * 1408:8448 + bi * 1408 + W]
    tb = AC[0:128, 5632 + bi * 1408:5632 + bi * 1408 + W]
    axq = AC[0:128, 2816 + bi * 1408:2816 + bi * 1408 + W]
    q2 = SM[0:128, bi * 1408:bi * 1408 + W]
    m = sm32[0:128, bi * 88:bi * 88 + nb]
    rcp = sm32[0:128, 256 + bi * 88:256 + bi * 88 + nb]
    rcp36 = sm32[0:128, 432 + bi * 88:432 + bi * 88 + nb]
    shb = SM[0:128, 2816 + bi * 96:2816 + bi * 96 + nb]
    if sgn_wait is not None:
        sc.wait_ge(p.cV.s, sgn_wait)
    p.cS.inc(sc.sign(sgn, xt))
    p.cS.inc(sc.activation(an2, xt, AF.Square))
    signm = p.cS.n
    vec.tensor_reduce(m, xv, mybir.AxisListType.X, ALU.max, apply_absolute_value=True)
    vec.drain()
    vec.tensor_scalar(m, m, 1e-12, None, ALU.max)
    vec.drain()
    vec.reciprocal(rcp, m)
    vec.tensor_scalar(shb, m, 1.0 / 12.0, None, ALU.mult)
    vec.drain()
    vec.tensor_scalar(rcp, rcp, 6.0, None, ALU.mult)
    vec.drain()
    vec.tensor_tensor(rcp36, rcp, rcp, ALU.mult)
    vec.wait_ge(p.cS.s, signm)
    vec.tensor_tensor(an2, an2, _bc16(sm32, 1024, 432 + bi * 88, nb), ALU.mult)
    vec.tensor_scalar(q2, an2, 0.0625, 1.0, ALU.is_gt, ALU.mult)
    for th2, wgt in [(0.5625, 1.0), (1.5625, 1.0), (3.0625, 1.0),
                     (6.25, 2.0), (12.25, 2.0), (25.0, 4.0)]:
        vec.tensor_scalar(tb, an2, th2, wgt, ALU.is_gt, ALU.mult)
        vec.tensor_tensor(q2, q2, tb, ALU.add)
    anm = p.cV.n + 1
    p.cV.inc(vec.engine_nop())
    vec.wait_ge(p.cS.s, signm)
    vec.tensor_tensor(q2, q2, sgn, ALU.mult)
    p.cV.inc(vec.tensor_tensor(axq, q2, _bc16(SM, 5120, 2816 + bi * 96, nb), ALU.mult))
    return anm


def _l0_ffn(p):
    nc = p.nc
    pb = p.pb
    vec, sc, te, gp, sy = nc.vector, nc.scalar, nc.tensor, nc.gpsimd, nc.sync
    dW, dX, dS, cT, cV, cS = p.dW, p.dX, p.dS, p.cT, p.cV, p.cS
    SM, AC, AFX = p.SM, p.AC, p.AF

    jobs = []
    for tt in range(NT):
        jobs.append((p.scr["xln"][tt * 128:(tt + 1) * 128, :], 1024, "s_xq",
                     (tt * 128, 0), False))
    for wn, sn in [("L0_ffn_w_gate", "s_wqg"), ("L0_ffn_w_up", "s_wqu")]:
        for r in range(22):
            jobs.append((p.W[wn][r * 128:(r + 1) * 128, :], 1024, sn, (r * 128, 0), True))
    for r in range(8):
        for c0, w in [(0, 1024), (1024, 1024), (2048, 768)]:
            jobs.append((p.W["L0_ffn_w_down"][r * 128:(r + 1) * 128, c0:c0 + w], w,
                         "s_wqd", (r * 128, c0), True))
    st2 = [dW.n, dW.n]
    an2m = [cV.n, cV.n]
    sg2m = [cS.n, cS.n]
    ld2 = [dX.n, dX.n]
    vq2 = [cV.n, cV.n]
    prev_path = None
    for idx, (srcap, W, dst, (r0, c0), isw) in enumerate(jobs):
        bi = idx % 2
        if prev_path is not None and isw != prev_path:
            sy.wait_ge(cV.s, cV.n)
            sy.wait_ge(cS.s, cS.n)
        prev_path = isw
        xt = AFX[0:128, bi * 1024:bi * 1024 + W] if isw else AC[0:128, bi * 1408:bi * 1408 + W]
        sy.wait_ge(cV.s, an2m[bi])
        sy.wait_ge(cS.s, sg2m[bi])
        dX.incd(sy.dma_start(xt, srcap))
        ld2[bi] = dX.n
        sc.wait_ge(dX.s, ld2[bi])
        vec.wait_ge(dX.s, ld2[bi])
        vec.wait_ge(dW.s, st2[bi])
        an2m[bi] = _quant_fp4(p, W, bi, f32_in=isw)
        sg2m[bi] = cS.n
        vq2[bi] = cV.n
        cV.wait(gp)
        dW.incd(gp.dma_start(p.scr[dst][r0:r0 + 128, c0:c0 + W],
                             AC[0:128, 2816 + bi * 1408:2816 + bi * 1408 + W]))
        st2[bi] = dW.n

    sgv = [cV.n, cV.n]
    sgb = [AC[0:128, 11264:11776], AC[0:128, 11776:12288]]
    for nh in range(2):
        WG = [p.AB[0:128, kt * 1408:(kt + 1) * 1408] for kt in range(8)]
        WU = [p.AB[0:128, 11264 + kt * 1408:11264 + (kt + 1) * 1408] for kt in range(8)]
        sy.wait_ge(cT.s, cT.n)
        sy.wait_ge(dW.s, dW.n)
        for kt in range(8):
            dX.incd(sy.dma_start(WG[kt], p.scr["s_wqg"][nh * 1408:(nh + 1) * 1408,
                                                        kt * 128:(kt + 1) * 128], transpose=True))
            dX.incd(sy.dma_start(WU[kt], p.scr["s_wqu"][nh * 1408:(nh + 1) * 1408,
                                                        kt * 128:(kt + 1) * 128], transpose=True))
        for half in range(2):
            sy.wait_ge(cT.s, cT.n)
            XQ = [p.AA[0:128, kt * 2048:(kt + 1) * 2048] for kt in range(8)]
            for kt in range(8):
                dX.incd(sy.dma_start(XQ[kt], p.scr["s_xq"][half * 2048:(half + 1) * 2048,
                                                           kt * 128:(kt + 1) * 128], transpose=True))
            dX.wait(te)
            for tl in range(16):
                tt = half * 16 + tl
                bi = tl % 2
                h_row = AC[0:128, bi * 1408:bi * 1408 + 1408]
                hm = 0
                for nci, (nc0, nw) in enumerate([(0, 512), (512, 512), (1024, 384)]):
                    pg = pb[4 + nci % 2]
                    pu = pb[2 + nci % 2]
                    cV.wait(te)
                    cS.wait(te)
                    for kt in range(8):
                        cT.inc(te.matmul(pg[:, 0:nw], XQ[kt][:, tl * 128:(tl + 1) * 128],
                                         WG[kt][:, nc0:nc0 + nw], start=(kt == 0), stop=(kt == 7)))
                    for kt in range(8):
                        cT.inc(te.matmul(pu[:, 0:nw], XQ[kt][:, tl * 128:(tl + 1) * 128],
                                         WU[kt][:, nc0:nc0 + nw], start=(kt == 0), stop=(kt == 7)))
                    cT.wait(sc)
                    sg = sgb[nci % 2]
                    sc.wait_ge(cV.s, sgv[nci % 2])
                    cS.inc(sc.activation(sg[:, 0:nw], pg[:, 0:nw], AF.Sigmoid))
                    cS.wait(vec)
                    if nci == 0:
                        vec.wait_ge(cV.s if False else dW.s, st2[bi])
                        vec.wait_ge(cV.s, an2m[bi]) if False else None
                    vec.tensor_tensor(sg[:, 0:nw], sg[:, 0:nw], pg[:, 0:nw], ALU.mult)
                    p.cV.inc(vec.tensor_tensor(h_row[:, nc0:nc0 + nw], sg[:, 0:nw],
                                               pu[:, 0:nw], ALU.mult))
                    sgv[nci % 2] = cV.n
                hm = cV.n
                an2m[bi] = _quant_fp4(p, 1408, bi, f32_in=False, sgn_wait=hm)
                sg2m[bi] = cS.n
                cV.wait(gp)
                dW.incd(gp.dma_start(p.scr["s_hq"][tt * 128:(tt + 1) * 128,
                                                   nh * 1408:(nh + 1) * 1408],
                                     AC[0:128, 2816 + bi * 1408:2816 + bi * 1408 + 1408]))
                st2[bi] = dW.n
    WD = [p.AB[0:128, kt * 1024:(kt + 1) * 1024] for kt in range(22)]
    sy.wait_ge(cT.s, cT.n)
    sy.wait_ge(dW.s, dW.n)
    for kt in range(22):
        dX.incd(sy.dma_start(WD[kt], p.scr["s_wqd"][:, kt * 128:(kt + 1) * 128], transpose=True))
    ov = [dW.n, dW.n]
    for qt in range(4):
        HQ = []
        sy.wait_ge(cT.s, cT.n)
        for kt in range(22):
            dst = (p.AA[0:128, kt * 1024:(kt + 1) * 1024] if kt < 16
                   else AC[0:128, (kt - 16) * 1024:(kt - 15) * 1024])
            HQ.append(dst)
            dX.incd(sy.dma_start(dst, p.scr["s_hq"][qt * 1024:(qt + 1) * 1024,
                                                    kt * 128:(kt + 1) * 128], transpose=True))
        dX.wait(te)
        for dt in range(8):
            for tch in range(2):
                po = pb[tch % 2]
                cS.wait(te)
                for kt in range(22):
                    cT.inc(te.matmul(po[:, :], WD[kt][:, dt * 128:(dt + 1) * 128],
                                     HQ[kt][:, tch * 512:(tch + 1) * 512],
                                     start=(kt == 0), stop=(kt == 21)))
                cT.wait(sc)
                otile = SM[0:128, 3072 + (tch % 2) * 512:3584 + (tch % 2) * 512]
                sc.wait_ge(dW.s, ov[tch % 2])
                cS.inc(sc.copy(otile, po[:, :]))
                cS.wait(gp)
                dW.incd(gp.dma_start(
                    p.scr["oc"][dt * 128:(dt + 1) * 128,
                                qt * 1024 + tch * 512:qt * 1024 + (tch + 1) * 512], otile))
                ov[tch % 2] = dW.n
    dW.wait(sy)


def _l1_attn(p):
    nc = p.nc
    pb, pbt = p.pb, p.pbt
    vec, sc, te, gp, sy = nc.vector, nc.scalar, nc.tensor, nc.gpsimd, nc.sync
    dW, dX, dS, cT, cV, cS = p.dW, p.dX, p.dS, p.cT, p.cV, p.cS
    SM, sm32, AC, AA, AB, AFX = p.SM, p.sm32, p.AC, p.AA, p.AB, p.AF

    WSB = [AC[0:128, kt * 1024:(kt + 1) * 1024] for kt in range(8)]
    csrc2 = [AC[0:128, 8192 + j * 515:8707 + j * 515] for j in range(2)]
    convT = lambda wi, mt: sm32[0:128, (wi * 8 + mt) * 4:(wi * 8 + mt) * 4 + 4]
    WbSB = [AC[0:128, 9318 + kt * 4:9322 + kt * 4] for kt in range(8)]
    halo = lambda wi, mt: AC[0:128, 9350 + (wi * 8 + mt) * 3:9353 + (wi * 8 + mt) * 3]
    onesc = AC[0:128, 9422:9423]
    acc = SM[0:128, 0:512]
    tmp = SM[0:128, 512:1024]
    sg2 = [SM[0:128, 1024 + j * 512:1536 + j * 512] for j in range(2)]
    stash2 = [SM[0:128, 2048 + j * 512:2560 + j * 512] for j in range(2)]
    q2b2 = [SM[0:128, 3072 + j * 512:3584 + j * 512] for j in range(2)]
    rsqb = SM[0:1, 4096:4608]
    nbetaT = sm32[0:128, 384:512]
    betaT = sm32[0:128, 256:384]
    cvst = SM[0:4, 0:3072]
    bfm = p.bfm
    wnames = ["L1_attn_Wq", "L1_attn_Wk", "L1_attn_Wv"]
    cnames = ["L1_attn_conv_q", "L1_attn_conv_k", "L1_attn_conv_v"]

    for wi in range(3):
        dW.incd(gp.dma_start(cvst[:, wi * 1024:(wi + 1) * 1024], p.W[cnames[wi]]))
    for kt in range(8):
        dW.incd(gp.dma_start(WbSB[kt], p.W["L1_attn_Wb"][kt * 128:(kt + 1) * 128, :]))
    cV.inc(vec.memset(AC[0:128, 9350:9422], 0.0))
    cV.inc(vec.memset(onesc, 1.0))
    dW.wait(te)
    cV.wait(te)
    cS.wait(te)
    for wi in range(3):
        for mt in range(8):
            j4 = (wi * 8 + mt) * 4
            cT.inc(te.transpose(pbt[:, j4:j4 + 4],
                                cvst[0:4, wi * 1024 + mt * 128:wi * 1024 + (mt + 1) * 128],
                                p.cst["c_ident"][0:4, 0:4]))
    cT.wait(vec)
    cV.inc(vec.tensor_copy(sm32[0:128, 0:96], pbt[:, 0:96]))
    cvev = cV.n

    qst = [dW.n] * 2
    sgv = [cV.n] * 2
    csv = [cV.n] * 2
    pzm = [cS.n] * 2
    q2m = [cT.n] * 2
    accm = cS.n
    p2m = cV.n
    p3m = cV.n
    bcm = cT.n
    p4m = cS.n
    for half in range(2):
        _load_xlf(p, half)
        XLF = [AA[0:128, i * 2048:(i + 1) * 2048] for i in range(8)]
        for wi in range(3):
            wmark = cT.n
            for kt in range(8):
                gp.wait_ge(cT.s, wmark)
                dW.incd(gp.dma_start(WSB[kt], p.W[wnames[wi]][kt * 128:(kt + 1) * 128, :]))
            for cc in range(4):
                csl = slice(cc * 512, (cc + 1) * 512)
                g0 = half * 2048 + cc * 512
                dX.wait(te)
                dW.wait(te)
                for h4 in range(4):
                    for mi in range(2):
                        mt = 2 * h4 + mi
                        cb = mt % 2
                        pz = pb[mt % 2]
                        cs = csrc2[cb]
                        te.wait_ge(cS.s, pzm[mt % 2])
                        for kt in range(8):
                            cT.inc(te.matmul(pz[:, :], WSB[kt][:, mt * 128:(mt + 1) * 128],
                                             XLF[kt][:, csl], start=(kt == 0), stop=(kt == 7)))
                        sc.wait_ge(cV.s, csv[cb])
                        cS.inc(sc.copy(cs[:, 0:3], halo(wi, mt)))
                        cT.wait(sc)
                        cS.inc(sc.copy(cs[:, 3:515], pz[:, :]))
                        pzm[mt % 2] = cS.n
                        cS.wait(vec)
                        wc = convT(wi, mt)
                        vec.tensor_scalar(acc, cs[:, 0:512], wc[:, 0:1], None, ALU.mult)
                        for j in (1, 2, 3):
                            vec.tensor_scalar(tmp, cs[:, j:j + 512], wc[:, j:j + 1],
                                              None, ALU.mult)
                            vec.tensor_tensor(acc, acc, tmp, ALU.add)
                        cV.inc(vec.tensor_copy(halo(wi, mt), cs[:, 512:515]))
                        csv[cb] = cV.n
                        sg = sg2[mi]
                        cV.wait(sc)
                        cS.inc(sc.activation(sg, acc, AF.Sigmoid))
                        accm = cS.n
                        cS.wait(vec)
                        st = stash2[mi]
                        vec.wait_ge(dW.s, qst[mi])
                        cV.inc(vec.tensor_tensor(st, sg, acc, ALU.mult))
                        sgv[mi] = cV.n
                        if wi < 2:
                            cV.wait(sc)
                            sc.wait_ge(cT.s, q2m[mi])
                            cS.inc(sc.activation(q2b2[mi], st, AF.Square))
                            cS.wait(te)
                            te.wait_ge(cV.s, p2m)
                            cT.inc(te.matmul(pb[2][0:1, 0:512], onesc, q2b2[mi],
                                             start=(mi == 0), stop=(mi == 1)))
                            q2m[mi] = cT.n
                        else:
                            cV.wait(gp)
                            dW.incd(gp.dma_start(
                                p.scr["s_vv"][mt * 128:(mt + 1) * 128, g0:g0 + 512], st))
                            qst[mi] = dW.n
                    if wi < 2:
                        cT.wait(vec)
                        cV.inc(vec.tensor_scalar(AFX[0:1, 0:512], pb[2][0:1, 0:512],
                                                 1e-6, None, ALU.add))
                        p2m = cV.n
                        vec.drain()
                        vec.reciprocal(AFX[0:1, 512:1024], AFX[0:1, 0:512])
                        cV.inc(vec.engine_nop())
                        cV.wait(sc)
                        sc.wait_ge(cT.s, bcm)
                        cS.inc(sc.activation(rsqb, AFX[0:1, 512:1024], AF.Sqrt))
                        cS.wait(te)
                        te.wait_ge(cV.s, p3m)
                        cT.inc(te.matmul(pb[3][:, :], p.cst["c_ones_row"][0:1, :], rsqb,
                                         start=True, stop=True))
                        bcm = cT.n
                        cT.wait(vec)
                        for mi in range(2):
                            mt = 2 * h4 + mi
                            st = stash2[mi]
                            if wi == 0:
                                cV.inc(vec.tensor_tensor(
                                    AB[0:128, mt * 4096 + g0:mt * 4096 + g0 + 512],
                                    st, pb[3][:, :], ALU.mult))
                            else:
                                cV.inc(vec.tensor_tensor(st, st, pb[3][:, :], ALU.mult))
                                cV.wait(gp)
                                dW.incd(gp.dma_start(
                                    p.scr["s_kk"][mt * 128:(mt + 1) * 128, g0:g0 + 512], st))
                                qst[mi] = dW.n
                        p3m = cV.n
        for cc in range(4):
            csl = slice(cc * 512, (cc + 1) * 512)
            g0 = half * 2048 + cc * 512
            te.wait_ge(cS.s, p4m)
            for kt in range(8):
                cT.inc(te.matmul(pb[4][0:4, :], WbSB[kt], XLF[kt][:, csl],
                                 start=(kt == 0), stop=(kt == 7)))
            cT.wait(sc)
            cS.inc(sc.activation(bfm[0:4, g0:g0 + 512], pb[4][0:4, :], AF.Sigmoid))
            p4m = cS.n
    cS.wait(te)
    cV.wait(te)
    for sco in range(32):
        cT.inc(te.transpose(pbt[:, 4 * sco:4 * sco + 4], bfm[0:4, sco * 128:(sco + 1) * 128],
                            p.cst["c_ident"][0:4, 0:4]))
    cT.wait(vec)
    cV.inc(vec.tensor_copy(betaT, pbt[:, 0:128]))
    vec.tensor_scalar(nbetaT, betaT, -1.0, None, ALU.mult)
    cV.inc(vec.engine_nop())

    S_f = lambda h, kt: AFX[0:128, (2 * h + kt) * 256:(2 * h + kt + 1) * 256]
    S_b = lambda h, kt: AC[0:128, (2 * h + kt) * 256:(2 * h + kt + 1) * 256]
    ktm2 = [AC[0:128, 2048 + j * 1024:3072 + j * 1024] for j in range(2)]
    vtm2 = [AC[0:128, 4096 + j * 1024:5120 + j * 1024] for j in range(2)]
    Dl = [[SM[0:128, s * 512 + j * 256:s * 512 + (j + 1) * 256] for j in range(2)]
          for s in range(2)]
    Cp = [[(SM[0:128, 1024 + s * 512 + j * 256:1152 + s * 512 + j * 256],
            SM[0:128, 1152 + s * 512 + j * 256:1280 + s * 512 + j * 256])
           for j in range(2)] for s in range(2)]
    Cpair = [[SM[0:128, 1024 + s * 512 + j * 256:1280 + s * 512 + j * 256]
              for j in range(2)] for s in range(2)]
    PTb = [SM[0:128, 2048 + s * 128:2176 + s * 128] for s in range(2)]
    scrap2 = [SM[0:128, 2304 + s * 256:2560 + s * 256] for s in range(2)]
    Rv2 = [SM[0:128, 2816 + s * 256:3072 + s * 256] for s in range(2)]
    och = SM[0:128, 3328:4352]
    dW.wait(sy)
    dW.wait(te)
    cV.wait(te)
    cS.wait(te)
    cV.inc(vec.memset(AFX[0:128, 0:2048], 0.0))
    cV.inc(vec.memset(AC[0:128, 0:2048], 0.0))
    mk = {"p0m": [cV.n] * 2, "p4m2": [cV.n] * 2, "p5mS": [cS.n] * 2,
          "p5mV": [cV.n] * 2, "p1mV": [cV.n] * 2, "p1mS": [cS.n] * 2,
          "p3ev": [cV.n] * 2, "pbtm": [cS.n] * 2, "ost": dW.n}
    sbm = [cS.n] * 4
    kvmark = [cT.n, cT.n]

    def head_prog(h, s, sci, b, lsl):
        kf = [AA[0:128, (2 * h + kt) * 2048 + lsl:(2 * h + kt) * 2048 + lsl + 128]
              for kt in range(2)]
        qf = [AB[0:128, (2 * h + kt) * 4096 + sci * 128:(2 * h + kt) * 4096 + sci * 128 + 128]
              for kt in range(2)]
        bcol = betaT[:, 4 * sci + h:4 * sci + h + 1]
        nbcol = nbetaT[:, 4 * sci + h:4 * sci + h + 1]
        bank = pb[0] if s == 0 else pb[4]
        pKK = bank[:, 0:128]
        pPT = bank[:, 128:256]
        pR = bank[:, 256:512]
        pO = bank[:, 256:512]
        pA = (pb[1] if s == 0 else pb[2])[:, 0:256]
        pSq = (pb[3] if s == 0 else pb[6])[:, 0:256]
        pC1T = pbt[:, s * 128:(s + 1) * 128]
        te.wait_ge(cV.s, mk["p0m"][s])
        te.wait_ge(cV.s, mk["p5mV"][s])
        te.wait_ge(cS.s, mk["p5mS"][s])
        for kt in range(2):
            cT.inc(te.matmul(pKK, kf[kt], kf[kt], start=(kt == 0), stop=(kt == 1)))
        for kt in range(2):
            cT.inc(te.matmul(pPT, kf[kt], qf[kt], start=(kt == 0), stop=(kt == 1)))
        te.wait_ge(cS.s, sbm[h])
        te.wait_ge(cV.s, mk["p4m2"][s])
        for kt in range(2):
            cT.inc(te.matmul(pR, kf[kt], S_b(h, kt), start=(kt == 0), stop=(kt == 1)))
        mmk = cT.n
        yield
        vec.wait_ge(cT.s, mmk)
        vec.wait_ge(cS.s, mk["p1mS"][s])
        C0, C0T = Cp[s][0]
        vec.tensor_scalar(C0, pKK, nbcol, None, ALU.mult)
        cV.inc(vec.tensor_tensor(C0, C0, p.cst["c_trilS"][:, :], ALU.mult))
        c1m = cV.n
        cV.inc(vec.tensor_tensor(PTb[s], pPT, p.cst["c_triuI"][:, :], ALU.mult))
        mk["p0m"][s] = cV.n
        vec.tensor_scalar(Rv2[s], vtm2[b][:, 256 * h:256 * h + 256], bcol, None, ALU.mult)
        cV.inc(vec.scalar_tensor_tensor(Dl[s][0], pR, nbcol, Rv2[s], ALU.mult, ALU.add))
        mk["p4m2"][s] = cV.n
        dme, dm = "V", cV.n
        te.wait_ge(cV.s, c1m)
        te.wait_ge(cS.s, mk["pbtm"][s])
        cT.inc(te.transpose(pC1T, C0, p.cst["c_ident"][:, :]))
        cT.wait(sc)
        cS.inc(sc.copy(C0T, pC1T))
        mk["pbtm"][s] = cS.n
        ctme, ctm = ["S", None], [cS.n, 0]
        yield
        for i in range(7):
            cur = i % 2
            nxt = 1 - cur
            Cc, CcT = Cp[s][cur]
            te.wait_ge(cV.s, mk["p1mV"][s])
            te.wait_ge(cS.s, mk["p1mS"][s])
            te.wait_ge(cV.s if ctme[cur] == "V" else cS.s, ctm[cur]) \
                if ctm[cur] is not None else None
            te.wait_ge(cV.s if dme == "V" else cS.s, dm)
            cT.inc(te.matmul(pA, p.cst["c_ident"][:, :], Dl[s][cur],
                             start=True, stop=False))
            cT.inc(te.matmul(pA, CcT, Dl[s][cur], start=False, stop=True))
            am = cT.n
            if i < 6:
                te.wait_ge(cV.s, mk["p3ev"][s])
                cT.inc(te.matmul(pSq[:, 0:128], CcT, Cc, start=True, stop=True))
                cT.inc(te.matmul(pSq[:, 128:256], Cc, CcT, start=True, stop=True))
                sqm = cT.n
            if i % 2 == 0:
                vec.wait_ge(cT.s, am)
                cV.inc(vec.tensor_copy(Dl[s][nxt], pA))
                mk["p1mV"][s] = cV.n
                dme, dm = "V", cV.n
            else:
                sc.wait_ge(cT.s, am)
                cS.inc(sc.copy(Dl[s][nxt], pA))
                mk["p1mS"][s] = cS.n
                dme, dm = "S", cS.n
            if i < 6:
                vec.wait_ge(cT.s, sqm)
                cV.inc(vec.tensor_copy(Cpair[s][nxt], pSq))
                mk["p3ev"][s] = cV.n
                ctme[nxt], ctm[nxt] = "V", cV.n
            yield
        te.wait_ge(cS.s, mk["p5mS"][s])
        te.wait_ge(cV.s, mk["p5mV"][s])
        te.wait_ge(cV.s if dme == "V" else cS.s, dm)
        cT.inc(te.matmul(pO, PTb[s], Dl[s][1], start=True, stop=False))
        for kt in range(2):
            cT.inc(te.matmul(pO, qf[kt], S_b(h, kt), start=False, stop=(kt == 1)))
        om = cT.n
        yield
        sc.wait_ge(cT.s, om)
        cS.inc(sc.activation(scrap2[s], pO, AF.Square,
                             accum_out=sm32[0:128, 16 + h:17 + h]))
        mk["p5mS"][s] = cS.n
        cS.wait(vec)
        vec.tensor_scalar(sm32[:, 20 + h:21 + h], sm32[:, 16 + h:17 + h],
                          1.0 / 256.0, 1e-5, ALU.mult, ALU.add)
        vec.drain()
        cV.inc(vec.reciprocal(sm32[:, 24 + h:25 + h], sm32[:, 20 + h:21 + h]))
        cV.wait(sc)
        cS.inc(sc.activation(sm32[:, 28 + h:29 + h], sm32[:, 24 + h:25 + h], AF.Sqrt))
        cS.wait(vec)
        if h == 0:
            vec.wait_ge(dW.s, mk["ost"])
        osl = och[:, 256 * h:256 * h + 256]
        vec.tensor_scalar(osl, pO, sm32[:, 28 + h:29 + h], None, ALU.mult)
        cV.inc(vec.tensor_tensor(osl, osl, p.grms[:, 0:256], ALU.mult))
        mk["p5mV"][s] = cV.n
        pSup = pb[1] if s == 0 else pb[2]
        te.wait_ge(cV.s if dme == "V" else cS.s, dm)
        te.wait_ge(cV.s, mk["p1mV"][s])
        for kt in range(2):
            cT.inc(te.matmul(pSup[:, kt * 256:(kt + 1) * 256],
                             ktm2[b][:, 256 * h + kt * 128:256 * h + (kt + 1) * 128],
                             Dl[s][1], start=True, stop=True))
        kvmark[b] = cT.n
        smk = cT.n
        yield
        vec.wait_ge(cT.s, smk)
        for kt in range(2):
            vec.tensor_tensor(S_f(h, kt), S_f(h, kt), pSup[:, kt * 256:(kt + 1) * 256],
                              ALU.add)
        cV.inc(vec.engine_nop())
        mk["p1mV"][s] = cV.n
        cV.wait(sc)
        for kt in range(2):
            cS.inc(sc.copy(S_b(h, kt), S_f(h, kt)))
        sbm[h] = cS.n

    for sci in range(32):
        half = sci // 16
        if sci % 16 == 0:
            sy.wait_ge(cT.s, cT.n)
            for kt2 in range(8):
                dX.incd(sy.dma_start(AA[:, kt2 * 2048:(kt2 + 1) * 2048],
                                     p.scr["s_kk"][kt2 * 128:(kt2 + 1) * 128,
                                                   half * 2048:(half + 1) * 2048]))
        b = sci % 2
        sy.wait_ge(cT.s, kvmark[b])
        tsl = slice(sci * 128, (sci + 1) * 128)
        dX.incd(sy.dma_start(ktm2[b], p.scr["s_kk"][:, tsl], transpose=True))
        dX.incd(sy.dma_start(vtm2[b], p.scr["s_vv"][:, tsl], transpose=True))
        dX.wait(te)
        dX.wait(vec)
        lsl = (sci % 16) * 128
        for pair in range(2):
            gens = [head_prog(2 * pair, 0, sci, b, lsl),
                    head_prog(2 * pair + 1, 1, sci, b, lsl)]
            alive = [True, True]
            while alive[0] or alive[1]:
                for gi in range(2):
                    if alive[gi]:
                        try:
                            next(gens[gi])
                        except StopIteration:
                            alive[gi] = False
        cV.wait(gp)
        dW.incd(gp.dma_start(p.scr["s_ot"][tsl, :], och))
        mk["ost"] = dW.n

    WoSB = [AC[0:128, kt * 1024:(kt + 1) * 1024] for kt in range(8)]
    wmark2 = cT.n
    for kt in range(8):
        gp.wait_ge(cT.s, wmark2)
        gp.wait_ge(cS.s, cS.n)
        dW.incd(gp.dma_start(WoSB[kt], p.W["L1_attn_Wo"][kt * 128:(kt + 1) * 128, :]))
    ovm = [dW.n, dW.n]
    pzm2 = [cS.n] * 2
    for half in range(2):
        sy.wait_ge(cT.s, cT.n)
        sy.wait_ge(dW.s, dW.n)
        for i in range(8):
            dX.incd(sy.dma_start(AA[:, i * 2048:(i + 1) * 2048],
                                 p.scr["s_ot"][half * 2048:(half + 1) * 2048,
                                               i * 128:(i + 1) * 128], transpose=True))
        dX.wait(te)
        dW.wait(te)
        for mo in range(8):
            for cc in range(4):
                po = pb[cc % 2]
                te.wait_ge(cS.s, pzm2[cc % 2])
                for kt in range(8):
                    cT.inc(te.matmul(po[:, :], WoSB[kt][:, mo * 128:(mo + 1) * 128],
                                     AA[0:128, kt * 2048 + cc * 512:kt * 2048 + (cc + 1) * 512],
                                     start=(kt == 0), stop=(kt == 7)))
                cT.wait(sc)
                otile = SM[0:128, (cc % 2) * 512:512 + (cc % 2) * 512]
                sc.wait_ge(dW.s, ovm[cc % 2])
                cS.inc(sc.copy(otile, po[:, :]))
                pzm2[cc % 2] = cS.n
                cS.wait(gp)
                dW.incd(gp.dma_start(
                    p.scr["oc"][mo * 128:(mo + 1) * 128,
                                half * 2048 + cc * 512:half * 2048 + (cc + 1) * 512], otile))
                ovm[cc % 2] = dW.n
    dW.wait(sy)


def _l1_ffn(p):
    nc = p.nc
    pb = p.pb
    vec, sc, te, gp, sy = nc.vector, nc.scalar, nc.tensor, nc.gpsimd, nc.sync
    dW, dX, dS, cT, cV, cS = p.dW, p.dX, p.dS, p.cT, p.cV, p.cS
    SM, sm32, AC, AA, AB, AFX = p.SM, p.sm32, p.AC, p.AA, p.AB, p.AF
    NW = 4096 * 1024

    scrap = AC[0:128, 0:4096]
    lnwrow = SM[0:1, 0:4096]
    lnwbc = AC[0:128, 8192:12288]
    dW.incd(gp.dma_start(lnwrow, p.W["L1_ffn_ln_w"][None, :]))
    dW.wait(te)
    cV.wait(te)
    cS.wait(te)
    for cb2 in range(8):
        cT.inc(te.matmul(pb[3][:, :], p.cst["c_ones_row"][0:1, :],
                         lnwrow[0:1, cb2 * 512:(cb2 + 1) * 512], start=True, stop=True))
        cT.wait(vec)
        cV.inc(vec.tensor_copy(lnwbc[:, cb2 * 512:(cb2 + 1) * 512], pb[3][:, :]))
        cV.wait(te)
    fcb = [AFX[0:128, 0:1024], AFX[0:128, 1024:2048]]
    pjb = [AFX[0:128, 0:2048], AFX[0:128, 2048:4096]]
    ldm = [dX.n, dX.n]
    csm = [cS.n, cS.n]
    for i in range(32):
        b = i % 2
        sy.wait_ge(cS.s, csm[b])
        dX.incd(sy.dma_start(fcb[b], p.W["L1_ffn_W_fc"][i * 128:(i + 1) * 128, :]))
        ldm[b] = dX.n
        sc.wait_ge(dX.s, ldm[b])
        cS.inc(sc.activation(scrap[:, 0:1024], fcb[b], AF.Abs,
                             accum_out=sm32[0:128, i:i + 1]))
        csm[b] = cS.n
    csm = [cS.n, cS.n]
    for i in range(16):
        r, hh = i // 2, i % 2
        b = i % 2
        sy.wait_ge(cS.s, csm[b])
        dX.incd(sy.dma_start(pjb[b], p.W["L1_ffn_W_proj"][r * 128:(r + 1) * 128,
                                                          hh * 2048:(hh + 1) * 2048]))
        ldm[b] = dX.n
        sc.wait_ge(dX.s, ldm[b])
        cS.inc(sc.activation(scrap[:, 0:2048], pjb[b], AF.Abs,
                             accum_out=sm32[0:128, 32 + i:33 + i]))
        csm[b] = cS.n
    cS.wait(vec)
    vec.reduce_sum(sm32[0:128, 56:57], sm32[0:128, 0:32], mybir.AxisListType.X)
    vec.reduce_sum(sm32[0:128, 57:58], sm32[0:128, 32:48], mybir.AxisListType.X)
    vec.memset(sm32[0:128, 58:59], 1.0)
    cV.inc(vec.memset(sm32[0:1, 64:192], 1.0))
    cV.wait(te)
    cT.inc(te.matmul(pb[2][0:1, 0:2], sm32[0:128, 58:59], sm32[0:128, 56:58],
                     start=True, stop=True))
    cT.wait(vec)
    vec.tensor_scalar(sm32[0:1, 60:62], pb[2][0:1, 0:2], 1.0 / NW, 1e-6, ALU.mult, ALU.add)
    vec.drain()
    vec.reciprocal(sm32[0:1, 62:64], sm32[0:1, 60:62])
    vec.drain()
    cV.inc(vec.tensor_scalar(sm32[0:1, 62:64], sm32[0:1, 62:64], 2.0, None, ALU.mult))
    cV.wait(te)
    cT.inc(te.matmul(pb[2][:, 8:10], sm32[0:1, 64:192], sm32[0:1, 62:64],
                     start=True, stop=True))
    cT.wait(vec)
    cV.inc(vec.tensor_copy(sm32[0:128, 48:50], pb[2][:, 8:10]))
    rgfc = sm32[0:128, 48:49]
    rgpj = sm32[0:128, 49:50]
    stm = [dW.n, dW.n]
    vcm = [cV.n, cV.n]
    m1f = [SM[0:128, 0:1024], SM[0:128, 1024:2048]]
    m2f = SM[0:128, 2048:3072]
    wqf = [SM[0:128, 3072:4096], SM[0:128, 4096:5120]]
    for i in range(32):
        b = i % 2
        sy.wait_ge(cV.s, vcm[b])
        dX.incd(sy.dma_start(fcb[b], p.W["L1_ffn_W_fc"][i * 128:(i + 1) * 128, :]))
        ldm[b] = dX.n
        vec.wait_ge(dX.s, ldm[b])
        vec.wait_ge(dW.s, stm[b])
        vec.tensor_scalar(m1f[b], fcb[b], rgfc, 1.0, ALU.mult, ALU.is_gt)
        vec.tensor_scalar(m2f, fcb[b], rgfc, -1.0, ALU.mult, ALU.is_lt)
        cV.inc(vec.tensor_tensor(wqf[b], m1f[b], m2f, ALU.subtract))
        vcm[b] = cV.n
        cV.wait(gp)
        dW.incd(gp.dma_start(p.scr["s_wf"][i * 128:(i + 1) * 128, :], wqf[b]))
        stm[b] = dW.n
    m1p = [SM[0:128, 0:2048], SM[0:128, 2048:4096]]
    dW.wait(vec)
    vcm = [cV.n, cV.n]
    stm = [dW.n, dW.n]
    for i in range(16):
        r, hh = i // 2, i % 2
        b = i % 2
        sy.wait_ge(cV.s, vcm[b])
        dX.incd(sy.dma_start(pjb[b], p.W["L1_ffn_W_proj"][r * 128:(r + 1) * 128,
                                                          hh * 2048:(hh + 1) * 2048]))
        ldm[b] = dX.n
        vec.wait_ge(dX.s, ldm[b])
        vec.wait_ge(dW.s, stm[b])
        vec.tensor_scalar(m1p[b], pjb[b], rgpj, 1.0, ALU.mult, ALU.is_gt)
        vec.tensor_scalar(scrap[:, 0:2048], pjb[b], rgpj, -1.0, ALU.mult, ALU.is_lt)
        cV.inc(vec.tensor_tensor(m1p[b], m1p[b], scrap[:, 0:2048], ALU.subtract))
        vcm[b] = cV.n
        cV.wait(gp)
        dW.incd(gp.dma_start(p.scr["s_wp"][r * 128:(r + 1) * 128,
                                           hh * 2048:(hh + 1) * 2048], m1p[b]))
        stm[b] = dW.n

    sy.wait_ge(cT.s, cT.n)
    sy.wait_ge(dW.s, dW.n)
    WF = [AB[0:128, dt * 4096:(dt + 1) * 4096] for dt in range(8)]
    for dt in range(8):
        dX.incd(sy.dma_start(WF[dt], p.scr["s_wf"][:, dt * 128:(dt + 1) * 128],
                             transpose=True))
    vt = AC[0:128, 4096:8192]
    zt = SM[0:128, 0:4096]
    c = p.sc1
    pzm = [cS.n, cS.n]
    acm = cV.n
    zsm = dW.n
    for half in range(2):
        _load_xlf(p, half)
        XLF = [AA[0:128, i * 2048:(i + 1) * 2048] for i in range(8)]
        dX.wait(te)
        for tl in range(16):
            tt = half * 16 + tl
            for ch in range(8):
                pz = pb[ch % 2]
                te.wait_ge(cS.s, pzm[ch % 2])
                for kt in range(8):
                    cT.inc(te.matmul(pz[:, :], XLF[kt][:, tl * 128:(tl + 1) * 128],
                                     AB[0:128, kt * 4096 + ch * 512:kt * 4096 + (ch + 1) * 512],
                                     start=(kt == 0), stop=(kt == 7)))
                cT.wait(sc)
                if ch == 0:
                    sc.wait_ge(cV.s, acm)
                cS.inc(sc.activation(vt[:, ch * 512:(ch + 1) * 512], pz[:, :], AF.Gelu,
                                     accum_out=sm32[0:128, 70 + ch:71 + ch]))
                pzm[ch % 2] = cS.n
            cS.inc(sc.activation(scrap[:, 0:4096], vt, AF.Square,
                                 accum_out=sm32[0:128, 78:79]))
            cS.wait(vec)
            cV.inc(vec.reduce_sum(c[:, 0:1], sm32[0:128, 70:78], mybir.AxisListType.X))
            cV.wait(sc)
            cS.inc(sc.activation(c[:, 2:3], c[:, 0:1], AF.Copy, scale=1.0 / 4096.0))
            cS.wait(vec)
            cV.inc(vec.engine_nop())
            cV.wait(sc)
            cS.inc(sc.activation(c[:, 3:4], c[:, 2:3], AF.Square))
            cS.wait(vec)
            vec.scalar_tensor_tensor(c[:, 4:5], sm32[0:128, 78:79], 1.0 / 4096.0,
                                     c[:, 3:4], ALU.mult, ALU.subtract)
            vec.drain()
            vec.tensor_scalar(c[:, 5:6], c[:, 4:5], 1e-5, None, ALU.add)
            vec.drain()
            cV.inc(vec.reciprocal(c[:, 6:7], c[:, 5:6]))
            cV.wait(sc)
            cS.inc(sc.activation(c[:, 7:8], c[:, 6:7], AF.Sqrt))
            cS.wait(vec)
            vec.wait_ge(dW.s, zsm)
            vec.tensor_scalar(zt, vt, c[:, 2:3], c[:, 7:8],
                              ALU.subtract, ALU.mult)
            cV.inc(vec.tensor_tensor(zt, zt, lnwbc, ALU.mult))
            acm = cV.n
            cV.wait(gp)
            dW.incd(gp.dma_start(p.scr["s_h1"][tt * 128:(tt + 1) * 128, :], zt))
            zsm = dW.n

    sy.wait_ge(cT.s, cT.n)
    sy.wait_ge(dW.s, dW.n)
    WP = [AB[0:128, kt * 1024:(kt + 1) * 1024] for kt in range(32)]
    for kt in range(32):
        dX.incd(sy.dma_start(WP[kt], p.scr["s_wp"][:, kt * 128:(kt + 1) * 128],
                             transpose=True))
    lnbT = SM[0:128, 4096:4128]
    for kt in range(32):
        dW.incd(gp.dma_start(lnbT[:, kt:kt + 1],
                             p.W["L1_ffn_ln_b"][kt * 128:(kt + 1) * 128][:, None]))
    dX.wait(te)
    dW.wait(te)
    for dt in range(8):
        for kt in range(32):
            cT.inc(te.matmul(pb[2][:, dt:dt + 1], WP[kt][:, dt * 128:(dt + 1) * 128],
                             lnbT[:, kt:kt + 1], start=(kt == 0), stop=(kt == 31)))
    cT.wait(vec)
    cV.inc(vec.tensor_copy(sm32[0:128, 80:88], pb[2][:, 0:8]))
    bvm = cV.n
    ovm = [dW.n, dW.n]
    pzm2 = [cV.n, cV.n]
    ldh = [dX.n, dX.n]
    lastuse = [cT.n, cT.n]
    for grp in range(16):
        hb = grp % 2
        base = hb * 8192
        sy.wait_ge(cT.s, lastuse[hb])
        for kt in range(32):
            dX.incd(sy.dma_start(AA[0:128, base + kt * 256:base + (kt + 1) * 256],
                                 p.scr["s_h1"][grp * 256:(grp + 1) * 256,
                                               kt * 128:(kt + 1) * 128], transpose=True))
        ldh[hb] = dX.n
        te.wait_ge(dX.s, ldh[hb])
        for dt in range(8):
            pz = pb[dt % 2]
            te.wait_ge(cV.s, pzm2[dt % 2])
            for kt in range(32):
                cT.inc(te.matmul(pz[:, 0:256], WP[kt][:, dt * 128:(dt + 1) * 128],
                                 AA[0:128, base + kt * 256:base + (kt + 1) * 256],
                                 start=(kt == 0), stop=(kt == 31)))
            cT.wait(vec)
            vec.wait_ge(cV.s, bvm)
            otile = SM[0:128, (dt % 2) * 256:256 + (dt % 2) * 256]
            vec.wait_ge(dW.s, ovm[dt % 2])
            cV.inc(vec.tensor_scalar(otile, pz[:, 0:256], sm32[0:128, 80 + dt:81 + dt],
                                     None, ALU.add))
            pzm2[dt % 2] = cV.n
            cV.wait(gp)
            dW.incd(gp.dma_start(p.scr["oc"][dt * 128:(dt + 1) * 128,
                                             grp * 256:(grp + 1) * 256], otile))
            ovm[dt % 2] = dW.n
        lastuse[hb] = cT.n
    dW.wait(sy)


def kernel(**inputs):
    nc = build()
    consts = _consts()
    in_maps = []
    x = np.ascontiguousarray(inputs["L0_attn_x"])
    for core in range(8):
        m = {"x0": x[core]}
        for k, v in inputs.items():
            if k == "L0_attn_x":
                continue
            m[k] = np.ascontiguousarray(v)
        for k, v in consts.items():
            m[k] = v
        in_maps.append(m)
    res = run_bass_kernel_spmd(nc, in_maps, core_ids=list(range(8)))
    out = np.stack([res.results[i]["out"] for i in range(8)], axis=0)
    return out.astype(np.float32)
